# version 16
# baseline (speedup 1.0000x reference)
import contextlib
import os
import numpy as np
import ml_dtypes
import concourse.bass as bass
import concourse.mybir as mybir
from concourse.bass_utils import run_bass_kernel_spmd

F32 = mybir.dt.float32
BF16 = mybir.dt.bfloat16
AF = mybir.ActivationFunctionType
ALU = mybir.AluOpType
AX = mybir.AxisListType

D = 1024
ND = D // 128
HEADS = 16
HD = 64
LAT = 512
NL = LAT // 128
NE = 8
EXPD = 2048
NH = EXPD // 128
LN_EPS = 1e-5
EPS_A = 1e-6
RMS_EPS = 1e-6
EXP_SHIFT = -4.0

ENGS = ("sync", "gpsimd", "vector", "scalar", "tensor")
OWN = {"vector": "dve", "scalar": "act", "tensor": "pe", "gpsimd": "pool", "sync": "syn"}
NQS = 8
QSEMS = {q: [f"{q}{i}" for i in range(NQS)] for q in ("ld", "st", "gd")}
SEMS = tuple(["pe", "dve", "act", "pool", "syn"] + [s for q in QSEMS.values() for s in q])


class Plan:
    def __init__(self, strict=False):
        self.strict = strict
        self.ops = {e: [] for e in ENGS}
        self.cnt = {s: 0 for s in SEMS}
        self.qidx = {"ld": 0, "st": 0, "gd": 0}
        self.hwm = {e: {} for e in ENGS}
        self.semh = {}

    def on(self, eng, fn):
        self.ops[eng].append(("op", fn))

    def tok(self, eng, fn, n=1):
        sem = OWN[eng]
        if (self.strict or eng == "gpsimd") and self.cnt[sem] > 0:
            self.wait(eng, (sem, self.cnt[sem]))
        self.cnt[sem] += n
        self.ops[eng].append(("tok", fn, sem, n))
        return (sem, self.cnt[sem])

    def wait(self, eng, token):
        if token is None:
            return
        if isinstance(token, list):
            for t in token:
                self.wait(eng, t)
            return
        sem, val = token
        if self.hwm[eng].get(sem, 0) >= val:
            return
        self.hwm[eng][sem] = val
        self.ops[eng].append(("wait", sem, val))

    def _dma(self, eng, q, fn):
        i = self.qidx[q] % NQS
        self.qidx[q] += 1
        sem = QSEMS[q][i]
        if self.cnt[sem] > 0:
            self.wait(eng, (sem, self.cnt[sem]))
        self.cnt[sem] += 16
        self.ops[eng].append(("tok", fn, sem, 16))
        return (sem, self.cnt[sem])

    def ld(self, fn):
        return self._dma("sync", "ld", fn)

    def st(self, fn):
        return self._dma("scalar", "st", fn)

    def gd(self, fn):
        return self._dma("gpsimd", "gd", fn)

    def barrier(self):
        import os
        mode = int(os.environ.get("BARRIER", "2"))
        if mode == 0:
            return
        toks = {e: self.tok(e, lambda E, e=e: E.nop() if e == "sync" else E.drain())
                for e in ENGS}
        dma_toks = [(s, self.cnt[s]) for q in ("ld", "st") for s in QSEMS[q]
                    if self.cnt[s] > 0] if mode >= 2 else []
        for e in ENGS:
            for o in ENGS:
                if o != e:
                    self.wait(e, toks[o])
            for t in dma_toks:
                self.wait(e, t)

    def emit(self, eng_obj, name):
        for item in self.ops[name]:
            kind = item[0]
            if kind == "op":
                item[1](eng_obj)
            elif kind == "tok":
                _, fn, sem, n = item
                fn(eng_obj).then_inc(self.semh[sem], n)
            else:
                _, sem, val = item
                eng_obj.wait_ge(self.semh[sem], val)


class Slot:
    def __init__(self):
        self.ready = None
        self.users = []

    def prep_write(self, plan, eng):
        for t in self.users:
            plan.wait(eng, t)
        self.users = []

    def use(self, plan, eng):
        plan.wait(eng, self.ready)


class PRing:
    def __init__(self, plan, slots):
        self.plan = plan
        self.slots = slots
        self.last = [None] * len(slots)
        self.i = 0

    def acquire(self):
        i = self.i % len(self.slots)
        self.i += 1
        self.plan.wait("tensor", self.last[i])
        return i, self.slots[i]

    def release(self, i, evac_tok):
        self.last[i] = evac_tok


def build_nc(C=2048, NSEQ=4, strict=False, max_phases=99):
    assert C % 512 == 0 and (C // 128) % 2 == 0
    NT = C // 128
    HT = NT // 2
    CH = C // 2
    nc = bass.Bass("TRN2", target_bir_lowering=False, debug=False,
                   detect_race_conditions=strict)
    plan = Plan(strict=strict)
    TOK = {}

    x_in = nc.dram_tensor("x", [NSEQ, C, D], F32, kind="ExternalInput").ap()
    out_t = nc.dram_tensor("out", [NSEQ, C, D], F32, kind="ExternalOutput").ap()
    ins = {}
    for nm, shp in [("wqa", [D, D]), ("wka", [D, D]), ("wva", [D, D]),
                    ("bq", [D]), ("bk", [D]), ("bv", [D]), ("gam", [1]),
                    ("g1", [D]), ("mq", [D, D]), ("mdkv", [D, LAT]),
                    ("muk", [LAT, D]), ("muv", [LAT, D]), ("mo", [D, D]),
                    ("g2", [D]), ("gw", [D, NE]), ("w1", [NE, D, EXPD]),
                    ("w2", [NE, EXPD, D])]:
        ins[nm] = nc.dram_tensor(nm, shp, F32, kind="ExternalInput").ap()
    scr = {}
    for nm, shp in [("wqa", [D, D]), ("wka", [D, D]), ("wva", [D, D]),
                    ("mq", [D, D]), ("mdkv", [D, LAT]), ("muk", [LAT, D]),
                    ("muv", [LAT, D]), ("mo", [D, D]), ("gw", [D, NE]),
                    ("w1", [NE, D, EXPD]), ("w2", [NE, EXPD, D])]:
        scr[nm] = nc.dram_tensor("s_" + nm, shp, BF16).ap()
    d_xn = nc.dram_tensor("d_xn", [C, D], F32).ap()
    d_xnfm = nc.dram_tensor("d_xnfm", [ND, 128, C], BF16).ap()
    d_s = nc.dram_tensor("d_s", [C, D], F32).ap()
    d_x2 = nc.dram_tensor("d_x2", [C, D], F32).ap()
    d_h2 = nc.dram_tensor("d_h2", [ND, 128, C], BF16).ap()

    REG = {}
    for r, nb in [("SA", 32768), ("SB", 32768), ("SC", 32768), ("SD", 32768),
                  ("SE", 32768), ("SF", 16384), ("SG", 8192)]:
        h = nc.alloc_sbuf_tensor("R_" + r, [128, nb], mybir.dt.uint8)
        REG[r] = nc.lookup_mloc(h).addr

    def at(reg, off, name, shape, dtype):
        return nc.alloc_sbuf_tensor_at(name, shape, dtype, offset=REG[reg] + off)

    def alloc(name, shape, dtype):
        return nc.alloc_sbuf_tensor(name, shape, dtype)

    c_idbf = alloc("c_idbf", [128, 128], BF16)
    c_idf = alloc("c_idf", [128, 128], F32)
    c_iotaP = alloc("c_iotaP", [128, 1], F32)
    c_mask = alloc("c_mask", [128, 128], F32)
    c_maskbf = alloc("c_maskbf", [128, 128], BF16)
    c_maskbfT = alloc("c_maskbfT", [128, 128], BF16)
    c_onebf = alloc("c_onebf", [128, 1], BF16)
    c_onerf = alloc("c_onerf", [1, 128], F32)
    c_g2bc = alloc("c_g2bc", [128, D], F32)
    c_g1col = alloc("c_g1col", [128, ND], F32)
    c_bcol = alloc("c_bcol", [128, 3 * ND], F32)
    c_gam = alloc("c_gam", [1, 1], F32)
    c_gamcol = alloc("c_gamcol", [128, 1], F32)
    c_gwsb = alloc("c_gwsb", [128, ND * NE], BF16)
    c_g2row = alloc("c_g2row", [1, D], F32)
    c_rowtmp = alloc("c_rowtmp", [1, D], F32)
    c_epsln = alloc("c_epsln", [128, 1], F32)
    c_epsrm = alloc("c_epsrm", [128, 1], F32)
    c_shift = alloc("c_shift", [128, 1], F32)
    w_cols = alloc("w_cols", [128, 64], F32)
    w_invq = alloc("w_invq", [128, ND], F32)
    w_invk = alloc("w_invk", [128, ND], F32)
    w_qsum = alloc("w_qsum", [128, ND], F32)
    w_rsc = alloc("w_rsc", [128, ND], F32)
    w_tgc = alloc("w_tgc", [128, ND], F32)
    w_krow = alloc("w_krow", [128, NT], F32)
    w_krbf = alloc("w_krbf", [128, NT], BF16)
    w_comb = alloc("w_comb", [128, NT * NE], F32)
    w_rst1 = alloc("w_rst1", [128, NT], F32)
    w_sums = alloc("w_sums", [128, 16], F32)
    w_rcp = alloc("w_rcp", [128, 4], F32)
    w_m1 = alloc("w_m1", [128, NT], F32)
    w_m2 = alloc("w_m2", [128, NT], F32)
    w_gt = alloc("w_gt", [128, NT], F32)
    w_logit = alloc("w_logit", [128, NT * NE], F32)
    w_gsc = alloc("w_gsc", [128, 2 * NE], F32)

    xn_fm = at("SA", 0, "xn_fm", [128, ND * C], BF16)
    A_nm = at("SA", 0, "A_nm", [128, ND * D], BF16)
    A_jm = at("SA", 2 * ND * D, "A_jm", [128, ND * D], BF16)
    c_fm = at("SA", 0, "c_fm", [128, NL * C], BF16)
    P_ = [at("SA", 0, "P0", [128, 2 * C], BF16), at("SA", 4 * C, "P1", [128, 2 * C], BF16)]
    h2_fm_h = at("SA", 0, "h2_fm_h", [128, ND * CH], BF16)
    attn_tm = at("SA", 16384, "attn_tm", [128, HT * D], BF16)
    acc0 = at("SA", 0, "acc0", [128, HT * D], F32)

    st_x = [at("SB", 0, "st_x0", [128, D], F32), at("SB", 4096, "st_x1", [128, D], F32)]
    st_xn = [at("SB", 8192, "st_xn0", [128, D], F32), at("SB", 12288, "st_xn1", [128, D], F32)]
    st_xb = [at("SB", 16384, "st_xb0", [128, D], BF16), at("SB", 18432, "st_xb1", [128, D], BF16)]
    Q_fm = at("SB", 0, "Q_fm", [128, ND * C], BF16)
    h1_fm = at("SB", 0, "h1_fm", [128, ND * C], BF16)
    k_fm = at("SB", 0, "k_fm", [128, ND * C], BF16)
    moe_w0 = at("SB", 0, "moe_w0", [128, 16384], BF16)

    K_fm = at("SC", 0, "K_fm", [128, ND * C], BF16)
    q_fm = at("SC", 0, "q_fm", [128, ND * C], BF16)
    moe_w1 = at("SC", 0, "moe_w1", [128, 16384], BF16)

    V_fm = at("SD", 0, "V_fm", [128, ND * C], BF16)
    v_tm = at("SD", 0, "v_tm", [128, NT * D], BF16)
    moe_w2 = at("SD", 0, "moe_w2", [128, 16384], BF16)
    moe_w = [moe_w0, moe_w2, moe_w1]
    moe_wsl_names = ["moe_w0", "moe_w2", "moe_w1"]

    WT = [at("SE", 0, "WT0", [128, ND * D], BF16),
          at("SE", 2 * ND * D, "WT1", [128, ND * D], BF16)]
    Kn_half = at("SE", 0, "Kn_half", [128, NT * 512], BF16)
    Qn_t = [at("SE", NT * 1024, "Qn_t0", [128, NT * 128], BF16),
            at("SE", NT * 1024 + NT * 256, "Qn_t1", [128, NT * 128], BF16)]
    st_wv = [at("SE", 0, "st_wv0", [128, D], F32), at("SE", 4096, "st_wv1", [128, D], F32)]
    st_s = [at("SE", 8192, "st_s0", [128, D], F32), at("SE", 12288, "st_s1", [128, D], F32)]
    mw_q = at("SE", 0, "mw_q", [128, ND * D], BF16)
    mw_o = at("SE", 0, "mw_o", [128, ND * D], BF16)
    mw_dkv = at("SE", 16384, "mw_dkv", [128, ND * LAT], BF16)
    mw_uv = at("SE", 16384, "mw_uv", [128, NL * D], BF16)
    mw_uk = at("SE", 24576, "mw_uk", [128, NL * D], BF16)
    acc1 = at("SE", 0, "acc1", [128, HT * D], F32)

    vs_row = at("SF", 0, "vs_row", [1, C], F32)
    rst1_row = at("SF", 4096, "rst1_row", [1, C], F32)
    tg_row = at("SF", 8192, "tg_row", [1, D], F32)
    scr_bf = at("SF", 12288, "scr_bf", [128, 2048], BF16)
    st_h2 = [at("SF", 0, "st_h20", [128, D], BF16), at("SF", 2048, "st_h21", [128, D], BF16)]
    st_x2 = at("SF", 4096, "st_x2", [128, D], F32)
    moe_hid = at("SF", 0, "moe_hid", [128, NH * 512], BF16)
    fin_s = [at("SD", 0, "fin_s0", [128, D], F32), at("SD", 4096, "fin_s1", [128, D], F32)]
    fin_g = [at("SD", 8192, "fin_g0", [128, D], F32), at("SD", 12288, "fin_g1", [128, D], F32)]

    rstd1_bc = at("SG", 0, "rstd1_bc", [128, C], F32)
    st_wo = [at("SG", 0, "st_wo0", [128, D], F32), at("SG", 4096, "st_wo1", [128, D], F32)]
    moe_h2c = at("SG", 0, "moe_h2c", [128, ND * 512], BF16)
    fin_x = [at("SD", 16384, "fin_x0", [128, D], F32), at("SD", 20480, "fin_x1", [128, D], F32)]

    pbank = [nc.alloc_psum_tensor(f"p{i}", [128, 512], F32) for i in range(6)]
    psD = pbank[4]
    psT0 = nc.alloc_psum_tensor("psT0", [128, 1024], BF16)
    psT1 = nc.alloc_psum_tensor("psT1", [128, 1024], BF16)

    SL = {}
    for name in ("xn_fm A_nm A_jm c_fm P0 P1 h2_fm_h attn_tm acc0 acc1 "
                 "stx0 stx1 stxn0 stxn1 stxb0 stxb1 Q_fm h1_fm k_fm K_fm q_fm V_fm v_tm "
                 "WT0 WT1 Kn_half Qn0 Qn1 stwv0 stwv1 sts0 sts1 mw_q mw_o mw_dkv mw_uv "
                 "mw_uk moe_w0 moe_w1 moe_w2 moe_h0 moe_h1 hid0 hid1 sth20 sth21 stx2 "
                 "stwo0 stwo1 fins0 fins1 fing0 fing1 finx0 finx1 rstd1_bc combine").split():
        SL[name] = Slot()

    ring4 = PRing(plan, [pbank[i][:] for i in range(4)])
    ringC = PRing(plan, [pbank[4][:], pbank[5][:]])
    ringT = PRing(plan, [psT0[:], psT1[:]])
    ringD = PRing(plan, [psD[:]])

    def transpose_pack4(src_blocks, dst_ap):
        ti, pst = ringT.acquire()
        n = len(src_blocks)
        assert n <= 8
        tl = None
        for i, blk in enumerate(src_blocks):
            fn = lambda E, blk=blk, i=i, pst=pst: E.transpose(
                pst[:, i * 128:(i + 1) * 128], blk, c_idbf[:])
            if i == n - 1:
                tl = plan.tok("tensor", fn)
            else:
                plan.on("tensor", fn)
        plan.wait("vector", tl)
        tv = plan.tok("vector", lambda E, dst_ap=dst_ap, n=n, pst=pst: E.tensor_copy(
            dst_ap, pst[:, 0:n * 128]))
        ringT.release(ti, tv)
        return tv

    def transpose_pack(src_blocks, dst_ap):
        n = len(src_blocks)
        nd = len(dst_ap.ap)
        tv = None
        for g in range(0, n, 8):
            m = min(8, n - g)
            if nd == 2:
                sub = dst_ap[:, g * 128:(g + m) * 128]
            else:
                sub = dst_ap[:, g:g + m, :]
            tv = transpose_pack4(src_blocks[g:g + m], sub)
        return tv

    cast = {}

    def phase_convert():
        for nm in ("wqa", "wka", "wva", "mq", "mdkv", "muk", "muv", "mo", "gw"):
            cast[nm] = plan.gd(lambda E, s_=ins[nm], d_=scr[nm]: E.dma_start(out=d_, in_=s_))
        for e in range(NE):
            cast[f"w1_{e}"] = plan.gd(lambda E, s_=ins["w1"][e], d_=scr["w1"][e]: E.dma_start(out=d_, in_=s_))
            cast[f"w2_{e}"] = plan.gd(lambda E, s_=ins["w2"][e], d_=scr["w2"][e]: E.dma_start(out=d_, in_=s_))

    def phase_consts():
        t_g2 = plan.ld(lambda E: E.dma_start(out=c_g2row[0:1, :], in_=ins["g2"][None, :]))
        t_gam = plan.ld(lambda E: E.dma_start(out=c_gam[0:1, :], in_=ins["gam"][None, :]))
        plan.wait("sync", cast["gw"])
        t_gw = plan.ld(lambda E: E.dma_start(
            out=c_gwsb[:].rearrange("p (dc e) -> p dc e", dc=ND),
            in_=scr["gw"].rearrange("(dc p) e -> p dc e", p=128)))
        TOK["gw"] = t_gw

        tp = plan.tok("gpsimd", lambda E: E.iota(
            c_iotaP[:], pattern=[[0, 1]], base=0, channel_multiplier=1,
            allow_small_or_imprecise_dtypes=True))
        tf = plan.tok("gpsimd", lambda E: E.iota(
            c_mask[:], pattern=[[1, 128]], base=0, channel_multiplier=0,
            allow_small_or_imprecise_dtypes=True))
        plan.wait("vector", tp)
        plan.wait("vector", tf)
        t_idb = plan.tok("vector", lambda E: E.tensor_scalar(
            out=c_idbf[:], in0=c_mask[:], scalar1=c_iotaP[:, 0:1], scalar2=None,
            op0=ALU.is_equal))
        t_idf = plan.tok("vector", lambda E: E.tensor_scalar(
            out=c_idf[:], in0=c_mask[:], scalar1=c_iotaP[:, 0:1], scalar2=None,
            op0=ALU.is_equal))
        plan.wait("vector", t_idf)
        plan.wait("vector", t_idb)
        t_mkT = plan.tok("vector", lambda E: E.tensor_scalar(
            out=c_maskbfT[:], in0=c_mask[:], scalar1=c_iotaP[:, 0:1], scalar2=-1e9,
            op0=ALU.is_lt, op1=ALU.mult))
        plan.wait("vector", t_mkT)
        t_msk = plan.tok("vector", lambda E: E.tensor_scalar(
            out=c_mask[:], in0=c_mask[:], scalar1=c_iotaP[:, 0:1], scalar2=-1e9,
            op0=ALU.is_gt, op1=ALU.mult))
        plan.wait("vector", t_msk)
        t_msk = plan.tok("vector", lambda E: E.tensor_copy(c_maskbf[:], c_mask[:]))
        t_ob = plan.tok("vector", lambda E: E.memset(c_onebf[:], 1.0))
        t_or = plan.tok("vector", lambda E: E.memset(c_onerf[:], 1.0))
        t_e1 = plan.tok("vector", lambda E: E.memset(c_epsln[:], LN_EPS))
        t_e2 = plan.tok("vector", lambda E: E.memset(c_epsrm[:], RMS_EPS))
        t_e3 = plan.tok("vector", lambda E: E.memset(c_shift[:], EXP_SHIFT))
        TOK["epsln"], TOK["epsrm"], TOK["shift"] = t_e1, t_e2, t_e3
        TOK["idbf"], TOK["idf"], TOK["mask"] = t_idb, t_idf, t_msk
        TOK["onebf"], TOK["onerf"] = t_ob, t_or

        specs = [("g1", c_g1col, 0), ("bq", c_bcol, 0),
                 ("bk", c_bcol, ND), ("bv", c_bcol, 2 * ND)]
        plan.wait("tensor", t_or)
        prev_cols = None
        for nm, dst, off in specs:
            for t in ([prev_cols] if prev_cols else []):
                plan.wait("sync", t)
            tk = plan.ld(lambda E, nm=nm: E.dma_start(
                out=c_rowtmp[0:1, :], in_=ins[nm][None, :]))
            plan.wait("tensor", tk)
            di, pd = ringD.acquire()
            tm = None
            for dc in range(ND):
                tm = plan.tok("tensor", lambda E, dc=dc, pd=pd: E.matmul(
                    pd[:, dc:dc + 1], c_rowtmp[0:1, dc * 128:(dc + 1) * 128],
                    c_onerf[0:1, 0:1], start=True, stop=True))
            plan.wait("vector", tm)
            tv = plan.tok("vector", lambda E, dst=dst, off=off, pd=pd: E.tensor_copy(
                dst[:, off:off + ND], pd[:, 0:ND]))
            ringD.release(di, tv)
            prev_cols = tm
            TOK["cols"] = tv
        plan.wait("tensor", t_gam)
        di, pd = ringD.acquire()
        t7 = plan.tok("tensor", lambda E, pd=pd: E.matmul(
            pd[:, 0:1], c_onerf[0:1, :], c_gam[0:1, 0:1], start=True, stop=True))
        plan.wait("vector", t7)
        t8 = plan.tok("vector", lambda E, pd=pd: E.tensor_copy(c_gamcol[:, 0:1], pd[:, 0:1]))
        ringD.release(di, t8)
        TOK["gamcol"] = t8
        plan.wait("tensor", t_g2)
        plan.wait("tensor", TOK["cols"])
        tv = None
        for half in range(D // 512):
            i4, pq = ring4.acquire()
            tm = plan.tok("tensor", lambda E, half=half, pq=pq: E.matmul(
                pq, c_onerf[0:1, :], c_g2row[0:1, half * 512:(half + 1) * 512],
                start=True, stop=True))
            plan.wait("vector", tm)
            tv = plan.tok("vector", lambda E, half=half, pq=pq: E.tensor_copy(
                c_g2bc[:, half * 512:(half + 1) * 512], pq))
            ring4.release(i4, tv)
        TOK["g2bc"] = tv
        if os.environ.get("GWWAIT"):
            plan.wait("tensor", TOK["gw"])

    S1CUT = int(os.environ.get("S1CUT", "99"))

    def phase_s1(s):
        if S1CUT < 1:
            return
        prev_sq = [None, None]
        prev_msq = [None, None]
        xnfm_tok = None
        for tc in range(NT):
            b = tc % 2
            cb = 16 * b
            xsl, xnsl, xbsl = SL[f"stx{b}"], SL[f"stxn{b}"], SL[f"stxb{b}"]
            xt, xnt, xbt = st_x[b], st_xn[b], st_xb[b]
            xsl.prep_write(plan, "sync")
            t_ld = plan.ld(lambda E, tc=tc, xt=xt: E.dma_start(
                out=xt[:], in_=x_in[s][tc * 128:(tc + 1) * 128, :]))
            if S1CUT < 2:
                continue
            plan.wait("vector", t_ld)
            plan.wait("vector", prev_sq[b])
            plan.wait("scalar", t_ld)
            plan.wait("scalar", prev_msq[b])
            t_ssq = plan.tok("scalar", lambda E, xt=xt, cb=cb: E.activation(
                scr_bf[:, 0:D], xt[:], AF.Square, bias=0.0, scale=1.0,
                accum_out=w_cols[:, cb:cb + 1]))
            t_sum = plan.tok("vector", lambda E, xt=xt, cb=cb: E.reduce_sum(
                w_cols[:, cb + 1:cb + 2], xt[:], axis=AX.X, negate=True))
            plan.wait("vector", t_sum)
            t_nm = plan.tok("vector", lambda E, cb=cb: E.tensor_scalar(
                out=w_cols[:, cb + 2:cb + 3], in0=w_cols[:, cb + 1:cb + 2],
                scalar1=1.0 / D, scalar2=None, op0=ALU.mult))
            plan.wait("vector", t_ssq)
            t_msq = plan.tok("vector", lambda E, cb=cb: E.tensor_scalar(
                out=w_cols[:, cb + 3:cb + 4], in0=w_cols[:, cb:cb + 1],
                scalar1=1.0 / D, scalar2=None, op0=ALU.mult))
            prev_msq[b] = t_msq
            plan.wait("vector", t_nm)
            plan.wait("vector", t_msq)
            t_nv = plan.tok("vector", lambda E, cb=cb: E.scalar_tensor_tensor(
                out=w_cols[:, cb + 4:cb + 5], in0=w_cols[:, cb + 2:cb + 3],
                scalar=w_cols[:, cb + 2:cb + 3], in1=w_cols[:, cb + 3:cb + 4],
                op0=ALU.mult, op1=ALU.subtract))
            plan.wait("scalar", t_nv)
            plan.wait("scalar", TOK["epsln"])
            plan.wait("scalar", TOK["epsrm"])
            t_sq = plan.tok("scalar", lambda E, cb=cb: E.activation(
                w_cols[:, cb + 5:cb + 6], w_cols[:, cb + 4:cb + 5], AF.Sqrt,
                bias=c_epsln[:, 0:1], scale=-1.0))
            plan.wait("vector", t_sq)
            t_rstd = plan.tok("vector", lambda E, cb=cb: E.reciprocal(
                w_cols[:, cb + 6:cb + 7], w_cols[:, cb + 5:cb + 6]))
            if S1CUT < 3:
                continue
            xnsl.prep_write(plan, "vector")
            plan.wait("vector", t_rstd)
            t_xn = plan.tok("vector", lambda E, xt=xt, xnt=xnt, cb=cb: E.tensor_scalar(
                out=xnt[:], in0=xt[:], scalar1=w_cols[:, cb + 2:cb + 3],
                scalar2=w_cols[:, cb + 6:cb + 7], op0=ALU.add, op1=ALU.mult))
            xsl.users.append(t_xn)
            xnsl.ready = t_xn
            t_r2 = plan.tok("vector", lambda E, cb=cb: E.tensor_scalar(
                out=w_cols[:, cb + 7:cb + 8], in0=w_cols[:, cb + 6:cb + 7],
                scalar1=w_cols[:, cb + 6:cb + 7], scalar2=None, op0=ALU.mult))
            plan.wait("vector", t_r2)
            t_vr = plan.tok("vector", lambda E, cb=cb: E.tensor_scalar(
                out=w_cols[:, cb + 8:cb + 9], in0=w_cols[:, cb + 4:cb + 5],
                scalar1=w_cols[:, cb + 7:cb + 8], scalar2=None, op0=ALU.mult))
            plan.wait("scalar", t_vr)
            t_sq2 = plan.tok("scalar", lambda E, cb=cb: E.activation(
                w_cols[:, cb + 9:cb + 10], w_cols[:, cb + 8:cb + 9], AF.Sqrt,
                bias=c_epsrm[:, 0:1], scale=-1.0))
            prev_sq[b] = t_sq2
            plan.wait("vector", t_sq2)
            t_rs1 = plan.tok("vector", lambda E, tc=tc, cb=cb: E.reciprocal(
                w_rst1[:, tc:tc + 1], w_cols[:, cb + 9:cb + 10]))
            TOK[f"rst1_{tc}"] = t_rs1
            if S1CUT < 4:
                continue
            t_b2 = plan.tok("vector", lambda E, cb=cb: E.tensor_scalar(
                out=w_cols[:, cb + 10:cb + 11], in0=w_cols[:, cb + 2:cb + 3],
                scalar1=w_cols[:, cb + 6:cb + 7], scalar2=None, op0=ALU.mult))
            xbsl.prep_write(plan, "scalar")
            plan.wait("scalar", t_b2)
            plan.wait("scalar", t_ld)
            t_xb = plan.tok("scalar", lambda E, xt=xt, xbt=xbt, cb=cb: E.activation(
                xbt[:], xt[:], AF.Identity, bias=w_cols[:, cb + 10:cb + 11],
                scale=w_cols[:, cb + 6:cb + 7]))
            xbsl.ready = t_xb
            xsl.users.append(t_xb)
            t_st = plan.st(lambda E, xnt=xnt, tc=tc: E.dma_start(
                out=d_xn[tc * 128:(tc + 1) * 128, :], in_=xnt[:]))
            xnsl.users.append(t_st)
            TOK[f"xn_st_{tc}"] = t_st
            plan.wait("tensor", TOK["idbf"])
            plan.wait("tensor", t_xb)
            blocks = [xbt[:, dc * 128:(dc + 1) * 128] for dc in range(ND)]
            dst = xn_fm[:].rearrange("p (d c) -> p d c", d=ND)[:, :, tc * 128:(tc + 1) * 128]
            xnfm_tok = transpose_pack(blocks, dst)
            xbsl.users.append(xnfm_tok)
        if S1CUT < 5:
            return
        SL["xn_fm"].ready = xnfm_tok
        SL["xn_fm"].users = []
        plan.wait("scalar", xnfm_tok)
        t_fmst = plan.st(lambda E: E.dma_start(
            out=d_xnfm.rearrange("e p c -> p e c"),
            in_=xn_fm[:].rearrange("p (e c) -> p e c", e=ND)))
        SL["xn_fm"].users.append(t_fmst)
        TOK["xnfm_st"] = t_fmst

        plan.wait("tensor", TOK["idf"])
        tvr = None
        for g in range((NT + 3) // 4):
            n = min(4, NT - g * 4)
            di, pd = ringD.acquire()
            tl = None
            for i in range(n):
                tc = g * 4 + i
                plan.wait("tensor", TOK[f"rst1_{tc}"])
                tl = plan.tok("tensor", lambda E, tc=tc, i=i, pd=pd: E.matmul(
                    pd[0:1, i * 128:(i + 1) * 128], w_rst1[:, tc:tc + 1], c_idf[:],
                    start=True, stop=True))
            plan.wait("vector", tl)
            tvr = plan.tok("vector", lambda E, g=g, n=n, pd=pd: E.tensor_copy(
                rst1_row[0:1, g * 512:g * 512 + n * 128], pd[0:1, 0:n * 128]))
            ringD.release(di, tvr)
        plan.wait("tensor", tvr)
        tb = None
        for ch in range(C // 512):
            i4, pq = ring4.acquire()
            tm = plan.tok("tensor", lambda E, ch=ch, pq=pq: E.matmul(
                pq, c_onerf[0:1, :], rst1_row[0:1, ch * 512:(ch + 1) * 512],
                start=True, stop=True))
            plan.wait("vector", tm)
            tb = plan.tok("vector", lambda E, ch=ch, pq=pq: E.tensor_copy(
                rstd1_bc[:, ch * 512:(ch + 1) * 512], pq))
            ring4.release(i4, tb)
        SL["rstd1_bc"].ready = tb
        SL["rstd1_bc"].users = []
        load_wt(0)
        load_wt(1)

    WQKV = ["wqa", "wka", "wva"]

    def load_wt(wi):
        nm = WQKV[wi]
        wt, wsl = WT[wi % 2], SL[f"WT{wi % 2}"]
        wsl.prep_write(plan, "sync")
        plan.wait("sync", cast[nm])
        tls = []
        for jd in range(ND):
            tls.append(plan.ld(lambda E, nm=nm, wt=wt, jd=jd: E.dma_start_transpose(
                wt[:, jd * D:(jd + 1) * D], scr[nm][:, jd * 128:(jd + 1) * 128])))
        wsl.ready = tls

    def phase_qkv(s):
        weights = [("wqa", Q_fm, SL["Q_fm"], 0), ("wka", K_fm, SL["K_fm"], ND),
                   ("wva", V_fm, SL["V_fm"], 2 * ND)]
        for wi, (nm, dst, dsl, boff) in enumerate(weights):
            if wi + 1 == 2:
                load_wt(2)
            wt, wsl = WT[wi % 2], SL[f"WT{wi % 2}"]
            wsl.use(plan, "tensor")
            SL["xn_fm"].use(plan, "tensor")
            t_ev = None
            t_mm = None
            for ic in range(ND):
                for th in range(C // 512):
                    i4, pq = ring4.acquire()
                    for jd in range(ND):
                        fn = lambda E, wt=wt, ic=ic, jd=jd, th=th, pq=pq: E.matmul(
                            pq, wt[:, jd * D + ic * 128: jd * D + (ic + 1) * 128],
                            xn_fm[:, jd * C + th * 512: jd * C + (th + 1) * 512],
                            start=(jd == 0), stop=(jd == ND - 1))
                        if jd == ND - 1:
                            t_mm = plan.tok("tensor", fn)
                        else:
                            plan.on("tensor", fn)
                    plan.wait("scalar", t_mm)
                    t_ev = plan.tok("scalar", lambda E, dst=dst, ic=ic, th=th,
                                    boff=boff, pq=pq: E.activation(
                        dst[:, ic * C + th * 512: ic * C + (th + 1) * 512],
                        pq, AF.Identity, bias=c_bcol[:, boff + ic:boff + ic + 1], scale=1.0))
                    ring4.release(i4, t_ev)
            wsl.users.append(t_mm)
            dsl.ready = t_ev
            dsl.users = []
            if nm == "wqa":
                norms(Q_fm, SL["Q_fm"], w_invq, True)
            elif nm == "wka":
                norms(K_fm, SL["K_fm"], w_invk, False)

    norm_chain = [None, None]

    def norms(fm, fsl, inv, isq):
            prev_act, prev_iv = norm_chain
            fsl.use(plan, "vector")
            for dc in range(ND):
                plan.wait("scalar", prev_act)
                t_ssq = plan.tok("scalar", lambda E, fm=fm, dc=dc: E.activation(
                    scr_bf[:, 0:min(C, 2048)], fm[:, dc * C:dc * C + min(C, 2048)],
                    AF.Square, bias=0.0, scale=1.0, accum_out=w_cols[:, 32:33]))
                t_qs = None
                if isq:
                    t_qs = plan.tok("vector", lambda E, fm=fm, dc=dc: E.reduce_sum(
                        w_qsum[:, dc:dc + 1], fm[:, dc * C:(dc + 1) * C], axis=AX.X))
                plan.wait("scalar", t_ssq)
                plan.wait("scalar", prev_iv)
                t_sq = plan.tok("scalar", lambda E: E.activation(
                    w_cols[:, 33:34], w_cols[:, 32:33], AF.Sqrt, bias=0.0, scale=1.0))
                prev_act = t_sq
                plan.wait("vector", t_sq)
                t_iv = plan.tok("vector", lambda E, inv=inv, dc=dc: E.reciprocal(
                    inv[:, dc:dc + 1], w_cols[:, 33:34]))
                prev_iv = t_iv
                plan.wait("vector", t_iv)
                t_sc = plan.tok("vector", lambda E, fm=fm, dc=dc, inv=inv: E.tensor_scalar(
                    out=fm[:, dc * C:(dc + 1) * C], in0=fm[:, dc * C:(dc + 1) * C],
                    scalar1=inv[:, dc:dc + 1], scalar2=None, op0=ALU.mult))
                fsl.ready = t_sc
                if isq:
                    plan.wait("vector", t_qs)
                    plan.tok("vector", lambda E, dc=dc, inv=inv: E.tensor_scalar(
                        out=w_qsum[:, dc:dc + 1], in0=w_qsum[:, dc:dc + 1],
                        scalar1=inv[:, dc:dc + 1], scalar2=None, op0=ALU.mult))
            norm_chain[0], norm_chain[1] = prev_act, prev_iv

    def phase_A(s):
        plan.wait("tensor", TOK["idbf"])
        krow_tok = None
        for jh in range(2):
            SL["Kn_half"].prep_write(plan, "vector")
            SL["K_fm"].use(plan, "tensor")
            kn_tok = None
            for tc in range(NT):
                blocks = [K_fm[:, (jh * 4 + i) * C + tc * 128:(jh * 4 + i) * C + (tc + 1) * 128]
                          for i in range(4)]
                kn_tok = transpose_pack(blocks, Kn_half[:, tc * 512:(tc + 1) * 512])
                plan.wait("vector", kn_tok)
                if jh == 0:
                    t_kr = plan.tok("vector", lambda E, tc=tc: E.reduce_sum(
                        w_krow[:, tc:tc + 1], Kn_half[:, tc * 512:(tc + 1) * 512], axis=AX.X))
                else:
                    t_kp = plan.tok("vector", lambda E, tc=tc: E.reduce_sum(
                        w_cols[:, 34:35], Kn_half[:, tc * 512:(tc + 1) * 512], axis=AX.X))
                    plan.wait("vector", t_kp)
                    t_kr = plan.tok("vector", lambda E, tc=tc: E.tensor_add(
                        w_krow[:, tc:tc + 1], w_krow[:, tc:tc + 1], w_cols[:, 34:35]))
                TOK[f"krow_{tc}"] = t_kr
            SL["Kn_half"].ready = kn_tok
            if jh == 1:
                for tc in range(NT):
                    plan.wait("vector", TOK[f"krow_{tc}"])
                krow_tok = plan.tok("vector", lambda E: E.tensor_copy(w_krbf[:], w_krow[:]))
            for nt in range(ND):
                qb, qsl = Qn_t[nt % 2], SL[f"Qn{nt % 2}"]
                qsl.prep_write(plan, "vector")
                SL["Q_fm"].use(plan, "tensor")
                t_q = None
                for g in range((NT + 7) // 8):
                    n = min(8, NT - g * 8)
                    blocks = [Q_fm[:, nt * C + (g * 8 + i) * 128: nt * C + (g * 8 + i + 1) * 128]
                              for i in range(n)]
                    t_q = transpose_pack(blocks, qb[:, g * 1024:g * 1024 + n * 128])
                qsl.ready = t_q
                qsl.use(plan, "tensor")
                SL["Kn_half"].use(plan, "tensor")
                i4, pq = ring4.acquire()
                t_mm = None
                for tc in range(NT):
                    fn = lambda E, qb=qb, tc=tc, pq=pq: E.matmul(
                        pq, qb[:, tc * 128:(tc + 1) * 128],
                        Kn_half[:, tc * 512:(tc + 1) * 512],
                        start=(tc == 0), stop=(tc == NT - 1))
                    if tc == NT - 1:
                        t_mm = plan.tok("tensor", fn)
                    else:
                        plan.on("tensor", fn)
                qsl.users.append(t_mm)
                SL["Kn_half"].users.append(t_mm)
                plan.wait("vector", t_mm)
                t_ev = plan.tok("vector", lambda E, nt=nt, jh=jh, pq=pq: E.tensor_copy(
                    A_nm[:, nt * D + jh * 512: nt * D + (jh + 1) * 512], pq))
                ring4.release(i4, t_ev)
                TOK[f"anm_{nt}_{jh}"] = t_ev
                if jh == 1:
                    plan.wait("tensor", krow_tok)
                    di, pd = ringD.acquire()
                    t_rs = None
                    for tc in range(NT):
                        fn = lambda E, qb=qb, tc=tc, pd=pd: E.matmul(
                            pd[:, 0:1], qb[:, tc * 128:(tc + 1) * 128],
                            w_krbf[:, tc:tc + 1], start=(tc == 0), stop=(tc == NT - 1))
                        if tc == NT - 1:
                            t_rs = plan.tok("tensor", fn)
                        else:
                            plan.on("tensor", fn)
                    qsl.users.append(t_rs)
                    plan.wait("vector", t_rs)
                    t_rc = plan.tok("vector", lambda E, nt=nt, pd=pd: E.tensor_copy(
                        w_rsc[:, nt:nt + 1], pd[:, 0:1]))
                    ringD.release(di, t_rc)
                    plan.wait("vector", t_rc)
                    t_d1 = plan.tok("vector", lambda E, nt=nt: E.scalar_tensor_tensor(
                        out=w_cols[:, 35:36], in0=w_qsum[:, nt:nt + 1], scalar=EPS_A,
                        in1=w_rsc[:, nt:nt + 1], op0=ALU.mult, op1=ALU.add))
                    plan.wait("vector", t_d1)
                    t_d2 = plan.tok("vector", lambda E: E.tensor_scalar(
                        out=w_cols[:, 36:37], in0=w_cols[:, 35:36], scalar1=float(D),
                        scalar2=None, op0=ALU.add))
                    plan.wait("vector", t_d2)
                    t_d3 = plan.tok("vector", lambda E: E.reciprocal(
                        w_cols[:, 37:38], w_cols[:, 36:37]))
                    plan.wait("vector", t_d3)
                    plan.wait("vector", TOK["gamcol"])
                    t_d4 = plan.tok("vector", lambda E, nt=nt: E.tensor_scalar(
                        out=w_tgc[:, nt:nt + 1], in0=w_cols[:, 37:38],
                        scalar1=c_gamcol[:, 0:1], scalar2=None, op0=ALU.mult))
                    TOK[f"tgc_{nt}"] = t_d4
        t_as = None
        for nt in range(ND):
            plan.wait("vector", TOK[f"tgc_{nt}"])
            plan.wait("vector", TOK[f"anm_{nt}_0"])
            plan.wait("vector", TOK[f"anm_{nt}_1"])
            t_as = plan.tok("vector", lambda E, nt=nt: E.tensor_scalar(
                out=A_nm[:, nt * D:(nt + 1) * D], in0=A_nm[:, nt * D:(nt + 1) * D],
                scalar1=w_tgc[:, nt:nt + 1], scalar2=None, op0=ALU.mult))
            TOK[f"anms_{nt}"] = t_as
        t_aj = None
        for jd in range(ND):
            for nt in range(ND):
                plan.wait("tensor", TOK[f"anms_{nt}"])
            blocks = [A_nm[:, nt * D + jd * 128: nt * D + (jd + 1) * 128] for nt in range(ND)]
            t_aj = transpose_pack(blocks, A_jm[:, jd * D:(jd + 1) * D])
        SL["A_jm"].ready = t_aj
        SL["A_jm"].users = []
        plan.wait("tensor", TOK["idf"])
        tgr = None
        for g in range(ND // 4):
            di, pd = ringD.acquire()
            tl = None
            for i in range(4):
                nt = g * 4 + i
                plan.wait("tensor", TOK[f"tgc_{nt}"])
                tl = plan.tok("tensor", lambda E, nt=nt, i=i, pd=pd: E.matmul(
                    pd[0:1, i * 128:(i + 1) * 128], w_tgc[:, nt:nt + 1], c_idf[:],
                    start=True, stop=True))
            plan.wait("vector", tl)
            tgr = plan.tok("vector", lambda E, g=g, pd=pd: E.tensor_copy(
                tg_row[0:1, g * 512:(g + 1) * 512], pd[0:1, 0:512]))
            ringD.release(di, tgr)
        TOK["tgrow"] = tgr
        SL["V_fm"].use(plan, "tensor")
        plan.wait("tensor", TOK["onebf"])
        tvs = None
        for ch in range(C // 512):
            di, pd = ringD.acquire()
            t_vs = None
            for jd in range(ND):
                fn = lambda E, jd=jd, ch=ch, pd=pd: E.matmul(
                    pd[0:1, :], c_onebf[:, 0:1],
                    V_fm[:, jd * C + ch * 512: jd * C + (ch + 1) * 512],
                    start=(jd == 0), stop=(jd == ND - 1))
                if jd == ND - 1:
                    t_vs = plan.tok("tensor", fn)
                else:
                    plan.on("tensor", fn)
            plan.wait("vector", t_vs)
            tvs = plan.tok("vector", lambda E, ch=ch, pd=pd: E.tensor_copy(
                vs_row[0:1, ch * 512:(ch + 1) * 512], pd[0:1, :]))
            ringD.release(di, tvs)
        TOK["vsrow"] = tvs

    def phase_wv2(s):
        SL["A_jm"].use(plan, "tensor")
        SL["V_fm"].use(plan, "tensor")
        plan.wait("tensor", TOK["vsrow"])
        plan.wait("tensor", TOK["tgrow"])
        for ct in range(NT):
            b = ct % 2
            wsl, ssl = SL[f"stwv{b}"], SL[f"sts{b}"]
            wsl.prep_write(plan, "sync")
            plan.wait("sync", TOK[f"xn_st_{ct}"])
            t_ld = plan.ld(lambda E, ct=ct, b=b: E.dma_start(
                out=st_wv[b][:], in_=d_xn[ct * 128:(ct + 1) * 128, :]))
            wsl.ready = t_ld
            ssl.prep_write(plan, "vector")
            for nh in range(2):
                i4, pq = ring4.acquire()
                plan.on("tensor", lambda E, ct=ct, nh=nh, pq=pq: E.matmul(
                    pq, vs_row[0:1, ct * 128:(ct + 1) * 128],
                    tg_row[0:1, nh * 512:(nh + 1) * 512], start=True, stop=False))
                t_mm = None
                for jd in range(ND):
                    fn = lambda E, ct=ct, nh=nh, jd=jd, pq=pq: E.matmul(
                        pq, V_fm[:, jd * C + ct * 128: jd * C + (ct + 1) * 128],
                        A_jm[:, jd * D + nh * 512: jd * D + (nh + 1) * 512],
                        start=False, stop=(jd == ND - 1))
                    if jd == ND - 1:
                        t_mm = plan.tok("tensor", fn)
                    else:
                        plan.on("tensor", fn)
                SL["A_jm"].users.append(t_mm)
                SL["V_fm"].users.append(t_mm)
                plan.wait("vector", t_mm)
                wsl.use(plan, "vector")
                t_ev = plan.tok("vector", lambda E, b=b, nh=nh, pq=pq: E.tensor_add(
                    st_s[b][:, nh * 512:(nh + 1) * 512], pq,
                    st_wv[b][:, nh * 512:(nh + 1) * 512]))
                ring4.release(i4, t_ev)
                ssl.ready = t_ev
                wsl.users.append(t_ev)
            plan.wait("scalar", ssl.ready)
            t_st = plan.st(lambda E, ct=ct, b=b: E.dma_start(
                out=d_s[ct * 128:(ct + 1) * 128, :], in_=st_s[b][:]))
            ssl.users.append(t_st)
            TOK[f"s_st_{ct}"] = t_st

    def phase_mla(s):
        SL["xn_fm"].prep_write(plan, "sync")
        plan.wait("sync", TOK["xnfm_st"])
        t_x = plan.ld(lambda E: E.dma_start(
            out=xn_fm[:].rearrange("p (e c) -> p e c", e=ND),
            in_=d_xnfm.rearrange("e p c -> p e c")))
        SL["xn_fm"].ready = t_x
        SL["xn_fm"].users = []
        for nm, dst, dsl, a in (("mq", mw_q, SL["mw_q"], ND),
                                ("mdkv", mw_dkv, SL["mw_dkv"], ND),
                                ("muk", mw_uk, SL["mw_uk"], NL)):
            dsl.prep_write(plan, "sync")
            plan.wait("sync", cast[nm])
            t = plan.ld(lambda E, nm=nm, dst=dst, a=a: E.dma_start(
                out=dst[:].rearrange("p (a b) -> p a b", a=a),
                in_=scr[nm].rearrange("(a p) b -> p a b", p=128)))
            dsl.ready = t
            dsl.users = []
        SL["xn_fm"].use(plan, "vector")
        SL["rstd1_bc"].use(plan, "vector")
        t_h1 = None
        for dc in range(ND):
            t_h1 = plan.tok("vector", lambda E, dc=dc: E.scalar_tensor_tensor(
                out=h1_fm[:, dc * C:(dc + 1) * C], in0=xn_fm[:, dc * C:(dc + 1) * C],
                scalar=c_g1col[:, dc:dc + 1], in1=rstd1_bc[:, 0:C],
                op0=ALU.mult, op1=ALU.mult))
        SL["h1_fm"].ready = t_h1
        SL["h1_fm"].users = []
        SL["xn_fm"].users.append(t_h1)
        SL["rstd1_bc"].users.append(t_h1)

        def proj_fm(dst, dsl, wsb, wsl, nin, nout, src, ssl, scale):
            dsl.prep_write(plan, "scalar")
            wsl.use(plan, "tensor")
            ssl.use(plan, "tensor")
            t_ev = None
            for ic in range(nout):
                for th in range(C // 512):
                    i4, pq = ring4.acquire()
                    t_mm = None
                    for jd in range(nin):
                        fn = lambda E, ic=ic, th=th, jd=jd, pq=pq, wsb=wsb, src=src, nout=nout, nin=nin: E.matmul(
                            pq, wsb[:, jd * (nout * 128) + ic * 128: jd * (nout * 128) + (ic + 1) * 128],
                            src[:, jd * C + th * 512: jd * C + (th + 1) * 512],
                            start=(jd == 0), stop=(jd == nin - 1))
                        if jd == nin - 1:
                            t_mm = plan.tok("tensor", fn)
                        else:
                            plan.on("tensor", fn)
                    wsl.users.append(t_mm)
                    ssl.users.append(t_mm)
                    plan.wait("scalar", t_mm)
                    t_ev = plan.tok("scalar", lambda E, ic=ic, th=th, pq=pq, dst=dst, scale=scale: E.activation(
                        dst[:, ic * C + th * 512: ic * C + (th + 1) * 512],
                        pq, AF.Copy, bias=0.0, scale=scale))
                    ring4.release(i4, t_ev)
            dsl.ready = t_ev
            dsl.users = []

        SL["q_fm"].prep_write(plan, "scalar")
        proj_fm(q_fm, SL["q_fm"], mw_q, SL["mw_q"], ND, ND, h1_fm, SL["h1_fm"], 0.125)
        SL["mw_o"].prep_write(plan, "sync")
        for t in SL["mw_q"].users:
            plan.wait("sync", t)
        plan.wait("sync", cast["mo"])
        t_o = plan.ld(lambda E: E.dma_start(
            out=mw_o[:].rearrange("p (a b) -> p a b", a=ND),
            in_=scr["mo"].rearrange("(a p) b -> p a b", p=128)))
        SL["mw_o"].ready = t_o
        SL["mw_o"].users = []
        SL["c_fm"].prep_write(plan, "scalar")
        for t in SL["xn_fm"].users:
            plan.wait("scalar", t)
        proj_fm(c_fm, SL["c_fm"], mw_dkv, SL["mw_dkv"], ND, NL, h1_fm, SL["h1_fm"], 1.0)
        SL["mw_uv"].prep_write(plan, "sync")
        for t in SL["mw_dkv"].users:
            plan.wait("sync", t)
        plan.wait("sync", cast["muv"])
        t_uv = plan.ld(lambda E: E.dma_start(
            out=mw_uv[:].rearrange("p (a b) -> p a b", a=NL),
            in_=scr["muv"].rearrange("(a p) b -> p a b", p=128)))
        SL["mw_uv"].ready = t_uv
        SL["mw_uv"].users = []
        SL["k_fm"].prep_write(plan, "scalar")
        for t in SL["h1_fm"].users:
            plan.wait("scalar", t)
        proj_fm(k_fm, SL["k_fm"], mw_uk, SL["mw_uk"], NL, ND, c_fm, SL["c_fm"], 1.0)
        SL["v_tm"].prep_write(plan, "scalar")
        SL["mw_uv"].use(plan, "tensor")
        SL["c_fm"].use(plan, "tensor")
        t_ev = None
        for tc in range(NT):
            for oh in range(D // 512):
                i4, pq = ring4.acquire()
                t_mm = None
                for jd in range(NL):
                    fn = lambda E, tc=tc, oh=oh, jd=jd, pq=pq: E.matmul(
                        pq, c_fm[:, jd * C + tc * 128: jd * C + (tc + 1) * 128],
                        mw_uv[:, jd * D + oh * 512: jd * D + (oh + 1) * 512],
                        start=(jd == 0), stop=(jd == NL - 1))
                    if jd == NL - 1:
                        t_mm = plan.tok("tensor", fn)
                    else:
                        plan.on("tensor", fn)
                SL["mw_uv"].users.append(t_mm)
                SL["c_fm"].users.append(t_mm)
                plan.wait("scalar", t_mm)
                t_ev = plan.tok("scalar", lambda E, tc=tc, oh=oh, pq=pq: E.activation(
                    v_tm[:, tc * D + oh * 512: tc * D + (oh + 1) * 512],
                    pq, AF.Copy, bias=0.0, scale=1.0))
                ring4.release(i4, t_ev)
        SL["v_tm"].ready = t_ev
        SL["v_tm"].users = []

    def load_w1(e):
        i1 = (2 * e) % 3
        r1, s1 = moe_w[i1], SL[moe_wsl_names[i1]]
        s1.prep_write(plan, "sync")
        plan.wait("sync", cast[f"w1_{e}"])
        t1 = plan.ld(lambda E, e=e, r1=r1: E.dma_start(
            out=r1[:].rearrange("p (a b) -> p a b", a=ND),
            in_=scr["w1"][e].rearrange("(a p) b -> p a b", p=128)))
        s1.ready = t1
        return r1, s1

    def load_w2(e):
        i2 = (2 * e + 1) % 3
        r2, s2 = moe_w[i2], SL[moe_wsl_names[i2]]
        s2.prep_write(plan, "sync")
        plan.wait("sync", cast[f"w2_{e}"])
        t2 = plan.ld(lambda E, e=e, r2=r2: E.dma_start(
            out=r2[:].rearrange("p (a b) -> p a b", a=NH),
            in_=scr["w2"][e].rearrange("(a p) b -> p a b", p=128)))
        s2.ready = t2
        return r2, s2

    def phase_attn(s, qh):
        SL["q_fm"].use(plan, "tensor")
        SL["k_fm"].use(plan, "tensor")
        SL["v_tm"].use(plan, "tensor")
        plan.wait("tensor", TOK["mask"])
        plan.wait("scalar", TOK["shift"])
        plan.wait("tensor", TOK["onebf"])
        prev_rcp = TOK.setdefault("prev_rcp", [None, None, None, None])
        attn_tok = None

        def do_scores(it, h, tp):
            dt = h // 2
            pb = 64 * (h % 2)
            qa = 2 * tp
            qb = qa + 1
            nk = qb + 1
            Pt, Ptsl = P_[it % 2], SL[f"P{it % 2}"]
            Ptsl.prep_write(plan, "scalar")
            exp_toks = []
            for g in range((nk + 1) // 2):
                m = min(2, nk - g * 2)
                si, pS = ring4.acquire()
                t_mm = None
                for i in range(m):
                    kc = g * 2 + i
                    off = i * 256
                    if kc < qb:
                        fn = lambda E, kc=kc, off=off, qa=qa, pS=pS, dt=dt, pb=pb: E.matmul(
                            pS[:, off:off + 256],
                            k_fm[pb:pb + 64, dt * C + kc * 128: dt * C + (kc + 1) * 128],
                            q_fm[pb:pb + 64, dt * C + qa * 128: dt * C + (qa + 2) * 128],
                            start=True, stop=(kc != qa))
                        if kc == qa:
                            plan.on("tensor", fn)
                            fn = lambda E, off=off, pS=pS: E.matmul(
                                pS[:, off:off + 128], c_idbf[:], c_maskbfT[:],
                                start=False, stop=True)
                    else:
                        fn0 = lambda E, kc=kc, off=off, qb=qb, pS=pS, dt=dt, pb=pb: E.matmul(
                            pS[:, off + 128:off + 256],
                            k_fm[pb:pb + 64, dt * C + kc * 128: dt * C + (kc + 1) * 128],
                            q_fm[pb:pb + 64, dt * C + qb * 128: dt * C + (qb + 1) * 128],
                            start=True, stop=False)
                        plan.on("tensor", fn0)
                        fn = lambda E, off=off, pS=pS: E.matmul(
                            pS[:, off + 128:off + 256], c_idbf[:], c_maskbfT[:],
                            start=False, stop=True)
                    if i == m - 1:
                        t_mm = plan.tok("tensor", fn)
                    else:
                        plan.on("tensor", fn)
                plan.wait("scalar", t_mm)
                lastpart = (g * 2 + m - 1 == qb)
                if not lastpart:
                    spans = [(0, m * 256)]
                elif m == 1:
                    spans = [(128, 256)]
                else:
                    spans = [(0, 256), (384, 512)]
                t_exp = None
                for (a, bnd) in spans:
                    t_exp = plan.tok("scalar", lambda E, g=g, Pt=Pt, pS=pS, a=a, bnd=bnd: E.activation(
                        Pt[:, g * 512 + a: g * 512 + bnd], pS[:, a:bnd], AF.Exp,
                        bias=c_shift[:, 0:1], scale=1.0))
                exp_toks.append(t_exp)
                ring4.release(si, t_exp)
            Ptsl.ready = exp_toks[-1]
            return (it, h, tp, exp_toks)

        def do_av(state):
            nonlocal attn_tok
            it, h, tp, exp_toks = state
            qa = 2 * tp
            qb = qa + 1
            Pt, Ptsl = P_[it % 2], SL[f"P{it % 2}"]
            for half, qt in ((0, qa), (1, qb)):
                par = (2 * it + half) % 4
                nk = qt + 1
                ci, pc = ringC.acquire()
                t_grp = None
                for kc in range(nk):
                    plan.wait("tensor", exp_toks[kc // 2])
                    col = kc * 256 + half * 128
                    plan.on("tensor", lambda E, kc=kc, h=h, Pt=Pt, pc=pc, col=col: E.matmul(
                        pc[:, 0:64], Pt[:, col:col + 128],
                        v_tm[:, kc * D + h * 64: kc * D + h * 64 + 64],
                        start=(kc == 0), stop=False))
                    fn_s = lambda E, kc=kc, Pt=Pt, pc=pc, nk=nk, col=col: E.matmul(
                        pc[:, 64:65], Pt[:, col:col + 128],
                        c_onebf[:, 0:1], start=False, stop=(kc == nk - 1))
                    if kc == nk - 1:
                        t_grp = plan.tok("tensor", fn_s)
                    else:
                        plan.on("tensor", fn_s)
                Ptsl.users.append(t_grp)
                plan.wait("vector", t_grp)
                plan.wait("vector", prev_rcp[par])
                t_rcp = plan.tok("vector", lambda E, par=par, pc=pc: E.reciprocal(
                    w_rcp[:, par:par + 1], pc[:, 64:65]))
                plan.wait("vector", t_rcp)
                t_ae = plan.tok("vector", lambda E, qt=qt, qh=qh, h=h, par=par, pc=pc: E.tensor_scalar(
                    out=attn_tm[:, (qt - qh * HT) * D + h * 64:(qt - qh * HT) * D + h * 64 + 64],
                    in0=pc[:, 0:64], scalar1=w_rcp[:, par:par + 1], scalar2=None,
                    op0=ALU.mult))
                ringC.release(ci, t_ae)
                prev_rcp[par] = t_ae
                attn_tok = t_ae

        items = [(h, tp) for h in range(HEADS) for tp in range(qh * HT // 2, qh * HT // 2 + HT // 2)]
        pending = []
        for it, (h, tp) in enumerate(items):
            pending.append(do_scores(it, h, tp))
            if len(pending) > 1:
                do_av(pending.pop(0))
        for st_ in pending:
            do_av(st_)
        SL["attn_tm"].ready = attn_tok
        SL["attn_tm"].users = []
        TOK["attn_done"] = attn_tok

    def phase_wo(s, qh):
        SL["attn_tm"].use(plan, "tensor")
        afm_tok = None
        for tt in range(HT):
            blocks = [attn_tm[:, tt * D + dc * 128: tt * D + (dc + 1) * 128] for dc in range(ND)]
            dst = q_fm[:].rearrange("p (d c) -> p d c", d=ND)[:, :, qh * CH + tt * 128: qh * CH + (tt + 1) * 128]
            afm_tok = transpose_pack(blocks, dst)
        SL["attn_tm"].users.append(afm_tok)
        SL["mw_o"].use(plan, "tensor")
        plan.wait("tensor", afm_tok)
        prev_x2 = None
        h2h_tok = None
        for tt in range(HT):
            T = qh * HT + tt
            b = tt % 2
            wsl = SL[f"stwo{b}"]
            wsl.prep_write(plan, "sync")
            plan.wait("sync", TOK[f"xn_st_{T}"])
            t_ld = plan.ld(lambda E, T=T, b=b: E.dma_start(
                out=st_wo[b][:], in_=d_xn[T * 128:(T + 1) * 128, :]))
            wsl.ready = t_ld
            x2_toks = []
            for oh in range(2):
                i4, pq = ring4.acquire()
                t_mm = None
                for dc in range(ND):
                    fn = lambda E, tt=tt, qh=qh, dc=dc, oh=oh, pq=pq: E.matmul(
                        pq, q_fm[:, dc * C + qh * CH + tt * 128: dc * C + qh * CH + (tt + 1) * 128],
                        mw_o[:, dc * D + oh * 512: dc * D + (oh + 1) * 512],
                        start=(dc == 0), stop=(dc == ND - 1))
                    if dc == ND - 1:
                        t_mm = plan.tok("tensor", fn)
                    else:
                        plan.on("tensor", fn)
                SL["mw_o"].users.append(t_mm)
                plan.wait("vector", t_mm)
                wsl.use(plan, "vector")
                plan.wait("vector", prev_x2)
                if TOK.get("x2_stprev") is not None:
                    plan.wait("vector", TOK["x2_stprev"])
                t_x2 = plan.tok("vector", lambda E, oh=oh, b=b, pq=pq: E.tensor_add(
                    st_x2[:, oh * 512:(oh + 1) * 512], pq, st_wo[b][:, oh * 512:(oh + 1) * 512]))
                ring4.release(i4, t_x2)
                x2_toks.append(t_x2)
                wsl.users.append(t_x2)
            for t in x2_toks:
                plan.wait("scalar", t)
            t_stx2 = plan.st(lambda E, T=T: E.dma_start(
                out=d_x2[T * 128:(T + 1) * 128, :], in_=st_x2[:]))
            TOK[f"x2_st_{T}"] = t_stx2
            TOK["x2_stprev"] = t_stx2
            for t in x2_toks:
                plan.wait("vector", t)
            cb = 38 + 4 * b
            t_ssq = plan.tok("scalar", lambda E, cb=cb: E.activation(
                scr_bf[:, 0:D], st_x2[:], AF.Square, bias=0.0, scale=1.0,
                accum_out=w_cols[:, cb:cb + 1]))
            plan.wait("scalar", t_ssq)
            t_sq = plan.tok("scalar", lambda E, cb=cb: E.activation(
                w_cols[:, cb + 1:cb + 2], w_cols[:, cb:cb + 1], AF.Sqrt,
                bias=c_epsrm[:, 0:1], scale=1.0 / D))
            plan.wait("vector", t_sq)
            t_r2 = plan.tok("vector", lambda E, cb=cb: E.reciprocal(
                w_cols[:, cb + 2:cb + 3], w_cols[:, cb + 1:cb + 2]))
            hsl = SL[f"sth2{b}"]
            hsl.prep_write(plan, "vector")
            plan.wait("vector", t_r2)
            plan.wait("vector", TOK["g2bc"])
            t_h2 = plan.tok("vector", lambda E, b=b, cb=cb: E.scalar_tensor_tensor(
                out=st_h2[b][:], in0=st_x2[:], scalar=w_cols[:, cb + 2:cb + 3],
                in1=c_g2bc[:], op0=ALU.mult, op1=ALU.mult))
            prev_x2 = t_h2
            hsl.ready = t_h2
            plan.wait("tensor", t_h2)
            blocks = [st_h2[b][:, dc * 128:(dc + 1) * 128] for dc in range(ND)]
            dst = h2_fm_h[:].rearrange("p (d c) -> p d c", d=ND)[:, :, tt * 128:(tt + 1) * 128]
            h2h_tok = transpose_pack(blocks, dst)
            hsl.users.append(h2h_tok)
            plan.wait("tensor", h2h_tok)
            plan.wait("tensor", TOK["gw"])
            di, pd = ringD.acquire()
            t_gl = None
            for dc in range(ND):
                fn = lambda E, tt=tt, dc=dc, pd=pd: E.matmul(
                    pd[:, 0:NE],
                    h2_fm_h[:, dc * CH + tt * 128: dc * CH + (tt + 1) * 128],
                    c_gwsb[:, dc * NE:(dc + 1) * NE],
                    start=(dc == 0), stop=(dc == ND - 1))
                if dc == ND - 1:
                    t_gl = plan.tok("tensor", fn)
                else:
                    plan.on("tensor", fn)
            plan.wait("vector", t_gl)
            t_lg = plan.tok("vector", lambda E, T=T, pd=pd: E.tensor_copy(
                w_logit[:, T * NE:(T + 1) * NE], pd[:, 0:NE]))
            ringD.release(di, t_lg)
            plan.wait("vector", t_lg)
            t_m1 = plan.tok("vector", lambda E, T=T: E.reduce_max(
                w_m1[:, T:T + 1], w_logit[:, T * NE:(T + 1) * NE], axis=AX.X, negate=True))
            plan.wait("vector", t_m1)
            t_mk1 = plan.tok("vector", lambda E, T=T: E.tensor_scalar(
                out=w_gsc[:, 0:NE], in0=w_logit[:, T * NE:(T + 1) * NE],
                scalar1=w_m1[:, T:T + 1], scalar2=0.0, op0=ALU.add, op1=ALU.is_equal))
            plan.wait("vector", t_mk1)
            t_msk = plan.tok("vector", lambda E, T=T: E.scalar_tensor_tensor(
                out=w_gsc[:, NE:2 * NE], in0=w_gsc[:, 0:NE], scalar=-1e30,
                in1=w_logit[:, T * NE:(T + 1) * NE], op0=ALU.mult, op1=ALU.add))
            plan.wait("vector", t_msk)
            t_m2 = plan.tok("vector", lambda E, T=T: E.reduce_max(
                w_m2[:, T:T + 1], w_gsc[:, NE:2 * NE], axis=AX.X, negate=True))
            TOK[f"gate_{T}"] = t_m2
        SL["h2_fm_h"].ready = h2h_tok
        plan.wait("scalar", h2h_tok)
        t_h2st = plan.st(lambda E, qh=qh: E.dma_start(
            out=d_h2.rearrange("e p c -> p e c")[:, :, qh * CH:(qh + 1) * CH],
            in_=h2_fm_h[:].rearrange("p (e c) -> p e c", e=ND)))
        SL["h2_fm_h"].users.append(t_h2st)
        TOK[f"h2_st_{qh}"] = t_h2st
        for tt in range(HT):
            T = qh * HT + tt
            plan.wait("scalar", TOK[f"gate_{T}"])
            TOK[f"sig_{T}"] = plan.tok("scalar", lambda E, T=T: E.activation(
                w_gt[:, T:T + 1], w_m1[:, T:T + 1], AF.Sigmoid,
                bias=w_m2[:, T:T + 1], scale=-1.0))
        comb_tok = SL["combine"].ready
        for tt in range(HT):
            T = qh * HT + tt
            plan.wait("vector", TOK[f"sig_{T}"])
            plan.wait("vector", comb_tok)
            t_c1 = plan.tok("vector", lambda E, T=T: E.tensor_scalar(
                out=w_gsc[:, 0:NE], in0=w_logit[:, T * NE:(T + 1) * NE],
                scalar1=w_m1[:, T:T + 1], scalar2=0.0, op0=ALU.add, op1=ALU.is_equal))
            t_c2 = plan.tok("vector", lambda E, T=T: E.tensor_scalar(
                out=w_gsc[:, NE:2 * NE], in0=w_logit[:, T * NE:(T + 1) * NE],
                scalar1=w_m2[:, T:T + 1], scalar2=0.0, op0=ALU.add, op1=ALU.is_equal))
            t_gb = plan.tok("vector", lambda E, T=T: E.tensor_scalar(
                out=w_cols[:, 46:47], in0=w_gt[:, T:T + 1], scalar1=-1.0, scalar2=1.0,
                op0=ALU.mult, op1=ALU.add))
            plan.wait("vector", t_c1)
            plan.wait("vector", t_gb)
            t_t1 = plan.tok("vector", lambda E, T=T: E.tensor_scalar(
                out=w_gsc[:, 0:NE], in0=w_gsc[:, 0:NE], scalar1=w_gt[:, T:T + 1],
                scalar2=None, op0=ALU.mult))
            plan.wait("vector", t_c2)
            plan.wait("vector", t_t1)
            comb_tok = plan.tok("vector", lambda E, T=T: E.scalar_tensor_tensor(
                out=w_comb[:, T * NE:(T + 1) * NE], in0=w_gsc[:, NE:2 * NE],
                scalar=w_cols[:, 46:47], in1=w_gsc[:, 0:NE],
                op0=ALU.mult, op1=ALU.add))
        SL["combine"].ready = comb_tok
        if qh == 1:
            TOK["moe_pf1"] = load_w1(0)
            TOK["moe_pf2"] = load_w2(0)

    def phase_moe(s):
        NCH = C // 512
        for half, acc, asl in ((0, acc0, SL["acc0"]), (1, acc1, SL["acc1"])):
            asl.prep_write(plan, "sync")
            for tt in range(HT):
                plan.wait("sync", TOK[f"x2_st_{half * HT + tt}"])
            t = plan.ld(lambda E, half=half, acc=acc: E.dma_start(
                out=acc[:].rearrange("p (t d) -> p t d", t=HT),
                in_=d_x2.rearrange("(t p) d -> p t d", p=128)[:, half * HT:(half + 1) * HT, :]))
            asl.ready = t
        acc_tok = [None] * NT

        cur1 = TOK["moe_pf1"]
        cur2 = TOK["moe_pf2"]
        hid, h2c = moe_hid, moe_h2c
        hsl, hcsl = SL["hid0"], SL["moe_h0"]
        for e in range(NE):
            r1, s1 = cur1
            r2, s2 = cur2
            if e + 1 < NE:
                nxt1 = load_w1(e + 1)
            for ch in range(NCH):
                hcsl.prep_write(plan, "sync")
                plan.wait("sync", TOK["h2_st_0"])
                plan.wait("sync", TOK["h2_st_1"])
                t_h = plan.ld(lambda E, ch=ch, h2c=h2c: E.dma_start(
                    out=h2c[:].rearrange("p (a b) -> p a b", a=ND),
                    in_=d_h2.rearrange("e p c -> p e c")[:, :, ch * 512:(ch + 1) * 512]))
                hcsl.ready = t_h
                hsl.prep_write(plan, "scalar")
                s1.use(plan, "tensor")
                hcsl.use(plan, "tensor")
                t_sil = None
                for ht in range(NH):
                    i4, pq = ring4.acquire()
                    t_mm = None
                    for dc in range(ND):
                        fn = lambda E, ht=ht, dc=dc, r1=r1, h2c=h2c, pq=pq: E.matmul(
                            pq, r1[:, dc * EXPD + ht * 128: dc * EXPD + (ht + 1) * 128],
                            h2c[:, dc * 512:(dc + 1) * 512],
                            start=(dc == 0), stop=(dc == ND - 1))
                        if dc == ND - 1:
                            t_mm = plan.tok("tensor", fn)
                        else:
                            plan.on("tensor", fn)
                    s1.users.append(t_mm)
                    hcsl.users.append(t_mm)
                    plan.wait("scalar", t_mm)
                    t_sil = plan.tok("scalar", lambda E, ht=ht, hid=hid, pq=pq: E.activation(
                        hid[:, ht * 512:(ht + 1) * 512], pq, AF.Silu,
                        bias=0.0, scale=1.0))
                    ring4.release(i4, t_sil)
                hsl.ready = t_sil
                s2.use(plan, "tensor")
                hsl.use(plan, "tensor")
                for ti in range(4):
                    T = ch * 4 + ti
                    half, tt = T // HT, T % HT
                    acc = acc0 if half == 0 else acc1
                    asl = SL["acc0"] if half == 0 else SL["acc1"]
                    asl.use(plan, "vector")
                    for oh in range(2):
                        ci, pc = ringC.acquire()
                        t_mm = None
                        for ht in range(NH):
                            fn = lambda E, ht=ht, ti=ti, oh=oh, r2=r2, hid=hid, pc=pc: E.matmul(
                                pc, hid[:, ht * 512 + ti * 128: ht * 512 + (ti + 1) * 128],
                                r2[:, ht * D + oh * 512: ht * D + (oh + 1) * 512],
                                start=(ht == 0), stop=(ht == NH - 1))
                            if ht == NH - 1:
                                t_mm = plan.tok("tensor", fn)
                            else:
                                plan.on("tensor", fn)
                        hsl.users.append(t_mm)
                        s2.users.append(t_mm)
                        plan.wait("vector", t_mm)
                        plan.wait("vector", acc_tok[T])
                        plan.wait("vector", SL["combine"].ready)
                        t_ac = plan.tok("vector", lambda E, e=e, T=T, tt=tt, oh=oh, acc=acc, pc=pc: E.scalar_tensor_tensor(
                            out=acc[:, tt * D + oh * 512: tt * D + (oh + 1) * 512],
                            in0=pc, scalar=w_comb[:, T * NE + e:T * NE + e + 1],
                            in1=acc[:, tt * D + oh * 512: tt * D + (oh + 1) * 512],
                            op0=ALU.mult, op1=ALU.add))
                        ringC.release(ci, t_ac)
                        acc_tok[T] = t_ac
                if e == NE - 1:
                    if ch == 0:
                        for t_ in list(SL["moe_w2"].users):
                            plan.wait("sync", t_)
                            plan.wait("scalar", t_)
                    for ti in range(4):
                        TOK["fin_last"] = fin_tile(s, ch * 4 + ti, acc_tok)
            if e + 1 < NE:
                cur2 = load_w2(e + 1)
                cur1 = nxt1
        TOK["acc_tok"] = acc_tok
        return TOK["fin_last"]

    def fin_tile(s, T, acc_tok):
        b = T % 2
        half, tt = T // HT, T % HT
        acc = acc0 if half == 0 else acc1
        aslice = acc[:, tt * D:(tt + 1) * D]
        xsl, ssl, gsl = SL[f"finx{b}"], SL[f"fins{b}"], SL[f"fing{b}"]
        xsl.prep_write(plan, "sync")
        t_lx = plan.ld(lambda E, T=T, b=b: E.dma_start(
            out=fin_x[b][:], in_=x_in[s][T * 128:(T + 1) * 128, :]))
        xsl.ready = t_lx
        ssl.prep_write(plan, "sync")
        plan.wait("sync", TOK[f"s_st_{T}"])
        t_ls = plan.ld(lambda E, T=T, b=b: E.dma_start(
            out=fin_s[b][:], in_=d_s[T * 128:(T + 1) * 128, :]))
        ssl.ready = t_ls
        gsl.prep_write(plan, "scalar")
        plan.wait("scalar", acc_tok[T])
        t_g = plan.tok("scalar", lambda E, b=b, aslice=aslice: E.activation(
            fin_g[b][:], aslice, AF.Silu, bias=0.0, scale=1.0))
        gsl.ready = t_g
        plan.wait("vector", t_g)
        ssl.use(plan, "vector")
        t_m = plan.tok("vector", lambda E, b=b: E.tensor_mul(
            fin_g[b][:], fin_s[b][:], fin_g[b][:]))
        ssl.users.append(t_m)
        plan.wait("vector", t_m)
        xsl.use(plan, "vector")
        t_o = plan.tok("vector", lambda E, b=b, aslice=aslice: E.tensor_add(
            aslice, fin_g[b][:], fin_x[b][:]))
        xsl.users.append(t_o)
        gsl.users.append(t_o)
        plan.wait("scalar", t_o)
        last_st = plan.st(lambda E, T=T, aslice=aslice: E.dma_start(
            out=out_t[s][T * 128:(T + 1) * 128, :], in_=aslice))
        return last_st

    PHASES = [(phase_convert, False), (phase_consts, False)]
    def mk(fn, *a):
        return lambda: fn(*a)
    for s_ in range(NSEQ):
        PHASES += [(mk(phase_s1, s_), True), (mk(phase_qkv, s_), True),
                   (mk(phase_A, s_), True), (mk(phase_wv2, s_), True),
                   (mk(phase_mla, s_), True),
                   (mk(phase_attn, s_, 0), True), (mk(phase_wo, s_, 0), True),
                   (mk(phase_attn, s_, 1), True), (mk(phase_wo, s_, 1), True),
                   (mk(phase_moe, s_), True)]
    final_tok = None
    for pi, (ph, need_bar) in enumerate(PHASES):
        if pi >= max_phases:
            break
        if need_bar:
            plan.barrier()
        r = ph()
        if r is not None:
            final_tok = r
    plan.wait("scalar", final_tok)

    with contextlib.ExitStack() as stack:
        for sname in SEMS:
            plan.semh[sname] = stack.enter_context(nc.semaphore("sem_" + sname))
        with nc.Block() as block:
            @block.sync
            def _(e):
                plan.emit(e, "sync")

            @block.gpsimd
            def _(e):
                plan.emit(e, "gpsimd")

            @block.vector
            def _(e):
                plan.emit(e, "vector")

            @block.scalar
            def _(e):
                plan.emit(e, "scalar")

            @block.tensor
            def _(e):
                plan.emit(e, "tensor")
    return nc


def kernel(**inputs):
    x = np.ascontiguousarray(np.asarray(inputs["L0_attn_x"], np.float32))
    B = x.shape[0]
    ncore = 8
    per = B // ncore
    nc = build_nc(C=x.shape[1], NSEQ=per)
    common = {
        "wqa": np.ascontiguousarray(inputs["L0_attn_Wq"], np.float32),
        "wka": np.ascontiguousarray(inputs["L0_attn_Wk"], np.float32),
        "wva": np.ascontiguousarray(inputs["L0_attn_Wv"], np.float32),
        "bq": np.ascontiguousarray(inputs["L0_attn_bq"], np.float32),
        "bk": np.ascontiguousarray(inputs["L0_attn_bk"], np.float32),
        "bv": np.ascontiguousarray(inputs["L0_attn_bv"], np.float32),
        "gam": np.ascontiguousarray(inputs["L0_attn_gamma"], np.float32),
        "g1": np.ascontiguousarray(inputs["L0_ffn_g1"], np.float32),
        "mq": np.ascontiguousarray(inputs["L0_ffn_Wq"], np.float32),
        "mdkv": np.ascontiguousarray(inputs["L0_ffn_Wdkv"], np.float32),
        "muk": np.ascontiguousarray(inputs["L0_ffn_Wuk"], np.float32),
        "muv": np.ascontiguousarray(inputs["L0_ffn_Wuv"], np.float32),
        "mo": np.ascontiguousarray(inputs["L0_ffn_Wo"], np.float32),
        "g2": np.ascontiguousarray(inputs["L0_ffn_g2"], np.float32),
        "gw": np.ascontiguousarray(inputs["L0_ffn_gate_w"], np.float32),
        "w1": np.ascontiguousarray(inputs["L0_ffn_w1"], np.float32),
        "w2": np.ascontiguousarray(inputs["L0_ffn_w2"], np.float32),
    }
    in_maps = [dict(common, x=np.ascontiguousarray(x[c * per:(c + 1) * per]))
               for c in range(ncore)]
    res = run_bass_kernel_spmd(nc, in_maps, core_ids=list(range(ncore)))
    return np.concatenate([r["out"] for r in res.results], axis=0)


# version 17
# speedup vs baseline: 1.1601x; 1.1601x over previous
import contextlib
import os
import numpy as np
import ml_dtypes
import concourse.bass as bass
import concourse.mybir as mybir
from concourse.bass_utils import run_bass_kernel_spmd

F32 = mybir.dt.float32
BF16 = mybir.dt.bfloat16
AF = mybir.ActivationFunctionType
ALU = mybir.AluOpType
AX = mybir.AxisListType

D = 1024
ND = D // 128
HEADS = 16
HD = 64
LAT = 512
NL = LAT // 128
NE = 8
EXPD = 2048
NH = EXPD // 128
LN_EPS = 1e-5
EPS_A = 1e-6
RMS_EPS = 1e-6
EXP_SHIFT = -4.0

ENGS = ("sync", "gpsimd", "vector", "scalar", "tensor")
OWN = {"vector": "dve", "scalar": "act", "tensor": "pe", "gpsimd": "pool", "sync": "syn"}
NQS = 8
QSEMS = {q: [f"{q}{i}" for i in range(NQS)] for q in ("ld", "st", "gd")}
SEMS = tuple(["pe", "dve", "act", "pool", "syn"] + [s for q in QSEMS.values() for s in q])


class Plan:
    def __init__(self, strict=False):
        self.strict = strict
        self.ops = {e: [] for e in ENGS}
        self.cnt = {s: 0 for s in SEMS}
        self.qidx = {"ld": 0, "st": 0, "gd": 0}
        self.hwm = {e: {} for e in ENGS}
        self.semh = {}

    def on(self, eng, fn):
        self.ops[eng].append(("op", fn))

    def tok(self, eng, fn, n=1):
        sem = OWN[eng]
        if (self.strict or eng == "gpsimd") and self.cnt[sem] > 0:
            self.wait(eng, (sem, self.cnt[sem]))
        self.cnt[sem] += n
        self.ops[eng].append(("tok", fn, sem, n))
        return (sem, self.cnt[sem])

    def wait(self, eng, token):
        if token is None:
            return
        if isinstance(token, list):
            for t in token:
                self.wait(eng, t)
            return
        sem, val = token
        if self.hwm[eng].get(sem, 0) >= val:
            return
        self.hwm[eng][sem] = val
        self.ops[eng].append(("wait", sem, val))

    def _dma(self, eng, q, fn):
        i = self.qidx[q] % NQS
        self.qidx[q] += 1
        sem = QSEMS[q][i]
        if self.cnt[sem] > 0:
            self.wait(eng, (sem, self.cnt[sem]))
        self.cnt[sem] += 16
        self.ops[eng].append(("tok", fn, sem, 16))
        return (sem, self.cnt[sem])

    def ld(self, fn):
        return self._dma("sync", "ld", fn)

    def st(self, fn):
        return self._dma("scalar", "st", fn)

    def gd(self, fn):
        return self._dma("gpsimd", "gd", fn)

    def barrier(self):
        import os
        mode = int(os.environ.get("BARRIER", "2"))
        if mode == 0:
            return
        toks = {e: self.tok(e, lambda E, e=e: E.nop() if e == "sync" else E.drain())
                for e in ENGS}
        dma_toks = [(s, self.cnt[s]) for q in ("ld", "st") for s in QSEMS[q]
                    if self.cnt[s] > 0] if mode >= 2 else []
        for e in ENGS:
            for o in ENGS:
                if o != e:
                    self.wait(e, toks[o])
            for t in dma_toks:
                self.wait(e, t)

    def emit(self, eng_obj, name):
        for item in self.ops[name]:
            kind = item[0]
            if kind == "op":
                item[1](eng_obj)
            elif kind == "tok":
                _, fn, sem, n = item
                fn(eng_obj).then_inc(self.semh[sem], n)
            else:
                _, sem, val = item
                eng_obj.wait_ge(self.semh[sem], val)


class Slot:
    def __init__(self):
        self.ready = None
        self.users = []

    def prep_write(self, plan, eng):
        for t in self.users:
            plan.wait(eng, t)
        self.users = []

    def use(self, plan, eng):
        plan.wait(eng, self.ready)


class PRing:
    def __init__(self, plan, slots):
        self.plan = plan
        self.slots = slots
        self.last = [None] * len(slots)
        self.i = 0

    def acquire(self):
        i = self.i % len(self.slots)
        self.i += 1
        self.plan.wait("tensor", self.last[i])
        return i, self.slots[i]

    def release(self, i, evac_tok):
        self.last[i] = evac_tok


def build_nc(C=2048, NSEQ=4, strict=False, max_phases=99):
    assert C % 512 == 0 and (C // 128) % 2 == 0
    NT = C // 128
    HT = NT // 2
    CH = C // 2
    nc = bass.Bass("TRN2", target_bir_lowering=False, debug=False,
                   detect_race_conditions=strict)
    plan = Plan(strict=strict)
    TOK = {}

    x_in = nc.dram_tensor("x", [NSEQ, C, D], F32, kind="ExternalInput").ap()
    out_t = nc.dram_tensor("out", [NSEQ, C, D], F32, kind="ExternalOutput").ap()
    ins = {}
    for nm, shp in [("wqa", [D, D]), ("wka", [D, D]), ("wva", [D, D]),
                    ("bq", [D]), ("bk", [D]), ("bv", [D]), ("gam", [1]),
                    ("g1", [D]), ("mq", [D, D]), ("mdkv", [D, LAT]),
                    ("muk", [LAT, D]), ("muv", [LAT, D]), ("mo", [D, D]),
                    ("g2", [D]), ("gw", [D, NE]), ("w1", [NE, D, EXPD]),
                    ("w2", [NE, EXPD, D])]:
        ins[nm] = nc.dram_tensor(nm, shp, F32, kind="ExternalInput").ap()
    scr = {}
    for nm, shp in [("wqa", [D, D]), ("wka", [D, D]), ("wva", [D, D]),
                    ("mq", [D, D]), ("mdkv", [D, LAT]), ("muk", [LAT, D]),
                    ("muv", [LAT, D]), ("mo", [D, D]), ("gw", [D, NE]),
                    ("w1", [NE, D, EXPD]), ("w2", [NE, EXPD, D])]:
        scr[nm] = nc.dram_tensor("s_" + nm, shp, BF16).ap()
    d_xn = nc.dram_tensor("d_xn", [C, D], F32).ap()
    d_xnfm = nc.dram_tensor("d_xnfm", [ND, 128, C], BF16).ap()
    d_s = nc.dram_tensor("d_s", [C, D], F32).ap()
    d_x2 = nc.dram_tensor("d_x2", [C, D], F32).ap()
    d_h2 = nc.dram_tensor("d_h2", [ND, 128, C], BF16).ap()

    REG = {}
    for r, nb in [("SA", 32768), ("SB", 32768), ("SC", 32768), ("SD", 32768),
                  ("SE", 32768), ("SF", 16384), ("SG", 8192)]:
        h = nc.alloc_sbuf_tensor("R_" + r, [128, nb], mybir.dt.uint8)
        REG[r] = nc.lookup_mloc(h).addr

    def at(reg, off, name, shape, dtype):
        return nc.alloc_sbuf_tensor_at(name, shape, dtype, offset=REG[reg] + off)

    def alloc(name, shape, dtype):
        return nc.alloc_sbuf_tensor(name, shape, dtype)

    c_idbf = alloc("c_idbf", [128, 128], BF16)
    c_idf = alloc("c_idf", [128, 128], F32)
    c_iotaP = alloc("c_iotaP", [128, 1], F32)
    c_mask = alloc("c_mask", [128, 128], F32)
    c_maskbf = alloc("c_maskbf", [128, 128], BF16)
    c_maskbfT = alloc("c_maskbfT", [128, 128], BF16)
    c_onebf = alloc("c_onebf", [128, 1], BF16)
    c_onerf = alloc("c_onerf", [1, 128], F32)
    c_g2bc = alloc("c_g2bc", [128, D], F32)
    c_g1col = alloc("c_g1col", [128, ND], F32)
    c_bcol = alloc("c_bcol", [128, 3 * ND], F32)
    c_gam = alloc("c_gam", [1, 1], F32)
    c_gamcol = alloc("c_gamcol", [128, 1], F32)
    c_gwsb = alloc("c_gwsb", [128, ND * NE], BF16)
    c_g2row = alloc("c_g2row", [1, D], F32)
    c_rowtmp = alloc("c_rowtmp", [1, D], F32)
    c_epsln = alloc("c_epsln", [128, 1], F32)
    c_epsrm = alloc("c_epsrm", [128, 1], F32)
    c_shift = alloc("c_shift", [128, 1], F32)
    w_cols = alloc("w_cols", [128, 64], F32)
    w_invq = alloc("w_invq", [128, ND], F32)
    w_invk = alloc("w_invk", [128, ND], F32)
    w_qsum = alloc("w_qsum", [128, ND], F32)
    w_rsc = alloc("w_rsc", [128, ND], F32)
    w_tgc = alloc("w_tgc", [128, ND], F32)
    w_krow = alloc("w_krow", [128, NT], F32)
    w_krbf = alloc("w_krbf", [128, NT], BF16)
    w_comb = alloc("w_comb", [128, NT * NE], F32)
    w_rst1 = alloc("w_rst1", [128, NT], F32)
    w_sums = alloc("w_sums", [128, 16], F32)
    w_rcp = alloc("w_rcp", [128, 4], F32)
    w_m1 = alloc("w_m1", [128, NT], F32)
    w_m2 = alloc("w_m2", [128, NT], F32)
    w_gt = alloc("w_gt", [128, NT], F32)
    w_logit = alloc("w_logit", [128, NT * NE], F32)
    w_gsc = alloc("w_gsc", [128, 2 * NE], F32)

    xn_fm = at("SA", 0, "xn_fm", [128, ND * C], BF16)
    A_nm = at("SA", 0, "A_nm", [128, ND * D], BF16)
    A_jm = at("SA", 2 * ND * D, "A_jm", [128, ND * D], BF16)
    c_fm = at("SA", 0, "c_fm", [128, NL * C], BF16)
    P_ = [at("SA", 0, "P0", [128, 2 * C], BF16), at("SA", 4 * C, "P1", [128, 2 * C], BF16)]
    h2_fm_h = at("SA", 0, "h2_fm_h", [128, ND * CH], BF16)
    attn_tm = at("SA", 16384, "attn_tm", [128, HT * D], BF16)
    acc0 = at("SA", 0, "acc0", [128, HT * D], F32)

    st_x = [at("SB", 0, "st_x0", [128, D], F32), at("SB", 4096, "st_x1", [128, D], F32)]
    st_xn = [at("SB", 8192, "st_xn0", [128, D], F32), at("SB", 12288, "st_xn1", [128, D], F32)]
    st_xb = [at("SB", 16384, "st_xb0", [128, D], BF16), at("SB", 18432, "st_xb1", [128, D], BF16)]
    Q_fm = at("SB", 0, "Q_fm", [128, ND * C], BF16)
    h1_fm = at("SB", 0, "h1_fm", [128, ND * C], BF16)
    k_fm = at("SB", 0, "k_fm", [128, ND * C], BF16)
    moe_w0 = at("SB", 0, "moe_w0", [128, 16384], BF16)

    K_fm = at("SC", 0, "K_fm", [128, ND * C], BF16)
    q_fm = at("SC", 0, "q_fm", [128, ND * C], BF16)
    moe_w1 = at("SC", 0, "moe_w1", [128, 16384], BF16)

    V_fm = at("SD", 0, "V_fm", [128, ND * C], BF16)
    v_tm = at("SD", 0, "v_tm", [128, NT * D], BF16)
    moe_w2 = at("SD", 0, "moe_w2", [128, 16384], BF16)
    moe_w = [moe_w0, moe_w2, moe_w1]
    moe_wsl_names = ["moe_w0", "moe_w2", "moe_w1"]

    WT = [at("SE", 0, "WT0", [128, ND * D], BF16),
          at("SE", 2 * ND * D, "WT1", [128, ND * D], BF16)]
    Kn_half = at("SE", 0, "Kn_half", [128, NT * 512], BF16)
    Qn_t = [at("SE", NT * 1024, "Qn_t0", [128, NT * 128], BF16),
            at("SE", NT * 1024 + NT * 256, "Qn_t1", [128, NT * 128], BF16)]
    st_wv = [at("SE", 0, "st_wv0", [128, D], F32), at("SE", 4096, "st_wv1", [128, D], F32)]
    st_s = [at("SE", 8192, "st_s0", [128, D], F32), at("SE", 12288, "st_s1", [128, D], F32)]
    mw_q = at("SE", 0, "mw_q", [128, ND * D], BF16)
    mw_o = at("SE", 0, "mw_o", [128, ND * D], BF16)
    mw_dkv = at("SE", 16384, "mw_dkv", [128, ND * LAT], BF16)
    mw_uv = at("SE", 16384, "mw_uv", [128, NL * D], BF16)
    mw_uk = at("SE", 24576, "mw_uk", [128, NL * D], BF16)
    acc1 = at("SE", 0, "acc1", [128, HT * D], F32)

    vs_row = at("SF", 0, "vs_row", [1, C], F32)
    rst1_row = at("SF", 4096, "rst1_row", [1, C], F32)
    tg_row = at("SF", 8192, "tg_row", [1, D], F32)
    scr_bf = at("SF", 12288, "scr_bf", [128, 2048], BF16)
    st_h2 = [at("SF", 0, "st_h20", [128, D], BF16), at("SF", 2048, "st_h21", [128, D], BF16)]
    st_x2 = at("SF", 4096, "st_x2", [128, D], F32)
    moe_hid = at("SF", 0, "moe_hid", [128, NH * 512], BF16)
    fin_s = [at("SD", 0, "fin_s0", [128, D], F32), at("SD", 4096, "fin_s1", [128, D], F32)]
    fin_g = [at("SD", 8192, "fin_g0", [128, D], F32), at("SD", 12288, "fin_g1", [128, D], F32)]

    rstd1_bc = at("SG", 0, "rstd1_bc", [128, C], F32)
    st_wo = [at("SG", 0, "st_wo0", [128, D], F32), at("SG", 4096, "st_wo1", [128, D], F32)]
    moe_h2c = at("SG", 0, "moe_h2c", [128, ND * 512], BF16)
    fin_x = [at("SD", 16384, "fin_x0", [128, D], F32), at("SD", 20480, "fin_x1", [128, D], F32)]

    pbank = [nc.alloc_psum_tensor(f"p{i}", [128, 512], F32) for i in range(6)]
    psD = pbank[4]
    psT0 = nc.alloc_psum_tensor("psT0", [128, 1024], BF16)
    psT1 = nc.alloc_psum_tensor("psT1", [128, 1024], BF16)

    SL = {}
    for name in ("xn_fm A_nm A_jm c_fm P0 P1 h2_fm_h attn_tm acc0 acc1 "
                 "stx0 stx1 stxn0 stxn1 stxb0 stxb1 Q_fm h1_fm k_fm K_fm q_fm V_fm v_tm "
                 "WT0 WT1 Kn_half Qn0 Qn1 stwv0 stwv1 sts0 sts1 mw_q mw_o mw_dkv mw_uv "
                 "mw_uk moe_w0 moe_w1 moe_w2 moe_h0 moe_h1 hid0 hid1 sth20 sth21 stx2 "
                 "stwo0 stwo1 fins0 fins1 fing0 fing1 finx0 finx1 rstd1_bc combine").split():
        SL[name] = Slot()

    ring4 = PRing(plan, [pbank[i][:] for i in range(4)])
    ringC = PRing(plan, [pbank[4][:], pbank[5][:]])
    ringT = PRing(plan, [psT0[:], psT1[:]])
    ringD = PRing(plan, [psD[:]])

    def transpose_pack4(src_blocks, dst_ap):
        ti, pst = ringT.acquire()
        n = len(src_blocks)
        assert n <= 8
        tl = None
        for i, blk in enumerate(src_blocks):
            fn = lambda E, blk=blk, i=i, pst=pst: E.transpose(
                pst[:, i * 128:(i + 1) * 128], blk, c_idbf[:])
            if i == n - 1:
                tl = plan.tok("tensor", fn)
            else:
                plan.on("tensor", fn)
        plan.wait("vector", tl)
        tv = plan.tok("vector", lambda E, dst_ap=dst_ap, n=n, pst=pst: E.tensor_copy(
            dst_ap, pst[:, 0:n * 128]))
        ringT.release(ti, tv)
        return tv

    def transpose_pack(src_blocks, dst_ap):
        n = len(src_blocks)
        nd = len(dst_ap.ap)
        tv = None
        for g in range(0, n, 8):
            m = min(8, n - g)
            if nd == 2:
                sub = dst_ap[:, g * 128:(g + m) * 128]
            else:
                sub = dst_ap[:, g:g + m, :]
            tv = transpose_pack4(src_blocks[g:g + m], sub)
        return tv

    cast = {}

    def phase_convert():
        for nm in ("wqa", "wka", "wva", "mq", "mdkv", "muk", "muv", "mo", "gw"):
            cast[nm] = plan.gd(lambda E, s_=ins[nm], d_=scr[nm]: E.dma_start(out=d_, in_=s_))
        for e in range(NE):
            cast[f"w1_{e}"] = plan.gd(lambda E, s_=ins["w1"][e], d_=scr["w1"][e]: E.dma_start(out=d_, in_=s_))
            cast[f"w2_{e}"] = plan.gd(lambda E, s_=ins["w2"][e], d_=scr["w2"][e]: E.dma_start(out=d_, in_=s_))

    def phase_consts():
        t_g2 = plan.ld(lambda E: E.dma_start(out=c_g2row[0:1, :], in_=ins["g2"][None, :]))
        t_gam = plan.ld(lambda E: E.dma_start(out=c_gam[0:1, :], in_=ins["gam"][None, :]))
        plan.wait("sync", cast["gw"])
        t_gw = plan.ld(lambda E: E.dma_start(
            out=c_gwsb[:].rearrange("p (dc e) -> p dc e", dc=ND),
            in_=scr["gw"].rearrange("(dc p) e -> p dc e", p=128)))
        TOK["gw"] = t_gw

        tp = plan.tok("gpsimd", lambda E: E.iota(
            c_iotaP[:], pattern=[[0, 1]], base=0, channel_multiplier=1,
            allow_small_or_imprecise_dtypes=True))
        tf = plan.tok("gpsimd", lambda E: E.iota(
            c_mask[:], pattern=[[1, 128]], base=0, channel_multiplier=0,
            allow_small_or_imprecise_dtypes=True))
        plan.wait("vector", tp)
        plan.wait("vector", tf)
        t_idb = plan.tok("vector", lambda E: E.tensor_scalar(
            out=c_idbf[:], in0=c_mask[:], scalar1=c_iotaP[:, 0:1], scalar2=None,
            op0=ALU.is_equal))
        t_idf = plan.tok("vector", lambda E: E.tensor_scalar(
            out=c_idf[:], in0=c_mask[:], scalar1=c_iotaP[:, 0:1], scalar2=None,
            op0=ALU.is_equal))
        plan.wait("vector", t_idf)
        plan.wait("vector", t_idb)
        t_mkT = plan.tok("vector", lambda E: E.tensor_scalar(
            out=c_maskbfT[:], in0=c_mask[:], scalar1=c_iotaP[:, 0:1], scalar2=-1e9,
            op0=ALU.is_lt, op1=ALU.mult))
        plan.wait("vector", t_mkT)
        t_msk = plan.tok("vector", lambda E: E.tensor_scalar(
            out=c_mask[:], in0=c_mask[:], scalar1=c_iotaP[:, 0:1], scalar2=-1e9,
            op0=ALU.is_gt, op1=ALU.mult))
        plan.wait("vector", t_msk)
        t_msk = plan.tok("vector", lambda E: E.tensor_copy(c_maskbf[:], c_mask[:]))
        t_ob = plan.tok("vector", lambda E: E.memset(c_onebf[:], 1.0))
        t_or = plan.tok("vector", lambda E: E.memset(c_onerf[:], 1.0))
        t_e1 = plan.tok("vector", lambda E: E.memset(c_epsln[:], LN_EPS))
        t_e2 = plan.tok("vector", lambda E: E.memset(c_epsrm[:], RMS_EPS))
        t_e3 = plan.tok("vector", lambda E: E.memset(c_shift[:], EXP_SHIFT))
        TOK["epsln"], TOK["epsrm"], TOK["shift"] = t_e1, t_e2, t_e3
        TOK["idbf"], TOK["idf"], TOK["mask"] = t_idb, t_idf, t_msk
        TOK["onebf"], TOK["onerf"] = t_ob, t_or

        specs = [("g1", c_g1col, 0), ("bq", c_bcol, 0),
                 ("bk", c_bcol, ND), ("bv", c_bcol, 2 * ND)]
        plan.wait("tensor", t_or)
        prev_cols = None
        for nm, dst, off in specs:
            for t in ([prev_cols] if prev_cols else []):
                plan.wait("sync", t)
            tk = plan.ld(lambda E, nm=nm: E.dma_start(
                out=c_rowtmp[0:1, :], in_=ins[nm][None, :]))
            plan.wait("tensor", tk)
            di, pd = ringD.acquire()
            tm = None
            for dc in range(ND):
                tm = plan.tok("tensor", lambda E, dc=dc, pd=pd: E.matmul(
                    pd[:, dc:dc + 1], c_rowtmp[0:1, dc * 128:(dc + 1) * 128],
                    c_onerf[0:1, 0:1], start=True, stop=True))
            plan.wait("vector", tm)
            tv = plan.tok("vector", lambda E, dst=dst, off=off, pd=pd: E.tensor_copy(
                dst[:, off:off + ND], pd[:, 0:ND]))
            ringD.release(di, tv)
            prev_cols = tm
            TOK["cols"] = tv
        plan.wait("tensor", t_gam)
        di, pd = ringD.acquire()
        t7 = plan.tok("tensor", lambda E, pd=pd: E.matmul(
            pd[:, 0:1], c_onerf[0:1, :], c_gam[0:1, 0:1], start=True, stop=True))
        plan.wait("vector", t7)
        t8 = plan.tok("vector", lambda E, pd=pd: E.tensor_copy(c_gamcol[:, 0:1], pd[:, 0:1]))
        ringD.release(di, t8)
        TOK["gamcol"] = t8
        plan.wait("tensor", t_g2)
        plan.wait("tensor", TOK["cols"])
        tv = None
        for half in range(D // 512):
            i4, pq = ring4.acquire()
            tm = plan.tok("tensor", lambda E, half=half, pq=pq: E.matmul(
                pq, c_onerf[0:1, :], c_g2row[0:1, half * 512:(half + 1) * 512],
                start=True, stop=True))
            plan.wait("vector", tm)
            tv = plan.tok("vector", lambda E, half=half, pq=pq: E.tensor_copy(
                c_g2bc[:, half * 512:(half + 1) * 512], pq))
            ring4.release(i4, tv)
        TOK["g2bc"] = tv
        if os.environ.get("GWWAIT"):
            plan.wait("tensor", TOK["gw"])

    S1CUT = int(os.environ.get("S1CUT", "99"))

    def phase_s1(s):
        if S1CUT < 1:
            return
        prev_sq = [None, None]
        prev_msq = [None, None]
        xnfm_tok = None
        for tc in range(NT):
            b = tc % 2
            cb = 16 * b
            xsl, xnsl, xbsl = SL[f"stx{b}"], SL[f"stxn{b}"], SL[f"stxb{b}"]
            xt, xnt, xbt = st_x[b], st_xn[b], st_xb[b]
            xsl.prep_write(plan, "sync")
            t_ld = plan.ld(lambda E, tc=tc, xt=xt: E.dma_start(
                out=xt[:], in_=x_in[s][tc * 128:(tc + 1) * 128, :]))
            if S1CUT < 2:
                continue
            plan.wait("vector", t_ld)
            plan.wait("vector", prev_sq[b])
            plan.wait("scalar", t_ld)
            plan.wait("scalar", prev_msq[b])
            t_ssq = plan.tok("scalar", lambda E, xt=xt, cb=cb: E.activation(
                scr_bf[:, 0:D], xt[:], AF.Square, bias=0.0, scale=1.0,
                accum_out=w_cols[:, cb:cb + 1]))
            t_sum = plan.tok("vector", lambda E, xt=xt, cb=cb: E.reduce_sum(
                w_cols[:, cb + 1:cb + 2], xt[:], axis=AX.X, negate=True))
            plan.wait("vector", t_sum)
            t_nm = plan.tok("vector", lambda E, cb=cb: E.tensor_scalar(
                out=w_cols[:, cb + 2:cb + 3], in0=w_cols[:, cb + 1:cb + 2],
                scalar1=1.0 / D, scalar2=None, op0=ALU.mult))
            plan.wait("vector", t_ssq)
            t_msq = plan.tok("vector", lambda E, cb=cb: E.tensor_scalar(
                out=w_cols[:, cb + 3:cb + 4], in0=w_cols[:, cb:cb + 1],
                scalar1=1.0 / D, scalar2=None, op0=ALU.mult))
            prev_msq[b] = t_msq
            plan.wait("vector", t_nm)
            plan.wait("vector", t_msq)
            t_nv = plan.tok("vector", lambda E, cb=cb: E.scalar_tensor_tensor(
                out=w_cols[:, cb + 4:cb + 5], in0=w_cols[:, cb + 2:cb + 3],
                scalar=w_cols[:, cb + 2:cb + 3], in1=w_cols[:, cb + 3:cb + 4],
                op0=ALU.mult, op1=ALU.subtract))
            plan.wait("scalar", t_nv)
            plan.wait("scalar", TOK["epsln"])
            plan.wait("scalar", TOK["epsrm"])
            t_sq = plan.tok("scalar", lambda E, cb=cb: E.activation(
                w_cols[:, cb + 5:cb + 6], w_cols[:, cb + 4:cb + 5], AF.Sqrt,
                bias=c_epsln[:, 0:1], scale=-1.0))
            plan.wait("vector", t_sq)
            t_rstd = plan.tok("vector", lambda E, cb=cb: E.reciprocal(
                w_cols[:, cb + 6:cb + 7], w_cols[:, cb + 5:cb + 6]))
            if S1CUT < 3:
                continue
            xnsl.prep_write(plan, "vector")
            plan.wait("vector", t_rstd)
            t_xn = plan.tok("vector", lambda E, xt=xt, xnt=xnt, cb=cb: E.tensor_scalar(
                out=xnt[:], in0=xt[:], scalar1=w_cols[:, cb + 2:cb + 3],
                scalar2=w_cols[:, cb + 6:cb + 7], op0=ALU.add, op1=ALU.mult))
            xsl.users.append(t_xn)
            xnsl.ready = t_xn
            t_r2 = plan.tok("vector", lambda E, cb=cb: E.tensor_scalar(
                out=w_cols[:, cb + 7:cb + 8], in0=w_cols[:, cb + 6:cb + 7],
                scalar1=w_cols[:, cb + 6:cb + 7], scalar2=None, op0=ALU.mult))
            plan.wait("vector", t_r2)
            t_vr = plan.tok("vector", lambda E, cb=cb: E.tensor_scalar(
                out=w_cols[:, cb + 8:cb + 9], in0=w_cols[:, cb + 4:cb + 5],
                scalar1=w_cols[:, cb + 7:cb + 8], scalar2=None, op0=ALU.mult))
            plan.wait("scalar", t_vr)
            t_sq2 = plan.tok("scalar", lambda E, cb=cb: E.activation(
                w_cols[:, cb + 9:cb + 10], w_cols[:, cb + 8:cb + 9], AF.Sqrt,
                bias=c_epsrm[:, 0:1], scale=-1.0))
            prev_sq[b] = t_sq2
            plan.wait("vector", t_sq2)
            t_rs1 = plan.tok("vector", lambda E, tc=tc, cb=cb: E.reciprocal(
                w_rst1[:, tc:tc + 1], w_cols[:, cb + 9:cb + 10]))
            TOK[f"rst1_{tc}"] = t_rs1
            if S1CUT < 4:
                continue
            t_b2 = plan.tok("vector", lambda E, cb=cb: E.tensor_scalar(
                out=w_cols[:, cb + 10:cb + 11], in0=w_cols[:, cb + 2:cb + 3],
                scalar1=w_cols[:, cb + 6:cb + 7], scalar2=None, op0=ALU.mult))
            xbsl.prep_write(plan, "scalar")
            plan.wait("scalar", t_b2)
            plan.wait("scalar", t_ld)
            t_xb = plan.tok("scalar", lambda E, xt=xt, xbt=xbt, cb=cb: E.activation(
                xbt[:], xt[:], AF.Identity, bias=w_cols[:, cb + 10:cb + 11],
                scale=w_cols[:, cb + 6:cb + 7]))
            xbsl.ready = t_xb
            xsl.users.append(t_xb)
            t_st = plan.st(lambda E, xnt=xnt, tc=tc: E.dma_start(
                out=d_xn[tc * 128:(tc + 1) * 128, :], in_=xnt[:]))
            xnsl.users.append(t_st)
            TOK[f"xn_st_{tc}"] = t_st
            plan.wait("tensor", TOK["idbf"])
            plan.wait("tensor", t_xb)
            blocks = [xbt[:, dc * 128:(dc + 1) * 128] for dc in range(ND)]
            dst = xn_fm[:].rearrange("p (d c) -> p d c", d=ND)[:, :, tc * 128:(tc + 1) * 128]
            xnfm_tok = transpose_pack(blocks, dst)
            xbsl.users.append(xnfm_tok)
        if S1CUT < 5:
            return
        SL["xn_fm"].ready = xnfm_tok
        SL["xn_fm"].users = []
        plan.wait("scalar", xnfm_tok)
        t_fmst = plan.st(lambda E: E.dma_start(
            out=d_xnfm.rearrange("e p c -> p e c"),
            in_=xn_fm[:].rearrange("p (e c) -> p e c", e=ND)))
        SL["xn_fm"].users.append(t_fmst)
        TOK["xnfm_st"] = t_fmst

        plan.wait("tensor", TOK["idf"])
        tvr = None
        for g in range((NT + 3) // 4):
            n = min(4, NT - g * 4)
            di, pd = ringD.acquire()
            tl = None
            for i in range(n):
                tc = g * 4 + i
                plan.wait("tensor", TOK[f"rst1_{tc}"])
                tl = plan.tok("tensor", lambda E, tc=tc, i=i, pd=pd: E.matmul(
                    pd[0:1, i * 128:(i + 1) * 128], w_rst1[:, tc:tc + 1], c_idf[:],
                    start=True, stop=True))
            plan.wait("vector", tl)
            tvr = plan.tok("vector", lambda E, g=g, n=n, pd=pd: E.tensor_copy(
                rst1_row[0:1, g * 512:g * 512 + n * 128], pd[0:1, 0:n * 128]))
            ringD.release(di, tvr)
        plan.wait("tensor", tvr)
        tb = None
        for ch in range(C // 512):
            i4, pq = ring4.acquire()
            tm = plan.tok("tensor", lambda E, ch=ch, pq=pq: E.matmul(
                pq, c_onerf[0:1, :], rst1_row[0:1, ch * 512:(ch + 1) * 512],
                start=True, stop=True))
            plan.wait("vector", tm)
            tb = plan.tok("vector", lambda E, ch=ch, pq=pq: E.tensor_copy(
                rstd1_bc[:, ch * 512:(ch + 1) * 512], pq))
            ring4.release(i4, tb)
        SL["rstd1_bc"].ready = tb
        SL["rstd1_bc"].users = []
        load_wt(0)
        load_wt(1)

    WQKV = ["wqa", "wka", "wva"]

    def load_wt(wi):
        nm = WQKV[wi]
        wt, wsl = WT[wi % 2], SL[f"WT{wi % 2}"]
        wsl.prep_write(plan, "sync")
        plan.wait("sync", cast[nm])
        tls = []
        for jd in range(ND):
            tls.append(plan.ld(lambda E, nm=nm, wt=wt, jd=jd: E.dma_start_transpose(
                wt[:, jd * D:(jd + 1) * D], scr[nm][:, jd * 128:(jd + 1) * 128])))
        wsl.ready = tls

    def phase_qkv(s):
        weights = [("wqa", Q_fm, SL["Q_fm"], 0), ("wka", K_fm, SL["K_fm"], ND),
                   ("wva", V_fm, SL["V_fm"], 2 * ND)]
        for wi, (nm, dst, dsl, boff) in enumerate(weights):
            if wi + 1 == 2:
                load_wt(2)
            wt, wsl = WT[wi % 2], SL[f"WT{wi % 2}"]
            wsl.use(plan, "tensor")
            SL["xn_fm"].use(plan, "tensor")
            t_ev = None
            t_mm = None
            for ic in range(ND):
                for th in range(C // 512):
                    i4, pq = ring4.acquire()
                    for jd in range(ND):
                        fn = lambda E, wt=wt, ic=ic, jd=jd, th=th, pq=pq: E.matmul(
                            pq, wt[:, jd * D + ic * 128: jd * D + (ic + 1) * 128],
                            xn_fm[:, jd * C + th * 512: jd * C + (th + 1) * 512],
                            start=(jd == 0), stop=(jd == ND - 1))
                        if jd == ND - 1:
                            t_mm = plan.tok("tensor", fn)
                        else:
                            plan.on("tensor", fn)
                    plan.wait("scalar", t_mm)
                    t_ev = plan.tok("scalar", lambda E, dst=dst, ic=ic, th=th,
                                    boff=boff, pq=pq: E.activation(
                        dst[:, ic * C + th * 512: ic * C + (th + 1) * 512],
                        pq, AF.Identity, bias=c_bcol[:, boff + ic:boff + ic + 1], scale=1.0))
                    ring4.release(i4, t_ev)
            wsl.users.append(t_mm)
            dsl.ready = t_ev
            dsl.users = []
            if nm == "wqa":
                norms(Q_fm, SL["Q_fm"], w_invq, True)
            elif nm == "wka":
                norms(K_fm, SL["K_fm"], w_invk, False)

    norm_chain = [None, None]

    def norms(fm, fsl, inv, isq):
            prev_act, prev_iv = norm_chain
            fsl.use(plan, "vector")
            for dc in range(ND):
                plan.wait("scalar", prev_act)
                t_ssq = plan.tok("scalar", lambda E, fm=fm, dc=dc: E.activation(
                    scr_bf[:, 0:min(C, 2048)], fm[:, dc * C:dc * C + min(C, 2048)],
                    AF.Square, bias=0.0, scale=1.0, accum_out=w_cols[:, 32:33]))
                t_qs = None
                if isq:
                    t_qs = plan.tok("vector", lambda E, fm=fm, dc=dc: E.reduce_sum(
                        w_qsum[:, dc:dc + 1], fm[:, dc * C:(dc + 1) * C], axis=AX.X))
                plan.wait("scalar", t_ssq)
                plan.wait("scalar", prev_iv)
                t_sq = plan.tok("scalar", lambda E: E.activation(
                    w_cols[:, 33:34], w_cols[:, 32:33], AF.Sqrt, bias=0.0, scale=1.0))
                prev_act = t_sq
                plan.wait("vector", t_sq)
                t_iv = plan.tok("vector", lambda E, inv=inv, dc=dc: E.reciprocal(
                    inv[:, dc:dc + 1], w_cols[:, 33:34]))
                prev_iv = t_iv
                plan.wait("vector", t_iv)
                t_sc = plan.tok("vector", lambda E, fm=fm, dc=dc, inv=inv: E.tensor_scalar(
                    out=fm[:, dc * C:(dc + 1) * C], in0=fm[:, dc * C:(dc + 1) * C],
                    scalar1=inv[:, dc:dc + 1], scalar2=None, op0=ALU.mult))
                fsl.ready = t_sc
                if isq:
                    plan.wait("vector", t_qs)
                    plan.tok("vector", lambda E, dc=dc, inv=inv: E.tensor_scalar(
                        out=w_qsum[:, dc:dc + 1], in0=w_qsum[:, dc:dc + 1],
                        scalar1=inv[:, dc:dc + 1], scalar2=None, op0=ALU.mult))
            norm_chain[0], norm_chain[1] = prev_act, prev_iv

    def phase_A(s):
        plan.wait("tensor", TOK["idbf"])
        krow_tok = None
        for jh in range(2):
            SL["Kn_half"].prep_write(plan, "vector")
            SL["K_fm"].use(plan, "tensor")
            kn_tok = None
            for tc in range(NT):
                blocks = [K_fm[:, (jh * 4 + i) * C + tc * 128:(jh * 4 + i) * C + (tc + 1) * 128]
                          for i in range(4)]
                kn_tok = transpose_pack(blocks, Kn_half[:, tc * 512:(tc + 1) * 512])
                plan.wait("vector", kn_tok)
                if jh == 0:
                    t_kr = plan.tok("vector", lambda E, tc=tc: E.reduce_sum(
                        w_krow[:, tc:tc + 1], Kn_half[:, tc * 512:(tc + 1) * 512], axis=AX.X))
                else:
                    t_kp = plan.tok("vector", lambda E, tc=tc: E.reduce_sum(
                        w_cols[:, 34:35], Kn_half[:, tc * 512:(tc + 1) * 512], axis=AX.X))
                    plan.wait("vector", t_kp)
                    t_kr = plan.tok("vector", lambda E, tc=tc: E.tensor_add(
                        w_krow[:, tc:tc + 1], w_krow[:, tc:tc + 1], w_cols[:, 34:35]))
                TOK[f"krow_{tc}"] = t_kr
            SL["Kn_half"].ready = kn_tok
            if jh == 1:
                for tc in range(NT):
                    plan.wait("vector", TOK[f"krow_{tc}"])
                krow_tok = plan.tok("vector", lambda E: E.tensor_copy(w_krbf[:], w_krow[:]))
            for nt in range(ND):
                qb, qsl = Qn_t[nt % 2], SL[f"Qn{nt % 2}"]
                qsl.prep_write(plan, "vector")
                SL["Q_fm"].use(plan, "tensor")
                t_q = None
                for g in range((NT + 7) // 8):
                    n = min(8, NT - g * 8)
                    blocks = [Q_fm[:, nt * C + (g * 8 + i) * 128: nt * C + (g * 8 + i + 1) * 128]
                              for i in range(n)]
                    t_q = transpose_pack(blocks, qb[:, g * 1024:g * 1024 + n * 128])
                qsl.ready = t_q
                qsl.use(plan, "tensor")
                SL["Kn_half"].use(plan, "tensor")
                i4, pq = ring4.acquire()
                t_mm = None
                for tc in range(NT):
                    fn = lambda E, qb=qb, tc=tc, pq=pq: E.matmul(
                        pq, qb[:, tc * 128:(tc + 1) * 128],
                        Kn_half[:, tc * 512:(tc + 1) * 512],
                        start=(tc == 0), stop=(tc == NT - 1))
                    if tc == NT - 1:
                        t_mm = plan.tok("tensor", fn)
                    else:
                        plan.on("tensor", fn)
                qsl.users.append(t_mm)
                SL["Kn_half"].users.append(t_mm)
                plan.wait("vector", t_mm)
                t_ev = plan.tok("vector", lambda E, nt=nt, jh=jh, pq=pq: E.tensor_copy(
                    A_nm[:, nt * D + jh * 512: nt * D + (jh + 1) * 512], pq))
                ring4.release(i4, t_ev)
                TOK[f"anm_{nt}_{jh}"] = t_ev
                if jh == 1:
                    plan.wait("tensor", krow_tok)
                    di, pd = ringD.acquire()
                    t_rs = None
                    for tc in range(NT):
                        fn = lambda E, qb=qb, tc=tc, pd=pd: E.matmul(
                            pd[:, 0:1], qb[:, tc * 128:(tc + 1) * 128],
                            w_krbf[:, tc:tc + 1], start=(tc == 0), stop=(tc == NT - 1))
                        if tc == NT - 1:
                            t_rs = plan.tok("tensor", fn)
                        else:
                            plan.on("tensor", fn)
                    qsl.users.append(t_rs)
                    plan.wait("vector", t_rs)
                    t_rc = plan.tok("vector", lambda E, nt=nt, pd=pd: E.tensor_copy(
                        w_rsc[:, nt:nt + 1], pd[:, 0:1]))
                    ringD.release(di, t_rc)
                    plan.wait("vector", t_rc)
                    t_d1 = plan.tok("vector", lambda E, nt=nt: E.scalar_tensor_tensor(
                        out=w_cols[:, 35:36], in0=w_qsum[:, nt:nt + 1], scalar=EPS_A,
                        in1=w_rsc[:, nt:nt + 1], op0=ALU.mult, op1=ALU.add))
                    plan.wait("vector", t_d1)
                    t_d2 = plan.tok("vector", lambda E: E.tensor_scalar(
                        out=w_cols[:, 36:37], in0=w_cols[:, 35:36], scalar1=float(D),
                        scalar2=None, op0=ALU.add))
                    plan.wait("vector", t_d2)
                    t_d3 = plan.tok("vector", lambda E: E.reciprocal(
                        w_cols[:, 37:38], w_cols[:, 36:37]))
                    plan.wait("vector", t_d3)
                    plan.wait("vector", TOK["gamcol"])
                    t_d4 = plan.tok("vector", lambda E, nt=nt: E.tensor_scalar(
                        out=w_tgc[:, nt:nt + 1], in0=w_cols[:, 37:38],
                        scalar1=c_gamcol[:, 0:1], scalar2=None, op0=ALU.mult))
                    TOK[f"tgc_{nt}"] = t_d4
        t_as = None
        for nt in range(ND):
            plan.wait("vector", TOK[f"tgc_{nt}"])
            plan.wait("vector", TOK[f"anm_{nt}_0"])
            plan.wait("vector", TOK[f"anm_{nt}_1"])
            t_as = plan.tok("vector", lambda E, nt=nt: E.tensor_scalar(
                out=A_nm[:, nt * D:(nt + 1) * D], in0=A_nm[:, nt * D:(nt + 1) * D],
                scalar1=w_tgc[:, nt:nt + 1], scalar2=None, op0=ALU.mult))
            TOK[f"anms_{nt}"] = t_as
        t_aj = None
        for jd in range(ND):
            for nt in range(ND):
                plan.wait("tensor", TOK[f"anms_{nt}"])
            blocks = [A_nm[:, nt * D + jd * 128: nt * D + (jd + 1) * 128] for nt in range(ND)]
            t_aj = transpose_pack(blocks, A_jm[:, jd * D:(jd + 1) * D])
        SL["A_jm"].ready = t_aj
        SL["A_jm"].users = []
        plan.wait("tensor", TOK["idf"])
        tgr = None
        for g in range(ND // 4):
            di, pd = ringD.acquire()
            tl = None
            for i in range(4):
                nt = g * 4 + i
                plan.wait("tensor", TOK[f"tgc_{nt}"])
                tl = plan.tok("tensor", lambda E, nt=nt, i=i, pd=pd: E.matmul(
                    pd[0:1, i * 128:(i + 1) * 128], w_tgc[:, nt:nt + 1], c_idf[:],
                    start=True, stop=True))
            plan.wait("vector", tl)
            tgr = plan.tok("vector", lambda E, g=g, pd=pd: E.tensor_copy(
                tg_row[0:1, g * 512:(g + 1) * 512], pd[0:1, 0:512]))
            ringD.release(di, tgr)
        TOK["tgrow"] = tgr
        SL["V_fm"].use(plan, "tensor")
        plan.wait("tensor", TOK["onebf"])
        tvs = None
        for ch in range(C // 512):
            di, pd = ringD.acquire()
            t_vs = None
            for jd in range(ND):
                fn = lambda E, jd=jd, ch=ch, pd=pd: E.matmul(
                    pd[0:1, :], c_onebf[:, 0:1],
                    V_fm[:, jd * C + ch * 512: jd * C + (ch + 1) * 512],
                    start=(jd == 0), stop=(jd == ND - 1))
                if jd == ND - 1:
                    t_vs = plan.tok("tensor", fn)
                else:
                    plan.on("tensor", fn)
            plan.wait("vector", t_vs)
            tvs = plan.tok("vector", lambda E, ch=ch, pd=pd: E.tensor_copy(
                vs_row[0:1, ch * 512:(ch + 1) * 512], pd[0:1, :]))
            ringD.release(di, tvs)
        TOK["vsrow"] = tvs

    def phase_wv2(s):
        SL["A_jm"].use(plan, "tensor")
        SL["V_fm"].use(plan, "tensor")
        plan.wait("tensor", TOK["vsrow"])
        plan.wait("tensor", TOK["tgrow"])
        for ct in range(NT):
            b = ct % 2
            wsl, ssl = SL[f"stwv{b}"], SL[f"sts{b}"]
            wsl.prep_write(plan, "sync")
            plan.wait("sync", TOK[f"xn_st_{ct}"])
            t_ld = plan.ld(lambda E, ct=ct, b=b: E.dma_start(
                out=st_wv[b][:], in_=d_xn[ct * 128:(ct + 1) * 128, :]))
            wsl.ready = t_ld
            ssl.prep_write(plan, "vector")
            for nh in range(2):
                i4, pq = ring4.acquire()
                plan.on("tensor", lambda E, ct=ct, nh=nh, pq=pq: E.matmul(
                    pq, vs_row[0:1, ct * 128:(ct + 1) * 128],
                    tg_row[0:1, nh * 512:(nh + 1) * 512], start=True, stop=False))
                t_mm = None
                for jd in range(ND):
                    fn = lambda E, ct=ct, nh=nh, jd=jd, pq=pq: E.matmul(
                        pq, V_fm[:, jd * C + ct * 128: jd * C + (ct + 1) * 128],
                        A_jm[:, jd * D + nh * 512: jd * D + (nh + 1) * 512],
                        start=False, stop=(jd == ND - 1))
                    if jd == ND - 1:
                        t_mm = plan.tok("tensor", fn)
                    else:
                        plan.on("tensor", fn)
                SL["A_jm"].users.append(t_mm)
                SL["V_fm"].users.append(t_mm)
                plan.wait("vector", t_mm)
                wsl.use(plan, "vector")
                t_ev = plan.tok("vector", lambda E, b=b, nh=nh, pq=pq: E.tensor_add(
                    st_s[b][:, nh * 512:(nh + 1) * 512], pq,
                    st_wv[b][:, nh * 512:(nh + 1) * 512]))
                ring4.release(i4, t_ev)
                ssl.ready = t_ev
                wsl.users.append(t_ev)
            plan.wait("scalar", ssl.ready)
            t_st = plan.st(lambda E, ct=ct, b=b: E.dma_start(
                out=d_s[ct * 128:(ct + 1) * 128, :], in_=st_s[b][:]))
            ssl.users.append(t_st)
            TOK[f"s_st_{ct}"] = t_st

    def phase_mla(s):
        SL["xn_fm"].prep_write(plan, "sync")
        plan.wait("sync", TOK["xnfm_st"])
        t_x = plan.ld(lambda E: E.dma_start(
            out=xn_fm[:].rearrange("p (e c) -> p e c", e=ND),
            in_=d_xnfm.rearrange("e p c -> p e c")))
        SL["xn_fm"].ready = t_x
        SL["xn_fm"].users = []
        for nm, dst, dsl, a in (("mq", mw_q, SL["mw_q"], ND),
                                ("mdkv", mw_dkv, SL["mw_dkv"], ND),
                                ("muk", mw_uk, SL["mw_uk"], NL)):
            dsl.prep_write(plan, "sync")
            plan.wait("sync", cast[nm])
            t = plan.ld(lambda E, nm=nm, dst=dst, a=a: E.dma_start(
                out=dst[:].rearrange("p (a b) -> p a b", a=a),
                in_=scr[nm].rearrange("(a p) b -> p a b", p=128)))
            dsl.ready = t
            dsl.users = []
        SL["xn_fm"].use(plan, "vector")
        SL["rstd1_bc"].use(plan, "vector")
        t_h1 = None
        for dc in range(ND):
            t_h1 = plan.tok("vector", lambda E, dc=dc: E.scalar_tensor_tensor(
                out=h1_fm[:, dc * C:(dc + 1) * C], in0=xn_fm[:, dc * C:(dc + 1) * C],
                scalar=c_g1col[:, dc:dc + 1], in1=rstd1_bc[:, 0:C],
                op0=ALU.mult, op1=ALU.mult))
        SL["h1_fm"].ready = t_h1
        SL["h1_fm"].users = []
        SL["xn_fm"].users.append(t_h1)
        SL["rstd1_bc"].users.append(t_h1)

        def proj_fm(dst, dsl, wsb, wsl, nin, nout, src, ssl, scale):
            dsl.prep_write(plan, "scalar")
            wsl.use(plan, "tensor")
            ssl.use(plan, "tensor")
            t_ev = None
            for ic in range(nout):
                for th in range(C // 512):
                    i4, pq = ring4.acquire()
                    t_mm = None
                    for jd in range(nin):
                        fn = lambda E, ic=ic, th=th, jd=jd, pq=pq, wsb=wsb, src=src, nout=nout, nin=nin: E.matmul(
                            pq, wsb[:, jd * (nout * 128) + ic * 128: jd * (nout * 128) + (ic + 1) * 128],
                            src[:, jd * C + th * 512: jd * C + (th + 1) * 512],
                            start=(jd == 0), stop=(jd == nin - 1))
                        if jd == nin - 1:
                            t_mm = plan.tok("tensor", fn)
                        else:
                            plan.on("tensor", fn)
                    wsl.users.append(t_mm)
                    ssl.users.append(t_mm)
                    plan.wait("scalar", t_mm)
                    t_ev = plan.tok("scalar", lambda E, ic=ic, th=th, pq=pq, dst=dst, scale=scale: E.activation(
                        dst[:, ic * C + th * 512: ic * C + (th + 1) * 512],
                        pq, AF.Copy, bias=0.0, scale=scale))
                    ring4.release(i4, t_ev)
            dsl.ready = t_ev
            dsl.users = []

        SL["q_fm"].prep_write(plan, "scalar")
        proj_fm(q_fm, SL["q_fm"], mw_q, SL["mw_q"], ND, ND, h1_fm, SL["h1_fm"], 0.125)
        SL["mw_o"].prep_write(plan, "sync")
        for t in SL["mw_q"].users:
            plan.wait("sync", t)
        plan.wait("sync", cast["mo"])
        t_o = plan.ld(lambda E: E.dma_start(
            out=mw_o[:].rearrange("p (a b) -> p a b", a=ND),
            in_=scr["mo"].rearrange("(a p) b -> p a b", p=128)))
        SL["mw_o"].ready = t_o
        SL["mw_o"].users = []
        SL["c_fm"].prep_write(plan, "scalar")
        for t in SL["xn_fm"].users:
            plan.wait("scalar", t)
        proj_fm(c_fm, SL["c_fm"], mw_dkv, SL["mw_dkv"], ND, NL, h1_fm, SL["h1_fm"], 1.0)
        SL["mw_uv"].prep_write(plan, "sync")
        for t in SL["mw_dkv"].users:
            plan.wait("sync", t)
        plan.wait("sync", cast["muv"])
        t_uv = plan.ld(lambda E: E.dma_start(
            out=mw_uv[:].rearrange("p (a b) -> p a b", a=NL),
            in_=scr["muv"].rearrange("(a p) b -> p a b", p=128)))
        SL["mw_uv"].ready = t_uv
        SL["mw_uv"].users = []
        SL["k_fm"].prep_write(plan, "scalar")
        for t in SL["h1_fm"].users:
            plan.wait("scalar", t)
        proj_fm(k_fm, SL["k_fm"], mw_uk, SL["mw_uk"], NL, ND, c_fm, SL["c_fm"], 1.0)
        SL["v_tm"].prep_write(plan, "scalar")
        SL["mw_uv"].use(plan, "tensor")
        SL["c_fm"].use(plan, "tensor")
        t_ev = None
        for tc in range(NT):
            for oh in range(D // 512):
                i4, pq = ring4.acquire()
                t_mm = None
                for jd in range(NL):
                    fn = lambda E, tc=tc, oh=oh, jd=jd, pq=pq: E.matmul(
                        pq, c_fm[:, jd * C + tc * 128: jd * C + (tc + 1) * 128],
                        mw_uv[:, jd * D + oh * 512: jd * D + (oh + 1) * 512],
                        start=(jd == 0), stop=(jd == NL - 1))
                    if jd == NL - 1:
                        t_mm = plan.tok("tensor", fn)
                    else:
                        plan.on("tensor", fn)
                SL["mw_uv"].users.append(t_mm)
                SL["c_fm"].users.append(t_mm)
                plan.wait("scalar", t_mm)
                t_ev = plan.tok("scalar", lambda E, tc=tc, oh=oh, pq=pq: E.activation(
                    v_tm[:, tc * D + oh * 512: tc * D + (oh + 1) * 512],
                    pq, AF.Copy, bias=0.0, scale=1.0))
                ring4.release(i4, t_ev)
        SL["v_tm"].ready = t_ev
        SL["v_tm"].users = []

    def load_w1(e):
        i1 = (2 * e) % 3
        r1, s1 = moe_w[i1], SL[moe_wsl_names[i1]]
        s1.prep_write(plan, "sync")
        plan.wait("sync", cast[f"w1_{e}"])
        t1 = plan.ld(lambda E, e=e, r1=r1: E.dma_start(
            out=r1[:].rearrange("p (a b) -> p a b", a=ND),
            in_=scr["w1"][e].rearrange("(a p) b -> p a b", p=128)))
        s1.ready = t1
        return r1, s1

    def load_w2(e):
        i2 = (2 * e + 1) % 3
        r2, s2 = moe_w[i2], SL[moe_wsl_names[i2]]
        s2.prep_write(plan, "sync")
        plan.wait("sync", cast[f"w2_{e}"])
        t2 = plan.ld(lambda E, e=e, r2=r2: E.dma_start(
            out=r2[:].rearrange("p (a b) -> p a b", a=NH),
            in_=scr["w2"][e].rearrange("(a p) b -> p a b", p=128)))
        s2.ready = t2
        return r2, s2

    def phase_attn(s, qh):
        SL["q_fm"].use(plan, "tensor")
        SL["k_fm"].use(plan, "tensor")
        SL["v_tm"].use(plan, "tensor")
        plan.wait("tensor", TOK["mask"])
        plan.wait("scalar", TOK["shift"])
        plan.wait("tensor", TOK["onebf"])
        prev_rcp = TOK.setdefault("prev_rcp", [None, None, None, None])
        attn_tok = None

        def do_scores(it, h, tp):
            dt = h // 2
            pb = 64 * (h % 2)
            qa = 2 * tp
            qb = qa + 1
            nk = qb + 1
            Pt, Ptsl = P_[it % 2], SL[f"P{it % 2}"]
            Ptsl.prep_write(plan, "scalar")
            exp_toks = []
            for g in range((nk + 1) // 2):
                m = min(2, nk - g * 2)
                si, pS = ring4.acquire()
                t_mm = None
                for i in range(m):
                    kc = g * 2 + i
                    off = i * 256
                    if kc < qb:
                        fn = lambda E, kc=kc, off=off, qa=qa, pS=pS, dt=dt, pb=pb: E.matmul(
                            pS[:, off:off + 256],
                            k_fm[pb:pb + 64, dt * C + kc * 128: dt * C + (kc + 1) * 128],
                            q_fm[pb:pb + 64, dt * C + qa * 128: dt * C + (qa + 2) * 128],
                            start=True, stop=(kc != qa))
                        if kc == qa:
                            plan.on("tensor", fn)
                            fn = lambda E, off=off, pS=pS: E.matmul(
                                pS[:, off:off + 128], c_idbf[:], c_maskbfT[:],
                                start=False, stop=True)
                    else:
                        fn0 = lambda E, kc=kc, off=off, qb=qb, pS=pS, dt=dt, pb=pb: E.matmul(
                            pS[:, off + 128:off + 256],
                            k_fm[pb:pb + 64, dt * C + kc * 128: dt * C + (kc + 1) * 128],
                            q_fm[pb:pb + 64, dt * C + qb * 128: dt * C + (qb + 1) * 128],
                            start=True, stop=False)
                        plan.on("tensor", fn0)
                        fn = lambda E, off=off, pS=pS: E.matmul(
                            pS[:, off + 128:off + 256], c_idbf[:], c_maskbfT[:],
                            start=False, stop=True)
                    if i == m - 1:
                        t_mm = plan.tok("tensor", fn)
                    else:
                        plan.on("tensor", fn)
                plan.wait("scalar", t_mm)
                lastpart = (g * 2 + m - 1 == qb)
                if not lastpart:
                    spans = [(0, m * 256)]
                elif m == 1:
                    spans = [(128, 256)]
                else:
                    spans = [(0, 256), (384, 512)]
                t_exp = None
                for (a, bnd) in spans:
                    t_exp = plan.tok("scalar", lambda E, g=g, Pt=Pt, pS=pS, a=a, bnd=bnd: E.activation(
                        Pt[:, g * 512 + a: g * 512 + bnd], pS[:, a:bnd], AF.Exp,
                        bias=c_shift[:, 0:1], scale=1.0))
                exp_toks.append(t_exp)
                ring4.release(si, t_exp)
            Ptsl.ready = exp_toks[-1]
            return (it, h, tp, exp_toks)

        def do_av(state):
            nonlocal attn_tok
            it, h, tp, exp_toks = state
            qa = 2 * tp
            qb = qa + 1
            Pt, Ptsl = P_[it % 2], SL[f"P{it % 2}"]
            for half, qt in ((0, qa), (1, qb)):
                par = (2 * it + half) % 4
                nk = qt + 1
                ci, pc = ringC.acquire()
                t_grp = None
                for kc in range(nk):
                    plan.wait("tensor", exp_toks[kc // 2])
                    col = kc * 256 + half * 128
                    plan.on("tensor", lambda E, kc=kc, h=h, Pt=Pt, pc=pc, col=col: E.matmul(
                        pc[:, 0:64], Pt[:, col:col + 128],
                        v_tm[:, kc * D + h * 64: kc * D + h * 64 + 64],
                        start=(kc == 0), stop=False))
                    fn_s = lambda E, kc=kc, Pt=Pt, pc=pc, nk=nk, col=col: E.matmul(
                        pc[:, 64:65], Pt[:, col:col + 128],
                        c_onebf[:, 0:1], start=False, stop=(kc == nk - 1))
                    if kc == nk - 1:
                        t_grp = plan.tok("tensor", fn_s)
                    else:
                        plan.on("tensor", fn_s)
                Ptsl.users.append(t_grp)
                plan.wait("vector", t_grp)
                plan.wait("vector", prev_rcp[par])
                t_rcp = plan.tok("vector", lambda E, par=par, pc=pc: E.reciprocal(
                    w_rcp[:, par:par + 1], pc[:, 64:65]))
                plan.wait("vector", t_rcp)
                t_ae = plan.tok("vector", lambda E, qt=qt, qh=qh, h=h, par=par, pc=pc: E.tensor_scalar(
                    out=attn_tm[:, (qt - qh * HT) * D + h * 64:(qt - qh * HT) * D + h * 64 + 64],
                    in0=pc[:, 0:64], scalar1=w_rcp[:, par:par + 1], scalar2=None,
                    op0=ALU.mult))
                ringC.release(ci, t_ae)
                prev_rcp[par] = t_ae
                attn_tok = t_ae

        items = [(h, tp) for h in range(HEADS) for tp in range(qh * HT // 2, qh * HT // 2 + HT // 2)]
        pending = []
        for it, (h, tp) in enumerate(items):
            pending.append(do_scores(it, h, tp))
            if len(pending) > 1:
                do_av(pending.pop(0))
        for st_ in pending:
            do_av(st_)
        SL["attn_tm"].ready = attn_tok
        SL["attn_tm"].users = []
        TOK["attn_done"] = attn_tok

    def phase_wo(s, qh):
        SL["attn_tm"].use(plan, "tensor")
        afm_tok = None
        for tt in range(HT):
            blocks = [attn_tm[:, tt * D + dc * 128: tt * D + (dc + 1) * 128] for dc in range(ND)]
            dst = q_fm[:].rearrange("p (d c) -> p d c", d=ND)[:, :, qh * CH + tt * 128: qh * CH + (tt + 1) * 128]
            afm_tok = transpose_pack(blocks, dst)
        SL["attn_tm"].users.append(afm_tok)
        SL["mw_o"].use(plan, "tensor")
        plan.wait("tensor", afm_tok)
        prev_x2 = None
        h2h_tok = None
        for tt in range(HT):
            T = qh * HT + tt
            b = tt % 2
            wsl = SL[f"stwo{b}"]
            wsl.prep_write(plan, "sync")
            plan.wait("sync", TOK[f"xn_st_{T}"])
            t_ld = plan.ld(lambda E, T=T, b=b: E.dma_start(
                out=st_wo[b][:], in_=d_xn[T * 128:(T + 1) * 128, :]))
            wsl.ready = t_ld
            x2_toks = []
            for oh in range(2):
                i4, pq = ring4.acquire()
                t_mm = None
                for dc in range(ND):
                    fn = lambda E, tt=tt, qh=qh, dc=dc, oh=oh, pq=pq: E.matmul(
                        pq, q_fm[:, dc * C + qh * CH + tt * 128: dc * C + qh * CH + (tt + 1) * 128],
                        mw_o[:, dc * D + oh * 512: dc * D + (oh + 1) * 512],
                        start=(dc == 0), stop=(dc == ND - 1))
                    if dc == ND - 1:
                        t_mm = plan.tok("tensor", fn)
                    else:
                        plan.on("tensor", fn)
                SL["mw_o"].users.append(t_mm)
                plan.wait("vector", t_mm)
                wsl.use(plan, "vector")
                plan.wait("vector", prev_x2)
                if TOK.get("x2_stprev") is not None:
                    plan.wait("vector", TOK["x2_stprev"])
                t_x2 = plan.tok("vector", lambda E, oh=oh, b=b, pq=pq: E.tensor_add(
                    st_x2[:, oh * 512:(oh + 1) * 512], pq, st_wo[b][:, oh * 512:(oh + 1) * 512]))
                ring4.release(i4, t_x2)
                x2_toks.append(t_x2)
                wsl.users.append(t_x2)
            for t in x2_toks:
                plan.wait("scalar", t)
            t_stx2 = plan.st(lambda E, T=T: E.dma_start(
                out=d_x2[T * 128:(T + 1) * 128, :], in_=st_x2[:]))
            TOK[f"x2_st_{T}"] = t_stx2
            TOK["x2_stprev"] = t_stx2
            for t in x2_toks:
                plan.wait("vector", t)
            cb = 38 + 4 * b
            t_ssq = plan.tok("scalar", lambda E, cb=cb: E.activation(
                scr_bf[:, 0:D], st_x2[:], AF.Square, bias=0.0, scale=1.0,
                accum_out=w_cols[:, cb:cb + 1]))
            plan.wait("scalar", t_ssq)
            t_sq = plan.tok("scalar", lambda E, cb=cb: E.activation(
                w_cols[:, cb + 1:cb + 2], w_cols[:, cb:cb + 1], AF.Sqrt,
                bias=c_epsrm[:, 0:1], scale=1.0 / D))
            plan.wait("vector", t_sq)
            t_r2 = plan.tok("vector", lambda E, cb=cb: E.reciprocal(
                w_cols[:, cb + 2:cb + 3], w_cols[:, cb + 1:cb + 2]))
            hsl = SL[f"sth2{b}"]
            hsl.prep_write(plan, "vector")
            plan.wait("vector", t_r2)
            plan.wait("vector", TOK["g2bc"])
            t_h2 = plan.tok("vector", lambda E, b=b, cb=cb: E.scalar_tensor_tensor(
                out=st_h2[b][:], in0=st_x2[:], scalar=w_cols[:, cb + 2:cb + 3],
                in1=c_g2bc[:], op0=ALU.mult, op1=ALU.mult))
            prev_x2 = t_h2
            hsl.ready = t_h2
            plan.wait("tensor", t_h2)
            blocks = [st_h2[b][:, dc * 128:(dc + 1) * 128] for dc in range(ND)]
            dst = h2_fm_h[:].rearrange("p (d c) -> p d c", d=ND)[:, :, tt * 128:(tt + 1) * 128]
            h2h_tok = transpose_pack(blocks, dst)
            hsl.users.append(h2h_tok)
            plan.wait("tensor", h2h_tok)
            plan.wait("tensor", TOK["gw"])
            di, pd = ringD.acquire()
            t_gl = None
            for dc in range(ND):
                fn = lambda E, tt=tt, dc=dc, pd=pd: E.matmul(
                    pd[:, 0:NE],
                    h2_fm_h[:, dc * CH + tt * 128: dc * CH + (tt + 1) * 128],
                    c_gwsb[:, dc * NE:(dc + 1) * NE],
                    start=(dc == 0), stop=(dc == ND - 1))
                if dc == ND - 1:
                    t_gl = plan.tok("tensor", fn)
                else:
                    plan.on("tensor", fn)
            plan.wait("vector", t_gl)
            t_lg = plan.tok("vector", lambda E, T=T, pd=pd: E.tensor_copy(
                w_logit[:, T * NE:(T + 1) * NE], pd[:, 0:NE]))
            ringD.release(di, t_lg)
            plan.wait("vector", t_lg)
            t_m1 = plan.tok("vector", lambda E, T=T: E.reduce_max(
                w_m1[:, T:T + 1], w_logit[:, T * NE:(T + 1) * NE], axis=AX.X, negate=True))
            plan.wait("vector", t_m1)
            t_mk1 = plan.tok("vector", lambda E, T=T: E.tensor_scalar(
                out=w_gsc[:, 0:NE], in0=w_logit[:, T * NE:(T + 1) * NE],
                scalar1=w_m1[:, T:T + 1], scalar2=0.0, op0=ALU.add, op1=ALU.is_equal))
            plan.wait("vector", t_mk1)
            t_msk = plan.tok("vector", lambda E, T=T: E.scalar_tensor_tensor(
                out=w_gsc[:, NE:2 * NE], in0=w_gsc[:, 0:NE], scalar=-1e30,
                in1=w_logit[:, T * NE:(T + 1) * NE], op0=ALU.mult, op1=ALU.add))
            plan.wait("vector", t_msk)
            t_m2 = plan.tok("vector", lambda E, T=T: E.reduce_max(
                w_m2[:, T:T + 1], w_gsc[:, NE:2 * NE], axis=AX.X, negate=True))
            TOK[f"gate_{T}"] = t_m2
        SL["h2_fm_h"].ready = h2h_tok
        plan.wait("scalar", h2h_tok)
        t_h2st = plan.st(lambda E, qh=qh: E.dma_start(
            out=d_h2.rearrange("e p c -> p e c")[:, :, qh * CH:(qh + 1) * CH],
            in_=h2_fm_h[:].rearrange("p (e c) -> p e c", e=ND)))
        SL["h2_fm_h"].users.append(t_h2st)
        TOK[f"h2_st_{qh}"] = t_h2st
        for tt in range(HT):
            T = qh * HT + tt
            plan.wait("scalar", TOK[f"gate_{T}"])
            TOK[f"sig_{T}"] = plan.tok("scalar", lambda E, T=T: E.activation(
                w_gt[:, T:T + 1], w_m1[:, T:T + 1], AF.Sigmoid,
                bias=w_m2[:, T:T + 1], scale=-1.0))
        comb_tok = SL["combine"].ready
        for tt in range(HT):
            T = qh * HT + tt
            plan.wait("vector", TOK[f"sig_{T}"])
            plan.wait("vector", comb_tok)
            t_c1 = plan.tok("vector", lambda E, T=T: E.tensor_scalar(
                out=w_gsc[:, 0:NE], in0=w_logit[:, T * NE:(T + 1) * NE],
                scalar1=w_m1[:, T:T + 1], scalar2=0.0, op0=ALU.add, op1=ALU.is_equal))
            t_c2 = plan.tok("vector", lambda E, T=T: E.tensor_scalar(
                out=w_gsc[:, NE:2 * NE], in0=w_logit[:, T * NE:(T + 1) * NE],
                scalar1=w_m2[:, T:T + 1], scalar2=0.0, op0=ALU.add, op1=ALU.is_equal))
            t_gb = plan.tok("vector", lambda E, T=T: E.tensor_scalar(
                out=w_cols[:, 46:47], in0=w_gt[:, T:T + 1], scalar1=-1.0, scalar2=1.0,
                op0=ALU.mult, op1=ALU.add))
            plan.wait("vector", t_c1)
            plan.wait("vector", t_gb)
            t_t1 = plan.tok("vector", lambda E, T=T: E.tensor_scalar(
                out=w_gsc[:, 0:NE], in0=w_gsc[:, 0:NE], scalar1=w_gt[:, T:T + 1],
                scalar2=None, op0=ALU.mult))
            plan.wait("vector", t_c2)
            plan.wait("vector", t_t1)
            comb_tok = plan.tok("vector", lambda E, T=T: E.scalar_tensor_tensor(
                out=w_comb[:, T * NE:(T + 1) * NE], in0=w_gsc[:, NE:2 * NE],
                scalar=w_cols[:, 46:47], in1=w_gsc[:, 0:NE],
                op0=ALU.mult, op1=ALU.add))
        SL["combine"].ready = comb_tok
        if qh == 1:
            TOK["moe_pf1"] = load_w1(0)
            TOK["moe_pf2"] = load_w2(0)

    def phase_moe(s):
        NCH = C // 512
        for half, acc, asl in ((0, acc0, SL["acc0"]), (1, acc1, SL["acc1"])):
            asl.prep_write(plan, "sync")
            for tt in range(HT):
                plan.wait("sync", TOK[f"x2_st_{half * HT + tt}"])
            t = plan.ld(lambda E, half=half, acc=acc: E.dma_start(
                out=acc[:].rearrange("p (t d) -> p t d", t=HT),
                in_=d_x2.rearrange("(t p) d -> p t d", p=128)[:, half * HT:(half + 1) * HT, :]))
            asl.ready = t
        acc_tok = [None] * NT

        cur1 = TOK["moe_pf1"]
        cur2 = TOK["moe_pf2"]
        hid, h2c = moe_hid, moe_h2c
        hsl, hcsl = SL["hid0"], SL["moe_h0"]
        pending_w2 = None
        for e in range(NE):
            r1, s1 = cur1
            if pending_w2 is not None:
                cur2 = None
            r2, s2 = cur2 if cur2 is not None else (None, None)
            if e + 1 < NE:
                nxt1 = load_w1(e + 1)
            for ch in range(NCH):
                hcsl.prep_write(plan, "sync")
                plan.wait("sync", TOK["h2_st_0"])
                plan.wait("sync", TOK["h2_st_1"])
                t_h = plan.ld(lambda E, ch=ch, h2c=h2c: E.dma_start(
                    out=h2c[:].rearrange("p (a b) -> p a b", a=ND),
                    in_=d_h2.rearrange("e p c -> p e c")[:, :, ch * 512:(ch + 1) * 512]))
                hcsl.ready = t_h
                if ch == 0 and pending_w2 is not None:
                    cur2 = load_w2(pending_w2)
                    r2, s2 = cur2
                    pending_w2 = None
                hsl.prep_write(plan, "scalar")
                s1.use(plan, "tensor")
                hcsl.use(plan, "tensor")
                t_sil = None
                for ht in range(NH):
                    i4, pq = ring4.acquire()
                    t_mm = None
                    for dc in range(ND):
                        fn = lambda E, ht=ht, dc=dc, r1=r1, h2c=h2c, pq=pq: E.matmul(
                            pq, r1[:, dc * EXPD + ht * 128: dc * EXPD + (ht + 1) * 128],
                            h2c[:, dc * 512:(dc + 1) * 512],
                            start=(dc == 0), stop=(dc == ND - 1))
                        if dc == ND - 1:
                            t_mm = plan.tok("tensor", fn)
                        else:
                            plan.on("tensor", fn)
                    s1.users.append(t_mm)
                    hcsl.users.append(t_mm)
                    plan.wait("scalar", t_mm)
                    t_sil = plan.tok("scalar", lambda E, ht=ht, hid=hid, pq=pq: E.activation(
                        hid[:, ht * 512:(ht + 1) * 512], pq, AF.Silu,
                        bias=0.0, scale=1.0))
                    ring4.release(i4, t_sil)
                hsl.ready = t_sil
                s2.use(plan, "tensor")
                hsl.use(plan, "tensor")
                for ti in range(4):
                    T = ch * 4 + ti
                    half, tt = T // HT, T % HT
                    acc = acc0 if half == 0 else acc1
                    asl = SL["acc0"] if half == 0 else SL["acc1"]
                    asl.use(plan, "vector")
                    for oh in range(2):
                        ci, pc = ringC.acquire()
                        t_mm = None
                        for ht in range(NH):
                            fn = lambda E, ht=ht, ti=ti, oh=oh, r2=r2, hid=hid, pc=pc: E.matmul(
                                pc, hid[:, ht * 512 + ti * 128: ht * 512 + (ti + 1) * 128],
                                r2[:, ht * D + oh * 512: ht * D + (oh + 1) * 512],
                                start=(ht == 0), stop=(ht == NH - 1))
                            if ht == NH - 1:
                                t_mm = plan.tok("tensor", fn)
                            else:
                                plan.on("tensor", fn)
                        hsl.users.append(t_mm)
                        s2.users.append(t_mm)
                        plan.wait("vector", t_mm)
                        plan.wait("vector", acc_tok[T])
                        plan.wait("vector", SL["combine"].ready)
                        t_ac = plan.tok("vector", lambda E, e=e, T=T, tt=tt, oh=oh, acc=acc, pc=pc: E.scalar_tensor_tensor(
                            out=acc[:, tt * D + oh * 512: tt * D + (oh + 1) * 512],
                            in0=pc, scalar=w_comb[:, T * NE + e:T * NE + e + 1],
                            in1=acc[:, tt * D + oh * 512: tt * D + (oh + 1) * 512],
                            op0=ALU.mult, op1=ALU.add))
                        ringC.release(ci, t_ac)
                        acc_tok[T] = t_ac
                if e == NE - 1:
                    if ch == 0:
                        for t_ in list(SL["moe_w2"].users):
                            plan.wait("sync", t_)
                            plan.wait("scalar", t_)
                    for ti in range(4):
                        TOK["fin_last"] = fin_tile(s, ch * 4 + ti, acc_tok)
            if e + 1 < NE:
                pending_w2 = e + 1
                cur1 = nxt1
        TOK["acc_tok"] = acc_tok
        return TOK["fin_last"]

    def fin_tile(s, T, acc_tok):
        b = T % 2
        half, tt = T // HT, T % HT
        acc = acc0 if half == 0 else acc1
        aslice = acc[:, tt * D:(tt + 1) * D]
        xsl, ssl, gsl = SL[f"finx{b}"], SL[f"fins{b}"], SL[f"fing{b}"]
        xsl.prep_write(plan, "sync")
        t_lx = plan.ld(lambda E, T=T, b=b: E.dma_start(
            out=fin_x[b][:], in_=x_in[s][T * 128:(T + 1) * 128, :]))
        xsl.ready = t_lx
        ssl.prep_write(plan, "sync")
        plan.wait("sync", TOK[f"s_st_{T}"])
        t_ls = plan.ld(lambda E, T=T, b=b: E.dma_start(
            out=fin_s[b][:], in_=d_s[T * 128:(T + 1) * 128, :]))
        ssl.ready = t_ls
        gsl.prep_write(plan, "scalar")
        plan.wait("scalar", acc_tok[T])
        t_g = plan.tok("scalar", lambda E, b=b, aslice=aslice: E.activation(
            fin_g[b][:], aslice, AF.Silu, bias=0.0, scale=1.0))
        gsl.ready = t_g
        plan.wait("vector", t_g)
        ssl.use(plan, "vector")
        t_m = plan.tok("vector", lambda E, b=b: E.tensor_mul(
            fin_g[b][:], fin_s[b][:], fin_g[b][:]))
        ssl.users.append(t_m)
        plan.wait("vector", t_m)
        xsl.use(plan, "vector")
        t_o = plan.tok("vector", lambda E, b=b, aslice=aslice: E.tensor_add(
            aslice, fin_g[b][:], fin_x[b][:]))
        xsl.users.append(t_o)
        gsl.users.append(t_o)
        plan.wait("scalar", t_o)
        last_st = plan.st(lambda E, T=T, aslice=aslice: E.dma_start(
            out=out_t[s][T * 128:(T + 1) * 128, :], in_=aslice))
        return last_st

    PHASES = [(phase_convert, False), (phase_consts, False)]
    def mk(fn, *a):
        return lambda: fn(*a)
    for s_ in range(NSEQ):
        PHASES += [(mk(phase_s1, s_), True), (mk(phase_qkv, s_), True),
                   (mk(phase_A, s_), True), (mk(phase_wv2, s_), True),
                   (mk(phase_mla, s_), True),
                   (mk(phase_attn, s_, 0), True), (mk(phase_wo, s_, 0), True),
                   (mk(phase_attn, s_, 1), True), (mk(phase_wo, s_, 1), True),
                   (mk(phase_moe, s_), True)]
    final_tok = None
    for pi, (ph, need_bar) in enumerate(PHASES):
        if pi >= max_phases:
            break
        if need_bar:
            plan.barrier()
        r = ph()
        if r is not None:
            final_tok = r
    plan.wait("scalar", final_tok)

    with contextlib.ExitStack() as stack:
        for sname in SEMS:
            plan.semh[sname] = stack.enter_context(nc.semaphore("sem_" + sname))
        with nc.Block() as block:
            @block.sync
            def _(e):
                plan.emit(e, "sync")

            @block.gpsimd
            def _(e):
                plan.emit(e, "gpsimd")

            @block.vector
            def _(e):
                plan.emit(e, "vector")

            @block.scalar
            def _(e):
                plan.emit(e, "scalar")

            @block.tensor
            def _(e):
                plan.emit(e, "tensor")
    return nc


def kernel(**inputs):
    x = np.ascontiguousarray(np.asarray(inputs["L0_attn_x"], np.float32))
    B = x.shape[0]
    ncore = 8
    per = B // ncore
    nc = build_nc(C=x.shape[1], NSEQ=per)
    common = {
        "wqa": np.ascontiguousarray(inputs["L0_attn_Wq"], np.float32),
        "wka": np.ascontiguousarray(inputs["L0_attn_Wk"], np.float32),
        "wva": np.ascontiguousarray(inputs["L0_attn_Wv"], np.float32),
        "bq": np.ascontiguousarray(inputs["L0_attn_bq"], np.float32),
        "bk": np.ascontiguousarray(inputs["L0_attn_bk"], np.float32),
        "bv": np.ascontiguousarray(inputs["L0_attn_bv"], np.float32),
        "gam": np.ascontiguousarray(inputs["L0_attn_gamma"], np.float32),
        "g1": np.ascontiguousarray(inputs["L0_ffn_g1"], np.float32),
        "mq": np.ascontiguousarray(inputs["L0_ffn_Wq"], np.float32),
        "mdkv": np.ascontiguousarray(inputs["L0_ffn_Wdkv"], np.float32),
        "muk": np.ascontiguousarray(inputs["L0_ffn_Wuk"], np.float32),
        "muv": np.ascontiguousarray(inputs["L0_ffn_Wuv"], np.float32),
        "mo": np.ascontiguousarray(inputs["L0_ffn_Wo"], np.float32),
        "g2": np.ascontiguousarray(inputs["L0_ffn_g2"], np.float32),
        "gw": np.ascontiguousarray(inputs["L0_ffn_gate_w"], np.float32),
        "w1": np.ascontiguousarray(inputs["L0_ffn_w1"], np.float32),
        "w2": np.ascontiguousarray(inputs["L0_ffn_w2"], np.float32),
    }
    in_maps = [dict(common, x=np.ascontiguousarray(x[c * per:(c + 1) * per]))
               for c in range(ncore)]
    res = run_bass_kernel_spmd(nc, in_maps, core_ids=list(range(ncore)))
    return np.concatenate([r["out"] for r in res.results], axis=0)


# version 18 (final)
# speedup vs baseline: 1.1823x; 1.0191x over previous
import contextlib
import os
import numpy as np
import ml_dtypes
import concourse.bass as bass
import concourse.mybir as mybir
from concourse.bass_utils import run_bass_kernel_spmd

F32 = mybir.dt.float32
BF16 = mybir.dt.bfloat16
AF = mybir.ActivationFunctionType
ALU = mybir.AluOpType
AX = mybir.AxisListType

D = 1024
ND = D // 128
HEADS = 16
HD = 64
LAT = 512
NL = LAT // 128
NE = 8
EXPD = 2048
NH = EXPD // 128
LN_EPS = 1e-5
EPS_A = 1e-6
RMS_EPS = 1e-6
EXP_SHIFT = -4.0

ENGS = ("sync", "gpsimd", "vector", "scalar", "tensor")
OWN = {"vector": "dve", "scalar": "act", "tensor": "pe", "gpsimd": "pool", "sync": "syn"}
NQS = 8
QSEMS = {q: [f"{q}{i}" for i in range(NQS)] for q in ("ld", "st", "gd")}
SEMS = tuple(["pe", "dve", "act", "pool", "syn"] + [s for q in QSEMS.values() for s in q])


class Plan:
    def __init__(self, strict=False):
        self.strict = strict
        self.ops = {e: [] for e in ENGS}
        self.cnt = {s: 0 for s in SEMS}
        self.qidx = {"ld": 0, "st": 0, "gd": 0}
        self.hwm = {e: {} for e in ENGS}
        self.semh = {}

    def on(self, eng, fn):
        self.ops[eng].append(("op", fn))

    def tok(self, eng, fn, n=1):
        sem = OWN[eng]
        if (self.strict or eng == "gpsimd") and self.cnt[sem] > 0:
            self.wait(eng, (sem, self.cnt[sem]))
        self.cnt[sem] += n
        self.ops[eng].append(("tok", fn, sem, n))
        return (sem, self.cnt[sem])

    def wait(self, eng, token):
        if token is None:
            return
        if isinstance(token, list):
            for t in token:
                self.wait(eng, t)
            return
        sem, val = token
        if self.hwm[eng].get(sem, 0) >= val:
            return
        self.hwm[eng][sem] = val
        self.ops[eng].append(("wait", sem, val))

    def _dma(self, eng, q, fn):
        i = self.qidx[q] % NQS
        self.qidx[q] += 1
        sem = QSEMS[q][i]
        if self.cnt[sem] > 0:
            self.wait(eng, (sem, self.cnt[sem]))
        self.cnt[sem] += 16
        self.ops[eng].append(("tok", fn, sem, 16))
        return (sem, self.cnt[sem])

    def ld(self, fn):
        return self._dma("sync", "ld", fn)

    def st(self, fn):
        return self._dma("scalar", "st", fn)

    def gd(self, fn):
        return self._dma("gpsimd", "gd", fn)

    def barrier(self):
        import os
        mode = int(os.environ.get("BARRIER", "2"))
        if mode == 0:
            return
        toks = {e: self.tok(e, lambda E, e=e: E.nop() if e == "sync" else E.drain())
                for e in ENGS}
        dma_toks = [(s, self.cnt[s]) for q in ("ld", "st") for s in QSEMS[q]
                    if self.cnt[s] > 0] if mode >= 2 else []
        for e in ENGS:
            for o in ENGS:
                if o != e:
                    self.wait(e, toks[o])
            for t in dma_toks:
                self.wait(e, t)

    def emit(self, eng_obj, name):
        for item in self.ops[name]:
            kind = item[0]
            if kind == "op":
                item[1](eng_obj)
            elif kind == "tok":
                _, fn, sem, n = item
                fn(eng_obj).then_inc(self.semh[sem], n)
            else:
                _, sem, val = item
                eng_obj.wait_ge(self.semh[sem], val)


class Slot:
    def __init__(self):
        self.ready = None
        self.users = []

    def prep_write(self, plan, eng):
        for t in self.users:
            plan.wait(eng, t)
        self.users = []

    def use(self, plan, eng):
        plan.wait(eng, self.ready)


class PRing:
    def __init__(self, plan, slots):
        self.plan = plan
        self.slots = slots
        self.last = [None] * len(slots)
        self.i = 0

    def acquire(self):
        i = self.i % len(self.slots)
        self.i += 1
        self.plan.wait("tensor", self.last[i])
        return i, self.slots[i]

    def release(self, i, evac_tok):
        self.last[i] = evac_tok


def build_nc(C=2048, NSEQ=4, strict=False, max_phases=99):
    assert C % 512 == 0 and (C // 128) % 2 == 0
    NT = C // 128
    HT = NT // 2
    CH = C // 2
    nc = bass.Bass("TRN2", target_bir_lowering=False, debug=False,
                   detect_race_conditions=strict)
    plan = Plan(strict=strict)
    TOK = {}

    x_in = nc.dram_tensor("x", [NSEQ, C, D], F32, kind="ExternalInput").ap()
    out_t = nc.dram_tensor("out", [NSEQ, C, D], F32, kind="ExternalOutput").ap()
    ins = {}
    for nm, shp in [("wqa", [D, D]), ("wka", [D, D]), ("wva", [D, D]),
                    ("bq", [D]), ("bk", [D]), ("bv", [D]), ("gam", [1]),
                    ("g1", [D]), ("mq", [D, D]), ("mdkv", [D, LAT]),
                    ("muk", [LAT, D]), ("muv", [LAT, D]), ("mo", [D, D]),
                    ("g2", [D]), ("gw", [D, NE]), ("w1", [NE, D, EXPD]),
                    ("w2", [NE, EXPD, D])]:
        ins[nm] = nc.dram_tensor(nm, shp, F32, kind="ExternalInput").ap()
    scr = {}
    for nm, shp in [("wqa", [D, D]), ("wka", [D, D]), ("wva", [D, D]),
                    ("mq", [D, D]), ("mdkv", [D, LAT]), ("muk", [LAT, D]),
                    ("muv", [LAT, D]), ("mo", [D, D]), ("gw", [D, NE]),
                    ("w1", [NE, D, EXPD]), ("w2", [NE, EXPD, D])]:
        scr[nm] = nc.dram_tensor("s_" + nm, shp, BF16).ap()
    d_xn = nc.dram_tensor("d_xn", [C, D], F32).ap()
    d_xnfm = nc.dram_tensor("d_xnfm", [ND, 128, C], BF16).ap()
    d_s = nc.dram_tensor("d_s", [C, D], F32).ap()
    d_x2 = nc.dram_tensor("d_x2", [C, D], F32).ap()
    d_h2 = nc.dram_tensor("d_h2", [ND, 128, C], BF16).ap()

    REG = {}
    for r, nb in [("SA", 32768), ("SB", 32768), ("SC", 32768), ("SD", 32768),
                  ("SE", 32768), ("SF", 16384), ("SG", 8192)]:
        h = nc.alloc_sbuf_tensor("R_" + r, [128, nb], mybir.dt.uint8)
        REG[r] = nc.lookup_mloc(h).addr

    def at(reg, off, name, shape, dtype):
        return nc.alloc_sbuf_tensor_at(name, shape, dtype, offset=REG[reg] + off)

    def alloc(name, shape, dtype):
        return nc.alloc_sbuf_tensor(name, shape, dtype)

    c_idbf = alloc("c_idbf", [128, 128], BF16)
    c_idf = alloc("c_idf", [128, 128], F32)
    c_iotaP = alloc("c_iotaP", [128, 1], F32)
    c_mask = alloc("c_mask", [128, 128], F32)
    c_maskbf = alloc("c_maskbf", [128, 128], BF16)
    c_maskbfT = alloc("c_maskbfT", [128, 128], BF16)
    c_onebf = alloc("c_onebf", [128, 1], BF16)
    c_onerf = alloc("c_onerf", [1, 128], F32)
    c_g2bc = alloc("c_g2bc", [128, D], F32)
    c_g1col = alloc("c_g1col", [128, ND], F32)
    c_bcol = alloc("c_bcol", [128, 3 * ND], F32)
    c_gam = alloc("c_gam", [1, 1], F32)
    c_gamcol = alloc("c_gamcol", [128, 1], F32)
    c_gwsb = alloc("c_gwsb", [128, ND * NE], BF16)
    c_g2row = alloc("c_g2row", [1, D], F32)
    c_rowtmp = alloc("c_rowtmp", [1, D], F32)
    c_epsln = alloc("c_epsln", [128, 1], F32)
    c_epsrm = alloc("c_epsrm", [128, 1], F32)
    c_shift = alloc("c_shift", [128, 1], F32)
    w_cols = alloc("w_cols", [128, 64], F32)
    w_invq = alloc("w_invq", [128, ND], F32)
    w_invk = alloc("w_invk", [128, ND], F32)
    w_qsum = alloc("w_qsum", [128, ND], F32)
    w_rsc = alloc("w_rsc", [128, ND], F32)
    w_tgc = alloc("w_tgc", [128, ND], F32)
    w_krow = alloc("w_krow", [128, NT], F32)
    w_krbf = alloc("w_krbf", [128, NT], BF16)
    w_comb = alloc("w_comb", [128, NT * NE], F32)
    w_rst1 = alloc("w_rst1", [128, NT], F32)
    w_sums = alloc("w_sums", [128, 16], F32)
    w_rcp = alloc("w_rcp", [128, 4], F32)
    w_m1 = alloc("w_m1", [128, NT], F32)
    w_m2 = alloc("w_m2", [128, NT], F32)
    w_gt = alloc("w_gt", [128, NT], F32)
    w_logit = alloc("w_logit", [128, NT * NE], F32)
    w_gsc = alloc("w_gsc", [128, 2 * NE], F32)

    xn_fm = at("SA", 0, "xn_fm", [128, ND * C], BF16)
    A_nm = at("SA", 0, "A_nm", [128, ND * D], BF16)
    A_jm = at("SA", 2 * ND * D, "A_jm", [128, ND * D], BF16)
    c_fm = at("SA", 0, "c_fm", [128, NL * C], BF16)
    P_ = [at("SA", 0, "P0", [128, 2 * C], BF16), at("SA", 4 * C, "P1", [128, 2 * C], BF16)]
    h2_fm_h = at("SA", 0, "h2_fm_h", [128, ND * CH], BF16)
    attn_tm = at("SA", 16384, "attn_tm", [128, HT * D], BF16)
    acc0 = at("SA", 0, "acc0", [128, HT * D], F32)

    st_x = [at("SB", 0, "st_x0", [128, D], F32), at("SB", 4096, "st_x1", [128, D], F32)]
    st_xn = [at("SB", 8192, "st_xn0", [128, D], F32), at("SB", 12288, "st_xn1", [128, D], F32)]
    st_xb = [at("SB", 16384, "st_xb0", [128, D], BF16), at("SB", 18432, "st_xb1", [128, D], BF16)]
    Q_fm = at("SB", 0, "Q_fm", [128, ND * C], BF16)
    h1_fm = at("SB", 0, "h1_fm", [128, ND * C], BF16)
    k_fm = at("SB", 0, "k_fm", [128, ND * C], BF16)
    moe_w0 = at("SB", 0, "moe_w0", [128, 16384], BF16)

    K_fm = at("SC", 0, "K_fm", [128, ND * C], BF16)
    q_fm = at("SC", 0, "q_fm", [128, ND * C], BF16)
    moe_w1 = at("SC", 0, "moe_w1", [128, 16384], BF16)

    V_fm = at("SD", 0, "V_fm", [128, ND * C], BF16)
    v_tm = at("SD", 0, "v_tm", [128, NT * D], BF16)
    moe_w2 = at("SD", 0, "moe_w2", [128, 16384], BF16)
    moe_w = [moe_w0, moe_w2, moe_w1]
    moe_wsl_names = ["moe_w0", "moe_w2", "moe_w1"]

    WT = [at("SE", 0, "WT0", [128, ND * D], BF16),
          at("SE", 2 * ND * D, "WT1", [128, ND * D], BF16)]
    Kn_half = at("SE", 0, "Kn_half", [128, NT * 512], BF16)
    Qn_t = [at("SE", NT * 1024, "Qn_t0", [128, NT * 128], BF16),
            at("SE", NT * 1024 + NT * 256, "Qn_t1", [128, NT * 128], BF16)]
    st_wv = [at("SE", 0, "st_wv0", [128, D], F32), at("SE", 4096, "st_wv1", [128, D], F32)]
    st_s = [at("SE", 8192, "st_s0", [128, D], F32), at("SE", 12288, "st_s1", [128, D], F32)]
    mw_q = at("SE", 0, "mw_q", [128, ND * D], BF16)
    mw_o = at("SE", 0, "mw_o", [128, ND * D], BF16)
    mw_dkv = at("SE", 16384, "mw_dkv", [128, ND * LAT], BF16)
    mw_uv = at("SE", 16384, "mw_uv", [128, NL * D], BF16)
    mw_uk = at("SE", 24576, "mw_uk", [128, NL * D], BF16)
    acc1 = at("SE", 0, "acc1", [128, HT * D], F32)

    vs_row = at("SF", 0, "vs_row", [1, C], F32)
    rst1_row = at("SF", 4096, "rst1_row", [1, C], F32)
    tg_row = at("SF", 8192, "tg_row", [1, D], F32)
    scr_bf = at("SF", 12288, "scr_bf", [128, 2048], BF16)
    st_h2 = [at("SF", 0, "st_h20", [128, D], BF16), at("SF", 2048, "st_h21", [128, D], BF16)]
    st_x2 = at("SF", 4096, "st_x2", [128, D], F32)
    moe_hid = at("SF", 0, "moe_hid", [128, NH * 512], BF16)
    fin_s = [at("SD", 0, "fin_s0", [128, D], F32), at("SD", 4096, "fin_s1", [128, D], F32)]
    fin_g = [at("SD", 8192, "fin_g0", [128, D], F32), at("SD", 12288, "fin_g1", [128, D], F32)]

    rstd1_bc = at("SG", 0, "rstd1_bc", [128, C], F32)
    st_wo = [at("SG", 0, "st_wo0", [128, D], F32), at("SG", 4096, "st_wo1", [128, D], F32)]
    moe_h2c = at("SG", 0, "moe_h2c", [128, ND * 512], BF16)
    fin_x = [at("SD", 16384, "fin_x0", [128, D], F32), at("SD", 20480, "fin_x1", [128, D], F32)]

    pbank = [nc.alloc_psum_tensor(f"p{i}", [128, 512], F32) for i in range(6)]
    psD = pbank[4]
    psT0 = nc.alloc_psum_tensor("psT0", [128, 1024], BF16)
    psT1 = nc.alloc_psum_tensor("psT1", [128, 1024], BF16)

    SL = {}
    for name in ("xn_fm A_nm A_jm c_fm P0 P1 h2_fm_h attn_tm acc0 acc1 "
                 "stx0 stx1 stxn0 stxn1 stxb0 stxb1 Q_fm h1_fm k_fm K_fm q_fm V_fm v_tm "
                 "WT0 WT1 Kn_half Qn0 Qn1 stwv0 stwv1 sts0 sts1 mw_q mw_o mw_dkv mw_uv "
                 "mw_uk moe_w0 moe_w1 moe_w2 moe_h0 moe_h1 hid0 hid1 sth20 sth21 stx2 "
                 "stwo0 stwo1 fins0 fins1 fing0 fing1 finx0 finx1 rstd1_bc combine").split():
        SL[name] = Slot()

    ring4 = PRing(plan, [pbank[i][:] for i in range(4)])
    ringC = PRing(plan, [pbank[4][:], pbank[5][:]])
    ringT = PRing(plan, [psT0[:], psT1[:]])
    ringD = PRing(plan, [psD[:]])

    def transpose_pack4(src_blocks, dst_ap):
        ti, pst = ringT.acquire()
        n = len(src_blocks)
        assert n <= 8
        tl = None
        for i, blk in enumerate(src_blocks):
            fn = lambda E, blk=blk, i=i, pst=pst: E.transpose(
                pst[:, i * 128:(i + 1) * 128], blk, c_idbf[:])
            if i == n - 1:
                tl = plan.tok("tensor", fn)
            else:
                plan.on("tensor", fn)
        plan.wait("vector", tl)
        tv = plan.tok("vector", lambda E, dst_ap=dst_ap, n=n, pst=pst: E.tensor_copy(
            dst_ap, pst[:, 0:n * 128]))
        ringT.release(ti, tv)
        return tv

    def transpose_pack(src_blocks, dst_ap):
        n = len(src_blocks)
        nd = len(dst_ap.ap)
        tv = None
        for g in range(0, n, 8):
            m = min(8, n - g)
            if nd == 2:
                sub = dst_ap[:, g * 128:(g + m) * 128]
            else:
                sub = dst_ap[:, g:g + m, :]
            tv = transpose_pack4(src_blocks[g:g + m], sub)
        return tv

    cast = {}

    def phase_convert():
        items = [(nm, ins[nm], scr[nm]) for nm in ("gw", "wqa", "wka", "wva")]
        for e in range(NE):
            items.append((f"w1_{e}", ins["w1"][e], scr["w1"][e]))
            items.append((f"w2_{e}", ins["w2"][e], scr["w2"][e]))
        items += [(nm, ins[nm], scr[nm]) for nm in ("mq", "mdkv", "muk", "muv", "mo")]
        semof = {}
        for j, (nm, s_, d_) in enumerate(items):
            sem = QSEMS["gd"][j] if j < 4 else QSEMS["gd"][4 + (j - 4) % 4]
            plan.cnt[sem] += 16
            plan.ops["gpsimd"].append(
                ("tok", lambda E, s_=s_, d_=d_: E.dma_start(out=d_, in_=s_), sem, 16))
            semof[nm] = sem
        plan.qidx["gd"] = len(items)
        for nm, _, _ in items:
            cast[nm] = (semof[nm], plan.cnt[semof[nm]])

    def phase_consts():
        t_g2 = plan.ld(lambda E: E.dma_start(out=c_g2row[0:1, :], in_=ins["g2"][None, :]))
        t_gam = plan.ld(lambda E: E.dma_start(out=c_gam[0:1, :], in_=ins["gam"][None, :]))
        plan.wait("sync", cast["gw"])
        t_gw = plan.ld(lambda E: E.dma_start(
            out=c_gwsb[:].rearrange("p (dc e) -> p dc e", dc=ND),
            in_=scr["gw"].rearrange("(dc p) e -> p dc e", p=128)))
        TOK["gw"] = t_gw

        tp = plan.tok("gpsimd", lambda E: E.iota(
            c_iotaP[:], pattern=[[0, 1]], base=0, channel_multiplier=1,
            allow_small_or_imprecise_dtypes=True))
        tf = plan.tok("gpsimd", lambda E: E.iota(
            c_mask[:], pattern=[[1, 128]], base=0, channel_multiplier=0,
            allow_small_or_imprecise_dtypes=True))
        plan.wait("vector", tp)
        plan.wait("vector", tf)
        t_idb = plan.tok("vector", lambda E: E.tensor_scalar(
            out=c_idbf[:], in0=c_mask[:], scalar1=c_iotaP[:, 0:1], scalar2=None,
            op0=ALU.is_equal))
        t_idf = plan.tok("vector", lambda E: E.tensor_scalar(
            out=c_idf[:], in0=c_mask[:], scalar1=c_iotaP[:, 0:1], scalar2=None,
            op0=ALU.is_equal))
        plan.wait("vector", t_idf)
        plan.wait("vector", t_idb)
        t_mkT = plan.tok("vector", lambda E: E.tensor_scalar(
            out=c_maskbfT[:], in0=c_mask[:], scalar1=c_iotaP[:, 0:1], scalar2=-1e9,
            op0=ALU.is_lt, op1=ALU.mult))
        plan.wait("vector", t_mkT)
        t_msk = plan.tok("vector", lambda E: E.tensor_scalar(
            out=c_mask[:], in0=c_mask[:], scalar1=c_iotaP[:, 0:1], scalar2=-1e9,
            op0=ALU.is_gt, op1=ALU.mult))
        plan.wait("vector", t_msk)
        t_msk = plan.tok("vector", lambda E: E.tensor_copy(c_maskbf[:], c_mask[:]))
        t_ob = plan.tok("vector", lambda E: E.memset(c_onebf[:], 1.0))
        t_or = plan.tok("vector", lambda E: E.memset(c_onerf[:], 1.0))
        t_e1 = plan.tok("vector", lambda E: E.memset(c_epsln[:], LN_EPS))
        t_e2 = plan.tok("vector", lambda E: E.memset(c_epsrm[:], RMS_EPS))
        t_e3 = plan.tok("vector", lambda E: E.memset(c_shift[:], EXP_SHIFT))
        TOK["epsln"], TOK["epsrm"], TOK["shift"] = t_e1, t_e2, t_e3
        TOK["idbf"], TOK["idf"], TOK["mask"] = t_idb, t_idf, t_msk
        TOK["onebf"], TOK["onerf"] = t_ob, t_or

        specs = [("g1", c_g1col, 0), ("bq", c_bcol, 0),
                 ("bk", c_bcol, ND), ("bv", c_bcol, 2 * ND)]
        plan.wait("tensor", t_or)
        prev_cols = None
        for nm, dst, off in specs:
            for t in ([prev_cols] if prev_cols else []):
                plan.wait("sync", t)
            tk = plan.ld(lambda E, nm=nm: E.dma_start(
                out=c_rowtmp[0:1, :], in_=ins[nm][None, :]))
            plan.wait("tensor", tk)
            di, pd = ringD.acquire()
            tm = None
            for dc in range(ND):
                tm = plan.tok("tensor", lambda E, dc=dc, pd=pd: E.matmul(
                    pd[:, dc:dc + 1], c_rowtmp[0:1, dc * 128:(dc + 1) * 128],
                    c_onerf[0:1, 0:1], start=True, stop=True))
            plan.wait("vector", tm)
            tv = plan.tok("vector", lambda E, dst=dst, off=off, pd=pd: E.tensor_copy(
                dst[:, off:off + ND], pd[:, 0:ND]))
            ringD.release(di, tv)
            prev_cols = tm
            TOK["cols"] = tv
        plan.wait("tensor", t_gam)
        di, pd = ringD.acquire()
        t7 = plan.tok("tensor", lambda E, pd=pd: E.matmul(
            pd[:, 0:1], c_onerf[0:1, :], c_gam[0:1, 0:1], start=True, stop=True))
        plan.wait("vector", t7)
        t8 = plan.tok("vector", lambda E, pd=pd: E.tensor_copy(c_gamcol[:, 0:1], pd[:, 0:1]))
        ringD.release(di, t8)
        TOK["gamcol"] = t8
        plan.wait("tensor", t_g2)
        plan.wait("tensor", TOK["cols"])
        tv = None
        for half in range(D // 512):
            i4, pq = ring4.acquire()
            tm = plan.tok("tensor", lambda E, half=half, pq=pq: E.matmul(
                pq, c_onerf[0:1, :], c_g2row[0:1, half * 512:(half + 1) * 512],
                start=True, stop=True))
            plan.wait("vector", tm)
            tv = plan.tok("vector", lambda E, half=half, pq=pq: E.tensor_copy(
                c_g2bc[:, half * 512:(half + 1) * 512], pq))
            ring4.release(i4, tv)
        TOK["g2bc"] = tv
        if os.environ.get("GWWAIT"):
            plan.wait("tensor", TOK["gw"])

    S1CUT = int(os.environ.get("S1CUT", "99"))

    def phase_s1(s):
        if S1CUT < 1:
            return
        prev_sq = [None, None]
        prev_msq = [None, None]
        xnfm_tok = None
        for tc in range(NT):
            b = tc % 2
            cb = 16 * b
            xsl, xnsl, xbsl = SL[f"stx{b}"], SL[f"stxn{b}"], SL[f"stxb{b}"]
            xt, xnt, xbt = st_x[b], st_xn[b], st_xb[b]
            xsl.prep_write(plan, "sync")
            t_ld = plan.ld(lambda E, tc=tc, xt=xt: E.dma_start(
                out=xt[:], in_=x_in[s][tc * 128:(tc + 1) * 128, :]))
            if S1CUT < 2:
                continue
            plan.wait("vector", t_ld)
            plan.wait("vector", prev_sq[b])
            plan.wait("scalar", t_ld)
            plan.wait("scalar", prev_msq[b])
            t_ssq = plan.tok("scalar", lambda E, xt=xt, cb=cb: E.activation(
                scr_bf[:, 0:D], xt[:], AF.Square, bias=0.0, scale=1.0,
                accum_out=w_cols[:, cb:cb + 1]))
            t_sum = plan.tok("vector", lambda E, xt=xt, cb=cb: E.reduce_sum(
                w_cols[:, cb + 1:cb + 2], xt[:], axis=AX.X, negate=True))
            plan.wait("vector", t_sum)
            t_nm = plan.tok("vector", lambda E, cb=cb: E.tensor_scalar(
                out=w_cols[:, cb + 2:cb + 3], in0=w_cols[:, cb + 1:cb + 2],
                scalar1=1.0 / D, scalar2=None, op0=ALU.mult))
            plan.wait("vector", t_ssq)
            t_msq = plan.tok("vector", lambda E, cb=cb: E.tensor_scalar(
                out=w_cols[:, cb + 3:cb + 4], in0=w_cols[:, cb:cb + 1],
                scalar1=1.0 / D, scalar2=None, op0=ALU.mult))
            prev_msq[b] = t_msq
            plan.wait("vector", t_nm)
            plan.wait("vector", t_msq)
            t_nv = plan.tok("vector", lambda E, cb=cb: E.scalar_tensor_tensor(
                out=w_cols[:, cb + 4:cb + 5], in0=w_cols[:, cb + 2:cb + 3],
                scalar=w_cols[:, cb + 2:cb + 3], in1=w_cols[:, cb + 3:cb + 4],
                op0=ALU.mult, op1=ALU.subtract))
            plan.wait("scalar", t_nv)
            plan.wait("scalar", TOK["epsln"])
            plan.wait("scalar", TOK["epsrm"])
            t_sq = plan.tok("scalar", lambda E, cb=cb: E.activation(
                w_cols[:, cb + 5:cb + 6], w_cols[:, cb + 4:cb + 5], AF.Sqrt,
                bias=c_epsln[:, 0:1], scale=-1.0))
            plan.wait("vector", t_sq)
            t_rstd = plan.tok("vector", lambda E, cb=cb: E.reciprocal(
                w_cols[:, cb + 6:cb + 7], w_cols[:, cb + 5:cb + 6]))
            if S1CUT < 3:
                continue
            xnsl.prep_write(plan, "vector")
            plan.wait("vector", t_rstd)
            t_xn = plan.tok("vector", lambda E, xt=xt, xnt=xnt, cb=cb: E.tensor_scalar(
                out=xnt[:], in0=xt[:], scalar1=w_cols[:, cb + 2:cb + 3],
                scalar2=w_cols[:, cb + 6:cb + 7], op0=ALU.add, op1=ALU.mult))
            xsl.users.append(t_xn)
            xnsl.ready = t_xn
            t_r2 = plan.tok("vector", lambda E, cb=cb: E.tensor_scalar(
                out=w_cols[:, cb + 7:cb + 8], in0=w_cols[:, cb + 6:cb + 7],
                scalar1=w_cols[:, cb + 6:cb + 7], scalar2=None, op0=ALU.mult))
            plan.wait("vector", t_r2)
            t_vr = plan.tok("vector", lambda E, cb=cb: E.tensor_scalar(
                out=w_cols[:, cb + 8:cb + 9], in0=w_cols[:, cb + 4:cb + 5],
                scalar1=w_cols[:, cb + 7:cb + 8], scalar2=None, op0=ALU.mult))
            plan.wait("scalar", t_vr)
            t_sq2 = plan.tok("scalar", lambda E, cb=cb: E.activation(
                w_cols[:, cb + 9:cb + 10], w_cols[:, cb + 8:cb + 9], AF.Sqrt,
                bias=c_epsrm[:, 0:1], scale=-1.0))
            prev_sq[b] = t_sq2
            plan.wait("vector", t_sq2)
            t_rs1 = plan.tok("vector", lambda E, tc=tc, cb=cb: E.reciprocal(
                w_rst1[:, tc:tc + 1], w_cols[:, cb + 9:cb + 10]))
            TOK[f"rst1_{tc}"] = t_rs1
            if S1CUT < 4:
                continue
            t_b2 = plan.tok("vector", lambda E, cb=cb: E.tensor_scalar(
                out=w_cols[:, cb + 10:cb + 11], in0=w_cols[:, cb + 2:cb + 3],
                scalar1=w_cols[:, cb + 6:cb + 7], scalar2=None, op0=ALU.mult))
            xbsl.prep_write(plan, "scalar")
            plan.wait("scalar", t_b2)
            plan.wait("scalar", t_ld)
            t_xb = plan.tok("scalar", lambda E, xt=xt, xbt=xbt, cb=cb: E.activation(
                xbt[:], xt[:], AF.Identity, bias=w_cols[:, cb + 10:cb + 11],
                scale=w_cols[:, cb + 6:cb + 7]))
            xbsl.ready = t_xb
            xsl.users.append(t_xb)
            t_st = plan.st(lambda E, xnt=xnt, tc=tc: E.dma_start(
                out=d_xn[tc * 128:(tc + 1) * 128, :], in_=xnt[:]))
            xnsl.users.append(t_st)
            TOK[f"xn_st_{tc}"] = t_st
            plan.wait("tensor", TOK["idbf"])
            plan.wait("tensor", t_xb)
            blocks = [xbt[:, dc * 128:(dc + 1) * 128] for dc in range(ND)]
            dst = xn_fm[:].rearrange("p (d c) -> p d c", d=ND)[:, :, tc * 128:(tc + 1) * 128]
            xnfm_tok = transpose_pack(blocks, dst)
            xbsl.users.append(xnfm_tok)
        if S1CUT < 5:
            return
        SL["xn_fm"].ready = xnfm_tok
        SL["xn_fm"].users = []
        plan.wait("scalar", xnfm_tok)
        t_fmst = plan.st(lambda E: E.dma_start(
            out=d_xnfm.rearrange("e p c -> p e c"),
            in_=xn_fm[:].rearrange("p (e c) -> p e c", e=ND)))
        SL["xn_fm"].users.append(t_fmst)
        TOK["xnfm_st"] = t_fmst

        plan.wait("tensor", TOK["idf"])
        tvr = None
        for g in range((NT + 3) // 4):
            n = min(4, NT - g * 4)
            di, pd = ringD.acquire()
            tl = None
            for i in range(n):
                tc = g * 4 + i
                plan.wait("tensor", TOK[f"rst1_{tc}"])
                tl = plan.tok("tensor", lambda E, tc=tc, i=i, pd=pd: E.matmul(
                    pd[0:1, i * 128:(i + 1) * 128], w_rst1[:, tc:tc + 1], c_idf[:],
                    start=True, stop=True))
            plan.wait("vector", tl)
            tvr = plan.tok("vector", lambda E, g=g, n=n, pd=pd: E.tensor_copy(
                rst1_row[0:1, g * 512:g * 512 + n * 128], pd[0:1, 0:n * 128]))
            ringD.release(di, tvr)
        plan.wait("tensor", tvr)
        tb = None
        for ch in range(C // 512):
            i4, pq = ring4.acquire()
            tm = plan.tok("tensor", lambda E, ch=ch, pq=pq: E.matmul(
                pq, c_onerf[0:1, :], rst1_row[0:1, ch * 512:(ch + 1) * 512],
                start=True, stop=True))
            plan.wait("vector", tm)
            tb = plan.tok("vector", lambda E, ch=ch, pq=pq: E.tensor_copy(
                rstd1_bc[:, ch * 512:(ch + 1) * 512], pq))
            ring4.release(i4, tb)
        SL["rstd1_bc"].ready = tb
        SL["rstd1_bc"].users = []
        load_wt(0)
        load_wt(1)

    WQKV = ["wqa", "wka", "wva"]

    def load_wt(wi):
        nm = WQKV[wi]
        wt, wsl = WT[wi % 2], SL[f"WT{wi % 2}"]
        wsl.prep_write(plan, "sync")
        plan.wait("sync", cast[nm])
        tls = []
        for jd in range(ND):
            tls.append(plan.ld(lambda E, nm=nm, wt=wt, jd=jd: E.dma_start_transpose(
                wt[:, jd * D:(jd + 1) * D], scr[nm][:, jd * 128:(jd + 1) * 128])))
        wsl.ready = tls

    def phase_qkv(s):
        weights = [("wqa", Q_fm, SL["Q_fm"], 0), ("wka", K_fm, SL["K_fm"], ND),
                   ("wva", V_fm, SL["V_fm"], 2 * ND)]
        for wi, (nm, dst, dsl, boff) in enumerate(weights):
            if wi + 1 == 2:
                load_wt(2)
            wt, wsl = WT[wi % 2], SL[f"WT{wi % 2}"]
            wsl.use(plan, "tensor")
            SL["xn_fm"].use(plan, "tensor")
            t_ev = None
            t_mm = None
            for ic in range(ND):
                for th in range(C // 512):
                    i4, pq = ring4.acquire()
                    for jd in range(ND):
                        fn = lambda E, wt=wt, ic=ic, jd=jd, th=th, pq=pq: E.matmul(
                            pq, wt[:, jd * D + ic * 128: jd * D + (ic + 1) * 128],
                            xn_fm[:, jd * C + th * 512: jd * C + (th + 1) * 512],
                            start=(jd == 0), stop=(jd == ND - 1))
                        if jd == ND - 1:
                            t_mm = plan.tok("tensor", fn)
                        else:
                            plan.on("tensor", fn)
                    plan.wait("scalar", t_mm)
                    t_ev = plan.tok("scalar", lambda E, dst=dst, ic=ic, th=th,
                                    boff=boff, pq=pq: E.activation(
                        dst[:, ic * C + th * 512: ic * C + (th + 1) * 512],
                        pq, AF.Identity, bias=c_bcol[:, boff + ic:boff + ic + 1], scale=1.0))
                    ring4.release(i4, t_ev)
            wsl.users.append(t_mm)
            dsl.ready = t_ev
            dsl.users = []
            if nm == "wqa":
                norms(Q_fm, SL["Q_fm"], w_invq, True)
            elif nm == "wka":
                norms(K_fm, SL["K_fm"], w_invk, False)

    norm_chain = [None, None]

    def norms(fm, fsl, inv, isq):
            prev_act, prev_iv = norm_chain
            fsl.use(plan, "vector")
            for dc in range(ND):
                plan.wait("scalar", prev_act)
                t_ssq = plan.tok("scalar", lambda E, fm=fm, dc=dc: E.activation(
                    scr_bf[:, 0:min(C, 2048)], fm[:, dc * C:dc * C + min(C, 2048)],
                    AF.Square, bias=0.0, scale=1.0, accum_out=w_cols[:, 32:33]))
                t_qs = None
                if isq:
                    t_qs = plan.tok("vector", lambda E, fm=fm, dc=dc: E.reduce_sum(
                        w_qsum[:, dc:dc + 1], fm[:, dc * C:(dc + 1) * C], axis=AX.X))
                plan.wait("scalar", t_ssq)
                plan.wait("scalar", prev_iv)
                t_sq = plan.tok("scalar", lambda E: E.activation(
                    w_cols[:, 33:34], w_cols[:, 32:33], AF.Sqrt, bias=0.0, scale=1.0))
                prev_act = t_sq
                plan.wait("vector", t_sq)
                t_iv = plan.tok("vector", lambda E, inv=inv, dc=dc: E.reciprocal(
                    inv[:, dc:dc + 1], w_cols[:, 33:34]))
                prev_iv = t_iv
                plan.wait("vector", t_iv)
                t_sc = plan.tok("vector", lambda E, fm=fm, dc=dc, inv=inv: E.tensor_scalar(
                    out=fm[:, dc * C:(dc + 1) * C], in0=fm[:, dc * C:(dc + 1) * C],
                    scalar1=inv[:, dc:dc + 1], scalar2=None, op0=ALU.mult))
                fsl.ready = t_sc
                if isq:
                    plan.wait("vector", t_qs)
                    plan.tok("vector", lambda E, dc=dc, inv=inv: E.tensor_scalar(
                        out=w_qsum[:, dc:dc + 1], in0=w_qsum[:, dc:dc + 1],
                        scalar1=inv[:, dc:dc + 1], scalar2=None, op0=ALU.mult))
            norm_chain[0], norm_chain[1] = prev_act, prev_iv

    def phase_A(s):
        plan.wait("tensor", TOK["idbf"])
        krow_tok = None
        for jh in range(2):
            SL["Kn_half"].prep_write(plan, "vector")
            SL["K_fm"].use(plan, "tensor")
            kn_tok = None
            for tc in range(NT):
                blocks = [K_fm[:, (jh * 4 + i) * C + tc * 128:(jh * 4 + i) * C + (tc + 1) * 128]
                          for i in range(4)]
                kn_tok = transpose_pack(blocks, Kn_half[:, tc * 512:(tc + 1) * 512])
                plan.wait("vector", kn_tok)
                if jh == 0:
                    t_kr = plan.tok("vector", lambda E, tc=tc: E.reduce_sum(
                        w_krow[:, tc:tc + 1], Kn_half[:, tc * 512:(tc + 1) * 512], axis=AX.X))
                else:
                    t_kp = plan.tok("vector", lambda E, tc=tc: E.reduce_sum(
                        w_cols[:, 34:35], Kn_half[:, tc * 512:(tc + 1) * 512], axis=AX.X))
                    plan.wait("vector", t_kp)
                    t_kr = plan.tok("vector", lambda E, tc=tc: E.tensor_add(
                        w_krow[:, tc:tc + 1], w_krow[:, tc:tc + 1], w_cols[:, 34:35]))
                TOK[f"krow_{tc}"] = t_kr
            SL["Kn_half"].ready = kn_tok
            if jh == 1:
                for tc in range(NT):
                    plan.wait("vector", TOK[f"krow_{tc}"])
                krow_tok = plan.tok("vector", lambda E: E.tensor_copy(w_krbf[:], w_krow[:]))
            for nt in range(ND):
                qb, qsl = Qn_t[nt % 2], SL[f"Qn{nt % 2}"]
                qsl.prep_write(plan, "vector")
                SL["Q_fm"].use(plan, "tensor")
                t_q = None
                for g in range((NT + 7) // 8):
                    n = min(8, NT - g * 8)
                    blocks = [Q_fm[:, nt * C + (g * 8 + i) * 128: nt * C + (g * 8 + i + 1) * 128]
                              for i in range(n)]
                    t_q = transpose_pack(blocks, qb[:, g * 1024:g * 1024 + n * 128])
                qsl.ready = t_q
                qsl.use(plan, "tensor")
                SL["Kn_half"].use(plan, "tensor")
                i4, pq = ring4.acquire()
                t_mm = None
                for tc in range(NT):
                    fn = lambda E, qb=qb, tc=tc, pq=pq: E.matmul(
                        pq, qb[:, tc * 128:(tc + 1) * 128],
                        Kn_half[:, tc * 512:(tc + 1) * 512],
                        start=(tc == 0), stop=(tc == NT - 1))
                    if tc == NT - 1:
                        t_mm = plan.tok("tensor", fn)
                    else:
                        plan.on("tensor", fn)
                qsl.users.append(t_mm)
                SL["Kn_half"].users.append(t_mm)
                plan.wait("vector", t_mm)
                t_ev = plan.tok("vector", lambda E, nt=nt, jh=jh, pq=pq: E.tensor_copy(
                    A_nm[:, nt * D + jh * 512: nt * D + (jh + 1) * 512], pq))
                ring4.release(i4, t_ev)
                TOK[f"anm_{nt}_{jh}"] = t_ev
                if jh == 1:
                    plan.wait("tensor", krow_tok)
                    di, pd = ringD.acquire()
                    t_rs = None
                    for tc in range(NT):
                        fn = lambda E, qb=qb, tc=tc, pd=pd: E.matmul(
                            pd[:, 0:1], qb[:, tc * 128:(tc + 1) * 128],
                            w_krbf[:, tc:tc + 1], start=(tc == 0), stop=(tc == NT - 1))
                        if tc == NT - 1:
                            t_rs = plan.tok("tensor", fn)
                        else:
                            plan.on("tensor", fn)
                    qsl.users.append(t_rs)
                    plan.wait("vector", t_rs)
                    t_rc = plan.tok("vector", lambda E, nt=nt, pd=pd: E.tensor_copy(
                        w_rsc[:, nt:nt + 1], pd[:, 0:1]))
                    ringD.release(di, t_rc)
                    plan.wait("vector", t_rc)
                    t_d1 = plan.tok("vector", lambda E, nt=nt: E.scalar_tensor_tensor(
                        out=w_cols[:, 35:36], in0=w_qsum[:, nt:nt + 1], scalar=EPS_A,
                        in1=w_rsc[:, nt:nt + 1], op0=ALU.mult, op1=ALU.add))
                    plan.wait("vector", t_d1)
                    t_d2 = plan.tok("vector", lambda E: E.tensor_scalar(
                        out=w_cols[:, 36:37], in0=w_cols[:, 35:36], scalar1=float(D),
                        scalar2=None, op0=ALU.add))
                    plan.wait("vector", t_d2)
                    t_d3 = plan.tok("vector", lambda E: E.reciprocal(
                        w_cols[:, 37:38], w_cols[:, 36:37]))
                    plan.wait("vector", t_d3)
                    plan.wait("vector", TOK["gamcol"])
                    t_d4 = plan.tok("vector", lambda E, nt=nt: E.tensor_scalar(
                        out=w_tgc[:, nt:nt + 1], in0=w_cols[:, 37:38],
                        scalar1=c_gamcol[:, 0:1], scalar2=None, op0=ALU.mult))
                    TOK[f"tgc_{nt}"] = t_d4
        t_as = None
        for nt in range(ND):
            plan.wait("vector", TOK[f"tgc_{nt}"])
            plan.wait("vector", TOK[f"anm_{nt}_0"])
            plan.wait("vector", TOK[f"anm_{nt}_1"])
            t_as = plan.tok("vector", lambda E, nt=nt: E.tensor_scalar(
                out=A_nm[:, nt * D:(nt + 1) * D], in0=A_nm[:, nt * D:(nt + 1) * D],
                scalar1=w_tgc[:, nt:nt + 1], scalar2=None, op0=ALU.mult))
            TOK[f"anms_{nt}"] = t_as
        t_aj = None
        for jd in range(ND):
            for nt in range(ND):
                plan.wait("tensor", TOK[f"anms_{nt}"])
            blocks = [A_nm[:, nt * D + jd * 128: nt * D + (jd + 1) * 128] for nt in range(ND)]
            t_aj = transpose_pack(blocks, A_jm[:, jd * D:(jd + 1) * D])
        SL["A_jm"].ready = t_aj
        SL["A_jm"].users = []
        plan.wait("tensor", TOK["idf"])
        tgr = None
        for g in range(ND // 4):
            di, pd = ringD.acquire()
            tl = None
            for i in range(4):
                nt = g * 4 + i
                plan.wait("tensor", TOK[f"tgc_{nt}"])
                tl = plan.tok("tensor", lambda E, nt=nt, i=i, pd=pd: E.matmul(
                    pd[0:1, i * 128:(i + 1) * 128], w_tgc[:, nt:nt + 1], c_idf[:],
                    start=True, stop=True))
            plan.wait("vector", tl)
            tgr = plan.tok("vector", lambda E, g=g, pd=pd: E.tensor_copy(
                tg_row[0:1, g * 512:(g + 1) * 512], pd[0:1, 0:512]))
            ringD.release(di, tgr)
        TOK["tgrow"] = tgr
        SL["V_fm"].use(plan, "tensor")
        plan.wait("tensor", TOK["onebf"])
        tvs = None
        for ch in range(C // 512):
            di, pd = ringD.acquire()
            t_vs = None
            for jd in range(ND):
                fn = lambda E, jd=jd, ch=ch, pd=pd: E.matmul(
                    pd[0:1, :], c_onebf[:, 0:1],
                    V_fm[:, jd * C + ch * 512: jd * C + (ch + 1) * 512],
                    start=(jd == 0), stop=(jd == ND - 1))
                if jd == ND - 1:
                    t_vs = plan.tok("tensor", fn)
                else:
                    plan.on("tensor", fn)
            plan.wait("vector", t_vs)
            tvs = plan.tok("vector", lambda E, ch=ch, pd=pd: E.tensor_copy(
                vs_row[0:1, ch * 512:(ch + 1) * 512], pd[0:1, :]))
            ringD.release(di, tvs)
        TOK["vsrow"] = tvs

    def phase_wv2(s):
        SL["A_jm"].use(plan, "tensor")
        SL["V_fm"].use(plan, "tensor")
        plan.wait("tensor", TOK["vsrow"])
        plan.wait("tensor", TOK["tgrow"])
        for ct in range(NT):
            b = ct % 2
            wsl, ssl = SL[f"stwv{b}"], SL[f"sts{b}"]
            wsl.prep_write(plan, "sync")
            plan.wait("sync", TOK[f"xn_st_{ct}"])
            t_ld = plan.ld(lambda E, ct=ct, b=b: E.dma_start(
                out=st_wv[b][:], in_=d_xn[ct * 128:(ct + 1) * 128, :]))
            wsl.ready = t_ld
            ssl.prep_write(plan, "vector")
            for nh in range(2):
                i4, pq = ring4.acquire()
                plan.on("tensor", lambda E, ct=ct, nh=nh, pq=pq: E.matmul(
                    pq, vs_row[0:1, ct * 128:(ct + 1) * 128],
                    tg_row[0:1, nh * 512:(nh + 1) * 512], start=True, stop=False))
                t_mm = None
                for jd in range(ND):
                    fn = lambda E, ct=ct, nh=nh, jd=jd, pq=pq: E.matmul(
                        pq, V_fm[:, jd * C + ct * 128: jd * C + (ct + 1) * 128],
                        A_jm[:, jd * D + nh * 512: jd * D + (nh + 1) * 512],
                        start=False, stop=(jd == ND - 1))
                    if jd == ND - 1:
                        t_mm = plan.tok("tensor", fn)
                    else:
                        plan.on("tensor", fn)
                SL["A_jm"].users.append(t_mm)
                SL["V_fm"].users.append(t_mm)
                plan.wait("vector", t_mm)
                wsl.use(plan, "vector")
                t_ev = plan.tok("vector", lambda E, b=b, nh=nh, pq=pq: E.tensor_add(
                    st_s[b][:, nh * 512:(nh + 1) * 512], pq,
                    st_wv[b][:, nh * 512:(nh + 1) * 512]))
                ring4.release(i4, t_ev)
                ssl.ready = t_ev
                wsl.users.append(t_ev)
            plan.wait("scalar", ssl.ready)
            t_st = plan.st(lambda E, ct=ct, b=b: E.dma_start(
                out=d_s[ct * 128:(ct + 1) * 128, :], in_=st_s[b][:]))
            ssl.users.append(t_st)
            TOK[f"s_st_{ct}"] = t_st

    def phase_mla(s):
        SL["xn_fm"].prep_write(plan, "sync")
        plan.wait("sync", TOK["xnfm_st"])
        t_x = plan.ld(lambda E: E.dma_start(
            out=xn_fm[:].rearrange("p (e c) -> p e c", e=ND),
            in_=d_xnfm.rearrange("e p c -> p e c")))
        SL["xn_fm"].ready = t_x
        SL["xn_fm"].users = []
        for nm, dst, dsl, a in (("mq", mw_q, SL["mw_q"], ND),
                                ("mdkv", mw_dkv, SL["mw_dkv"], ND),
                                ("muk", mw_uk, SL["mw_uk"], NL)):
            dsl.prep_write(plan, "sync")
            plan.wait("sync", cast[nm])
            t = plan.ld(lambda E, nm=nm, dst=dst, a=a: E.dma_start(
                out=dst[:].rearrange("p (a b) -> p a b", a=a),
                in_=scr[nm].rearrange("(a p) b -> p a b", p=128)))
            dsl.ready = t
            dsl.users = []
        SL["xn_fm"].use(plan, "vector")
        SL["rstd1_bc"].use(plan, "vector")
        t_h1 = None
        for dc in range(ND):
            t_h1 = plan.tok("vector", lambda E, dc=dc: E.scalar_tensor_tensor(
                out=h1_fm[:, dc * C:(dc + 1) * C], in0=xn_fm[:, dc * C:(dc + 1) * C],
                scalar=c_g1col[:, dc:dc + 1], in1=rstd1_bc[:, 0:C],
                op0=ALU.mult, op1=ALU.mult))
        SL["h1_fm"].ready = t_h1
        SL["h1_fm"].users = []
        SL["xn_fm"].users.append(t_h1)
        SL["rstd1_bc"].users.append(t_h1)

        def proj_fm(dst, dsl, wsb, wsl, nin, nout, src, ssl, scale):
            dsl.prep_write(plan, "scalar")
            wsl.use(plan, "tensor")
            ssl.use(plan, "tensor")
            t_ev = None
            for ic in range(nout):
                for th in range(C // 512):
                    i4, pq = ring4.acquire()
                    t_mm = None
                    for jd in range(nin):
                        fn = lambda E, ic=ic, th=th, jd=jd, pq=pq, wsb=wsb, src=src, nout=nout, nin=nin: E.matmul(
                            pq, wsb[:, jd * (nout * 128) + ic * 128: jd * (nout * 128) + (ic + 1) * 128],
                            src[:, jd * C + th * 512: jd * C + (th + 1) * 512],
                            start=(jd == 0), stop=(jd == nin - 1))
                        if jd == nin - 1:
                            t_mm = plan.tok("tensor", fn)
                        else:
                            plan.on("tensor", fn)
                    wsl.users.append(t_mm)
                    ssl.users.append(t_mm)
                    plan.wait("scalar", t_mm)
                    t_ev = plan.tok("scalar", lambda E, ic=ic, th=th, pq=pq, dst=dst, scale=scale: E.activation(
                        dst[:, ic * C + th * 512: ic * C + (th + 1) * 512],
                        pq, AF.Copy, bias=0.0, scale=scale))
                    ring4.release(i4, t_ev)
            dsl.ready = t_ev
            dsl.users = []

        SL["q_fm"].prep_write(plan, "scalar")
        proj_fm(q_fm, SL["q_fm"], mw_q, SL["mw_q"], ND, ND, h1_fm, SL["h1_fm"], 0.125)
        SL["mw_o"].prep_write(plan, "sync")
        for t in SL["mw_q"].users:
            plan.wait("sync", t)
        plan.wait("sync", cast["mo"])
        t_o = plan.ld(lambda E: E.dma_start(
            out=mw_o[:].rearrange("p (a b) -> p a b", a=ND),
            in_=scr["mo"].rearrange("(a p) b -> p a b", p=128)))
        SL["mw_o"].ready = t_o
        SL["mw_o"].users = []
        SL["c_fm"].prep_write(plan, "scalar")
        for t in SL["xn_fm"].users:
            plan.wait("scalar", t)
        proj_fm(c_fm, SL["c_fm"], mw_dkv, SL["mw_dkv"], ND, NL, h1_fm, SL["h1_fm"], 1.0)
        SL["mw_uv"].prep_write(plan, "sync")
        for t in SL["mw_dkv"].users:
            plan.wait("sync", t)
        plan.wait("sync", cast["muv"])
        t_uv = plan.ld(lambda E: E.dma_start(
            out=mw_uv[:].rearrange("p (a b) -> p a b", a=NL),
            in_=scr["muv"].rearrange("(a p) b -> p a b", p=128)))
        SL["mw_uv"].ready = t_uv
        SL["mw_uv"].users = []
        SL["k_fm"].prep_write(plan, "scalar")
        for t in SL["h1_fm"].users:
            plan.wait("scalar", t)
        proj_fm(k_fm, SL["k_fm"], mw_uk, SL["mw_uk"], NL, ND, c_fm, SL["c_fm"], 1.0)
        SL["v_tm"].prep_write(plan, "scalar")
        SL["mw_uv"].use(plan, "tensor")
        SL["c_fm"].use(plan, "tensor")
        t_ev = None
        for tc in range(NT):
            for oh in range(D // 512):
                i4, pq = ring4.acquire()
                t_mm = None
                for jd in range(NL):
                    fn = lambda E, tc=tc, oh=oh, jd=jd, pq=pq: E.matmul(
                        pq, c_fm[:, jd * C + tc * 128: jd * C + (tc + 1) * 128],
                        mw_uv[:, jd * D + oh * 512: jd * D + (oh + 1) * 512],
                        start=(jd == 0), stop=(jd == NL - 1))
                    if jd == NL - 1:
                        t_mm = plan.tok("tensor", fn)
                    else:
                        plan.on("tensor", fn)
                SL["mw_uv"].users.append(t_mm)
                SL["c_fm"].users.append(t_mm)
                plan.wait("scalar", t_mm)
                t_ev = plan.tok("scalar", lambda E, tc=tc, oh=oh, pq=pq: E.activation(
                    v_tm[:, tc * D + oh * 512: tc * D + (oh + 1) * 512],
                    pq, AF.Copy, bias=0.0, scale=1.0))
                ring4.release(i4, t_ev)
        SL["v_tm"].ready = t_ev
        SL["v_tm"].users = []

    def load_w1(e):
        i1 = (2 * e) % 3
        r1, s1 = moe_w[i1], SL[moe_wsl_names[i1]]
        s1.prep_write(plan, "sync")
        plan.wait("sync", cast[f"w1_{e}"])
        t1 = plan.ld(lambda E, e=e, r1=r1: E.dma_start(
            out=r1[:].rearrange("p (a b) -> p a b", a=ND),
            in_=scr["w1"][e].rearrange("(a p) b -> p a b", p=128)))
        s1.ready = t1
        return r1, s1

    def load_w2(e):
        i2 = (2 * e + 1) % 3
        r2, s2 = moe_w[i2], SL[moe_wsl_names[i2]]
        s2.prep_write(plan, "sync")
        plan.wait("sync", cast[f"w2_{e}"])
        t2 = plan.ld(lambda E, e=e, r2=r2: E.dma_start(
            out=r2[:].rearrange("p (a b) -> p a b", a=NH),
            in_=scr["w2"][e].rearrange("(a p) b -> p a b", p=128)))
        s2.ready = t2
        return r2, s2

    def phase_attn(s, qh):
        SL["q_fm"].use(plan, "tensor")
        SL["k_fm"].use(plan, "tensor")
        SL["v_tm"].use(plan, "tensor")
        plan.wait("tensor", TOK["mask"])
        plan.wait("scalar", TOK["shift"])
        plan.wait("tensor", TOK["onebf"])
        prev_rcp = TOK.setdefault("prev_rcp", [None, None, None, None])
        attn_tok = None

        def do_scores(it, h, tp):
            dt = h // 2
            pb = 64 * (h % 2)
            qa = 2 * tp
            qb = qa + 1
            nk = qb + 1
            Pt, Ptsl = P_[it % 2], SL[f"P{it % 2}"]
            Ptsl.prep_write(plan, "scalar")
            exp_toks = []
            for g in range((nk + 1) // 2):
                m = min(2, nk - g * 2)
                si, pS = ring4.acquire()
                t_mm = None
                for i in range(m):
                    kc = g * 2 + i
                    off = i * 256
                    if kc < qb:
                        fn = lambda E, kc=kc, off=off, qa=qa, pS=pS, dt=dt, pb=pb: E.matmul(
                            pS[:, off:off + 256],
                            k_fm[pb:pb + 64, dt * C + kc * 128: dt * C + (kc + 1) * 128],
                            q_fm[pb:pb + 64, dt * C + qa * 128: dt * C + (qa + 2) * 128],
                            start=True, stop=(kc != qa))
                        if kc == qa:
                            plan.on("tensor", fn)
                            fn = lambda E, off=off, pS=pS: E.matmul(
                                pS[:, off:off + 128], c_idbf[:], c_maskbfT[:],
                                start=False, stop=True)
                    else:
                        fn0 = lambda E, kc=kc, off=off, qb=qb, pS=pS, dt=dt, pb=pb: E.matmul(
                            pS[:, off + 128:off + 256],
                            k_fm[pb:pb + 64, dt * C + kc * 128: dt * C + (kc + 1) * 128],
                            q_fm[pb:pb + 64, dt * C + qb * 128: dt * C + (qb + 1) * 128],
                            start=True, stop=False)
                        plan.on("tensor", fn0)
                        fn = lambda E, off=off, pS=pS: E.matmul(
                            pS[:, off + 128:off + 256], c_idbf[:], c_maskbfT[:],
                            start=False, stop=True)
                    if i == m - 1:
                        t_mm = plan.tok("tensor", fn)
                    else:
                        plan.on("tensor", fn)
                plan.wait("scalar", t_mm)
                lastpart = (g * 2 + m - 1 == qb)
                if not lastpart:
                    spans = [(0, m * 256)]
                elif m == 1:
                    spans = [(128, 256)]
                else:
                    spans = [(0, 256), (384, 512)]
                t_exp = None
                for (a, bnd) in spans:
                    t_exp = plan.tok("scalar", lambda E, g=g, Pt=Pt, pS=pS, a=a, bnd=bnd: E.activation(
                        Pt[:, g * 512 + a: g * 512 + bnd], pS[:, a:bnd], AF.Exp,
                        bias=c_shift[:, 0:1], scale=1.0))
                exp_toks.append(t_exp)
                ring4.release(si, t_exp)
            Ptsl.ready = exp_toks[-1]
            return (it, h, tp, exp_toks)

        def do_av(state):
            nonlocal attn_tok
            it, h, tp, exp_toks = state
            qa = 2 * tp
            qb = qa + 1
            Pt, Ptsl = P_[it % 2], SL[f"P{it % 2}"]
            for half, qt in ((0, qa), (1, qb)):
                par = (2 * it + half) % 4
                nk = qt + 1
                ci, pc = ringC.acquire()
                t_grp = None
                for kc in range(nk):
                    plan.wait("tensor", exp_toks[kc // 2])
                    col = kc * 256 + half * 128
                    plan.on("tensor", lambda E, kc=kc, h=h, Pt=Pt, pc=pc, col=col: E.matmul(
                        pc[:, 0:64], Pt[:, col:col + 128],
                        v_tm[:, kc * D + h * 64: kc * D + h * 64 + 64],
                        start=(kc == 0), stop=False))
                    fn_s = lambda E, kc=kc, Pt=Pt, pc=pc, nk=nk, col=col: E.matmul(
                        pc[:, 64:65], Pt[:, col:col + 128],
                        c_onebf[:, 0:1], start=False, stop=(kc == nk - 1))
                    if kc == nk - 1:
                        t_grp = plan.tok("tensor", fn_s)
                    else:
                        plan.on("tensor", fn_s)
                Ptsl.users.append(t_grp)
                plan.wait("vector", t_grp)
                plan.wait("vector", prev_rcp[par])
                t_rcp = plan.tok("vector", lambda E, par=par, pc=pc: E.reciprocal(
                    w_rcp[:, par:par + 1], pc[:, 64:65]))
                plan.wait("vector", t_rcp)
                t_ae = plan.tok("vector", lambda E, qt=qt, qh=qh, h=h, par=par, pc=pc: E.tensor_scalar(
                    out=attn_tm[:, (qt - qh * HT) * D + h * 64:(qt - qh * HT) * D + h * 64 + 64],
                    in0=pc[:, 0:64], scalar1=w_rcp[:, par:par + 1], scalar2=None,
                    op0=ALU.mult))
                ringC.release(ci, t_ae)
                prev_rcp[par] = t_ae
                attn_tok = t_ae

        items = [(h, tp) for h in range(HEADS) for tp in range(qh * HT // 2, qh * HT // 2 + HT // 2)]
        pending = []
        for it, (h, tp) in enumerate(items):
            pending.append(do_scores(it, h, tp))
            if len(pending) > 1:
                do_av(pending.pop(0))
        for st_ in pending:
            do_av(st_)
        SL["attn_tm"].ready = attn_tok
        SL["attn_tm"].users = []
        TOK["attn_done"] = attn_tok

    def phase_wo(s, qh):
        SL["attn_tm"].use(plan, "tensor")
        afm_tok = None
        for tt in range(HT):
            blocks = [attn_tm[:, tt * D + dc * 128: tt * D + (dc + 1) * 128] for dc in range(ND)]
            dst = q_fm[:].rearrange("p (d c) -> p d c", d=ND)[:, :, qh * CH + tt * 128: qh * CH + (tt + 1) * 128]
            afm_tok = transpose_pack(blocks, dst)
        SL["attn_tm"].users.append(afm_tok)
        SL["mw_o"].use(plan, "tensor")
        plan.wait("tensor", afm_tok)
        prev_x2 = None
        h2h_tok = None
        for tt in range(HT):
            T = qh * HT + tt
            b = tt % 2
            wsl = SL[f"stwo{b}"]
            wsl.prep_write(plan, "sync")
            plan.wait("sync", TOK[f"xn_st_{T}"])
            t_ld = plan.ld(lambda E, T=T, b=b: E.dma_start(
                out=st_wo[b][:], in_=d_xn[T * 128:(T + 1) * 128, :]))
            wsl.ready = t_ld
            x2_toks = []
            for oh in range(2):
                i4, pq = ring4.acquire()
                t_mm = None
                for dc in range(ND):
                    fn = lambda E, tt=tt, qh=qh, dc=dc, oh=oh, pq=pq: E.matmul(
                        pq, q_fm[:, dc * C + qh * CH + tt * 128: dc * C + qh * CH + (tt + 1) * 128],
                        mw_o[:, dc * D + oh * 512: dc * D + (oh + 1) * 512],
                        start=(dc == 0), stop=(dc == ND - 1))
                    if dc == ND - 1:
                        t_mm = plan.tok("tensor", fn)
                    else:
                        plan.on("tensor", fn)
                SL["mw_o"].users.append(t_mm)
                plan.wait("vector", t_mm)
                wsl.use(plan, "vector")
                plan.wait("vector", prev_x2)
                if TOK.get("x2_stprev") is not None:
                    plan.wait("vector", TOK["x2_stprev"])
                t_x2 = plan.tok("vector", lambda E, oh=oh, b=b, pq=pq: E.tensor_add(
                    st_x2[:, oh * 512:(oh + 1) * 512], pq, st_wo[b][:, oh * 512:(oh + 1) * 512]))
                ring4.release(i4, t_x2)
                x2_toks.append(t_x2)
                wsl.users.append(t_x2)
            for t in x2_toks:
                plan.wait("scalar", t)
            t_stx2 = plan.st(lambda E, T=T: E.dma_start(
                out=d_x2[T * 128:(T + 1) * 128, :], in_=st_x2[:]))
            TOK[f"x2_st_{T}"] = t_stx2
            TOK["x2_stprev"] = t_stx2
            for t in x2_toks:
                plan.wait("vector", t)
            cb = 38 + 4 * b
            t_ssq = plan.tok("scalar", lambda E, cb=cb: E.activation(
                scr_bf[:, 0:D], st_x2[:], AF.Square, bias=0.0, scale=1.0,
                accum_out=w_cols[:, cb:cb + 1]))
            plan.wait("scalar", t_ssq)
            t_sq = plan.tok("scalar", lambda E, cb=cb: E.activation(
                w_cols[:, cb + 1:cb + 2], w_cols[:, cb:cb + 1], AF.Sqrt,
                bias=c_epsrm[:, 0:1], scale=1.0 / D))
            plan.wait("vector", t_sq)
            t_r2 = plan.tok("vector", lambda E, cb=cb: E.reciprocal(
                w_cols[:, cb + 2:cb + 3], w_cols[:, cb + 1:cb + 2]))
            hsl = SL[f"sth2{b}"]
            hsl.prep_write(plan, "vector")
            plan.wait("vector", t_r2)
            plan.wait("vector", TOK["g2bc"])
            t_h2 = plan.tok("vector", lambda E, b=b, cb=cb: E.scalar_tensor_tensor(
                out=st_h2[b][:], in0=st_x2[:], scalar=w_cols[:, cb + 2:cb + 3],
                in1=c_g2bc[:], op0=ALU.mult, op1=ALU.mult))
            prev_x2 = t_h2
            hsl.ready = t_h2
            plan.wait("tensor", t_h2)
            blocks = [st_h2[b][:, dc * 128:(dc + 1) * 128] for dc in range(ND)]
            dst = h2_fm_h[:].rearrange("p (d c) -> p d c", d=ND)[:, :, tt * 128:(tt + 1) * 128]
            h2h_tok = transpose_pack(blocks, dst)
            hsl.users.append(h2h_tok)
            plan.wait("tensor", h2h_tok)
            plan.wait("tensor", TOK["gw"])
            di, pd = ringD.acquire()
            t_gl = None
            for dc in range(ND):
                fn = lambda E, tt=tt, dc=dc, pd=pd: E.matmul(
                    pd[:, 0:NE],
                    h2_fm_h[:, dc * CH + tt * 128: dc * CH + (tt + 1) * 128],
                    c_gwsb[:, dc * NE:(dc + 1) * NE],
                    start=(dc == 0), stop=(dc == ND - 1))
                if dc == ND - 1:
                    t_gl = plan.tok("tensor", fn)
                else:
                    plan.on("tensor", fn)
            plan.wait("vector", t_gl)
            t_lg = plan.tok("vector", lambda E, T=T, pd=pd: E.tensor_copy(
                w_logit[:, T * NE:(T + 1) * NE], pd[:, 0:NE]))
            ringD.release(di, t_lg)
            plan.wait("vector", t_lg)
            t_m1 = plan.tok("vector", lambda E, T=T: E.reduce_max(
                w_m1[:, T:T + 1], w_logit[:, T * NE:(T + 1) * NE], axis=AX.X, negate=True))
            plan.wait("vector", t_m1)
            t_mk1 = plan.tok("vector", lambda E, T=T: E.tensor_scalar(
                out=w_gsc[:, 0:NE], in0=w_logit[:, T * NE:(T + 1) * NE],
                scalar1=w_m1[:, T:T + 1], scalar2=0.0, op0=ALU.add, op1=ALU.is_equal))
            plan.wait("vector", t_mk1)
            t_msk = plan.tok("vector", lambda E, T=T: E.scalar_tensor_tensor(
                out=w_gsc[:, NE:2 * NE], in0=w_gsc[:, 0:NE], scalar=-1e30,
                in1=w_logit[:, T * NE:(T + 1) * NE], op0=ALU.mult, op1=ALU.add))
            plan.wait("vector", t_msk)
            t_m2 = plan.tok("vector", lambda E, T=T: E.reduce_max(
                w_m2[:, T:T + 1], w_gsc[:, NE:2 * NE], axis=AX.X, negate=True))
            TOK[f"gate_{T}"] = t_m2
        SL["h2_fm_h"].ready = h2h_tok
        plan.wait("scalar", h2h_tok)
        t_h2st = plan.st(lambda E, qh=qh: E.dma_start(
            out=d_h2.rearrange("e p c -> p e c")[:, :, qh * CH:(qh + 1) * CH],
            in_=h2_fm_h[:].rearrange("p (e c) -> p e c", e=ND)))
        SL["h2_fm_h"].users.append(t_h2st)
        TOK[f"h2_st_{qh}"] = t_h2st
        for tt in range(HT):
            T = qh * HT + tt
            plan.wait("scalar", TOK[f"gate_{T}"])
            TOK[f"sig_{T}"] = plan.tok("scalar", lambda E, T=T: E.activation(
                w_gt[:, T:T + 1], w_m1[:, T:T + 1], AF.Sigmoid,
                bias=w_m2[:, T:T + 1], scale=-1.0))
        comb_tok = SL["combine"].ready
        for tt in range(HT):
            T = qh * HT + tt
            plan.wait("vector", TOK[f"sig_{T}"])
            plan.wait("vector", comb_tok)
            t_c1 = plan.tok("vector", lambda E, T=T: E.tensor_scalar(
                out=w_gsc[:, 0:NE], in0=w_logit[:, T * NE:(T + 1) * NE],
                scalar1=w_m1[:, T:T + 1], scalar2=0.0, op0=ALU.add, op1=ALU.is_equal))
            t_c2 = plan.tok("vector", lambda E, T=T: E.tensor_scalar(
                out=w_gsc[:, NE:2 * NE], in0=w_logit[:, T * NE:(T + 1) * NE],
                scalar1=w_m2[:, T:T + 1], scalar2=0.0, op0=ALU.add, op1=ALU.is_equal))
            t_gb = plan.tok("vector", lambda E, T=T: E.tensor_scalar(
                out=w_cols[:, 46:47], in0=w_gt[:, T:T + 1], scalar1=-1.0, scalar2=1.0,
                op0=ALU.mult, op1=ALU.add))
            plan.wait("vector", t_c1)
            plan.wait("vector", t_gb)
            t_t1 = plan.tok("vector", lambda E, T=T: E.tensor_scalar(
                out=w_gsc[:, 0:NE], in0=w_gsc[:, 0:NE], scalar1=w_gt[:, T:T + 1],
                scalar2=None, op0=ALU.mult))
            plan.wait("vector", t_c2)
            plan.wait("vector", t_t1)
            comb_tok = plan.tok("vector", lambda E, T=T: E.scalar_tensor_tensor(
                out=w_comb[:, T * NE:(T + 1) * NE], in0=w_gsc[:, NE:2 * NE],
                scalar=w_cols[:, 46:47], in1=w_gsc[:, 0:NE],
                op0=ALU.mult, op1=ALU.add))
        SL["combine"].ready = comb_tok
        if qh == 1:
            TOK["moe_pf1"] = load_w1(0)
            TOK["moe_pf2"] = load_w2(0)

    def phase_moe(s):
        NCH = C // 512
        for half, acc, asl in ((0, acc0, SL["acc0"]), (1, acc1, SL["acc1"])):
            asl.prep_write(plan, "sync")
            for tt in range(HT):
                plan.wait("sync", TOK[f"x2_st_{half * HT + tt}"])
            t = plan.ld(lambda E, half=half, acc=acc: E.dma_start(
                out=acc[:].rearrange("p (t d) -> p t d", t=HT),
                in_=d_x2.rearrange("(t p) d -> p t d", p=128)[:, half * HT:(half + 1) * HT, :]))
            asl.ready = t
        acc_tok = [None] * NT

        cur1 = TOK["moe_pf1"]
        cur2 = TOK["moe_pf2"]
        hid, h2c = moe_hid, moe_h2c
        hsl, hcsl = SL["hid0"], SL["moe_h0"]
        for e in range(NE):
            r1, s1 = cur1
            r2, s2 = cur2
            if e + 1 < NE:
                nxt1 = load_w1(e + 1)
            for ch in range(NCH):
                hcsl.prep_write(plan, "sync")
                plan.wait("sync", TOK["h2_st_0"])
                plan.wait("sync", TOK["h2_st_1"])
                t_h = plan.ld(lambda E, ch=ch, h2c=h2c: E.dma_start(
                    out=h2c[:].rearrange("p (a b) -> p a b", a=ND),
                    in_=d_h2.rearrange("e p c -> p e c")[:, :, ch * 512:(ch + 1) * 512]))
                hcsl.ready = t_h
                hsl.prep_write(plan, "scalar")
                s1.use(plan, "tensor")
                hcsl.use(plan, "tensor")
                t_sil = None
                for ht in range(NH):
                    i4, pq = ring4.acquire()
                    t_mm = None
                    for dc in range(ND):
                        fn = lambda E, ht=ht, dc=dc, r1=r1, h2c=h2c, pq=pq: E.matmul(
                            pq, r1[:, dc * EXPD + ht * 128: dc * EXPD + (ht + 1) * 128],
                            h2c[:, dc * 512:(dc + 1) * 512],
                            start=(dc == 0), stop=(dc == ND - 1))
                        if dc == ND - 1:
                            t_mm = plan.tok("tensor", fn)
                        else:
                            plan.on("tensor", fn)
                    s1.users.append(t_mm)
                    hcsl.users.append(t_mm)
                    plan.wait("scalar", t_mm)
                    t_sil = plan.tok("scalar", lambda E, ht=ht, hid=hid, pq=pq: E.activation(
                        hid[:, ht * 512:(ht + 1) * 512], pq, AF.Silu,
                        bias=0.0, scale=1.0))
                    ring4.release(i4, t_sil)
                hsl.ready = t_sil
                s2.use(plan, "tensor")
                hsl.use(plan, "tensor")
                for ti in range(4):
                    T = ch * 4 + ti
                    half, tt = T // HT, T % HT
                    acc = acc0 if half == 0 else acc1
                    asl = SL["acc0"] if half == 0 else SL["acc1"]
                    asl.use(plan, "vector")
                    for oh in range(2):
                        ci, pc = ringC.acquire()
                        t_mm = None
                        for ht in range(NH):
                            fn = lambda E, ht=ht, ti=ti, oh=oh, r2=r2, hid=hid, pc=pc: E.matmul(
                                pc, hid[:, ht * 512 + ti * 128: ht * 512 + (ti + 1) * 128],
                                r2[:, ht * D + oh * 512: ht * D + (oh + 1) * 512],
                                start=(ht == 0), stop=(ht == NH - 1))
                            if ht == NH - 1:
                                t_mm = plan.tok("tensor", fn)
                            else:
                                plan.on("tensor", fn)
                        hsl.users.append(t_mm)
                        s2.users.append(t_mm)
                        plan.wait("vector", t_mm)
                        plan.wait("vector", acc_tok[T])
                        plan.wait("vector", SL["combine"].ready)
                        t_ac = plan.tok("vector", lambda E, e=e, T=T, tt=tt, oh=oh, acc=acc, pc=pc: E.scalar_tensor_tensor(
                            out=acc[:, tt * D + oh * 512: tt * D + (oh + 1) * 512],
                            in0=pc, scalar=w_comb[:, T * NE + e:T * NE + e + 1],
                            in1=acc[:, tt * D + oh * 512: tt * D + (oh + 1) * 512],
                            op0=ALU.mult, op1=ALU.add))
                        ringC.release(ci, t_ac)
                        acc_tok[T] = t_ac
                if e == NE - 1:
                    if ch == 0:
                        for t_ in list(SL["moe_w2"].users):
                            plan.wait("sync", t_)
                            plan.wait("scalar", t_)
                    for ti in range(4):
                        TOK["fin_last"] = fin_tile(s, ch * 4 + ti, acc_tok)
            if e + 1 < NE:
                cur2 = load_w2(e + 1)
                cur1 = nxt1
        TOK["acc_tok"] = acc_tok
        return TOK["fin_last"]

    def fin_tile(s, T, acc_tok):
        b = T % 2
        half, tt = T // HT, T % HT
        acc = acc0 if half == 0 else acc1
        aslice = acc[:, tt * D:(tt + 1) * D]
        xsl, ssl, gsl = SL[f"finx{b}"], SL[f"fins{b}"], SL[f"fing{b}"]
        xsl.prep_write(plan, "sync")
        t_lx = plan.ld(lambda E, T=T, b=b: E.dma_start(
            out=fin_x[b][:], in_=x_in[s][T * 128:(T + 1) * 128, :]))
        xsl.ready = t_lx
        ssl.prep_write(plan, "sync")
        plan.wait("sync", TOK[f"s_st_{T}"])
        t_ls = plan.ld(lambda E, T=T, b=b: E.dma_start(
            out=fin_s[b][:], in_=d_s[T * 128:(T + 1) * 128, :]))
        ssl.ready = t_ls
        gsl.prep_write(plan, "scalar")
        plan.wait("scalar", acc_tok[T])
        t_g = plan.tok("scalar", lambda E, b=b, aslice=aslice: E.activation(
            fin_g[b][:], aslice, AF.Silu, bias=0.0, scale=1.0))
        gsl.ready = t_g
        plan.wait("vector", t_g)
        ssl.use(plan, "vector")
        t_m = plan.tok("vector", lambda E, b=b: E.tensor_mul(
            fin_g[b][:], fin_s[b][:], fin_g[b][:]))
        ssl.users.append(t_m)
        plan.wait("vector", t_m)
        xsl.use(plan, "vector")
        t_o = plan.tok("vector", lambda E, b=b, aslice=aslice: E.tensor_add(
            aslice, fin_g[b][:], fin_x[b][:]))
        xsl.users.append(t_o)
        gsl.users.append(t_o)
        plan.wait("scalar", t_o)
        last_st = plan.st(lambda E, T=T, aslice=aslice: E.dma_start(
            out=out_t[s][T * 128:(T + 1) * 128, :], in_=aslice))
        return last_st

    PHASES = [(phase_convert, False), (phase_consts, False)]
    def mk(fn, *a):
        return lambda: fn(*a)
    for s_ in range(NSEQ):
        PHASES += [(mk(phase_s1, s_), True), (mk(phase_qkv, s_), True),
                   (mk(phase_A, s_), True), (mk(phase_wv2, s_), True),
                   (mk(phase_mla, s_), True),
                   (mk(phase_attn, s_, 0), True), (mk(phase_wo, s_, 0), True),
                   (mk(phase_attn, s_, 1), True), (mk(phase_wo, s_, 1), True),
                   (mk(phase_moe, s_), True)]
    final_tok = None
    for pi, (ph, need_bar) in enumerate(PHASES):
        if pi >= max_phases:
            break
        if need_bar:
            plan.barrier()
        r = ph()
        if r is not None:
            final_tok = r
    plan.wait("scalar", final_tok)

    with contextlib.ExitStack() as stack:
        for sname in SEMS:
            plan.semh[sname] = stack.enter_context(nc.semaphore("sem_" + sname))
        with nc.Block() as block:
            @block.sync
            def _(e):
                plan.emit(e, "sync")

            @block.gpsimd
            def _(e):
                plan.emit(e, "gpsimd")

            @block.vector
            def _(e):
                plan.emit(e, "vector")

            @block.scalar
            def _(e):
                plan.emit(e, "scalar")

            @block.tensor
            def _(e):
                plan.emit(e, "tensor")
    return nc


def kernel(**inputs):
    x = np.ascontiguousarray(np.asarray(inputs["L0_attn_x"], np.float32))
    B = x.shape[0]
    ncore = 8
    per = B // ncore
    nc = build_nc(C=x.shape[1], NSEQ=per)
    common = {
        "wqa": np.ascontiguousarray(inputs["L0_attn_Wq"], np.float32),
        "wka": np.ascontiguousarray(inputs["L0_attn_Wk"], np.float32),
        "wva": np.ascontiguousarray(inputs["L0_attn_Wv"], np.float32),
        "bq": np.ascontiguousarray(inputs["L0_attn_bq"], np.float32),
        "bk": np.ascontiguousarray(inputs["L0_attn_bk"], np.float32),
        "bv": np.ascontiguousarray(inputs["L0_attn_bv"], np.float32),
        "gam": np.ascontiguousarray(inputs["L0_attn_gamma"], np.float32),
        "g1": np.ascontiguousarray(inputs["L0_ffn_g1"], np.float32),
        "mq": np.ascontiguousarray(inputs["L0_ffn_Wq"], np.float32),
        "mdkv": np.ascontiguousarray(inputs["L0_ffn_Wdkv"], np.float32),
        "muk": np.ascontiguousarray(inputs["L0_ffn_Wuk"], np.float32),
        "muv": np.ascontiguousarray(inputs["L0_ffn_Wuv"], np.float32),
        "mo": np.ascontiguousarray(inputs["L0_ffn_Wo"], np.float32),
        "g2": np.ascontiguousarray(inputs["L0_ffn_g2"], np.float32),
        "gw": np.ascontiguousarray(inputs["L0_ffn_gate_w"], np.float32),
        "w1": np.ascontiguousarray(inputs["L0_ffn_w1"], np.float32),
        "w2": np.ascontiguousarray(inputs["L0_ffn_w2"], np.float32),
    }
    in_maps = [dict(common, x=np.ascontiguousarray(x[c * per:(c + 1) * per]))
               for c in range(ncore)]
    res = run_bass_kernel_spmd(nc, in_maps, core_ids=list(range(ncore)))
    return np.concatenate([r["out"] for r in res.results], axis=0)
